# version 07
# baseline (speedup 1.0000x reference)
import math
import numpy as np
import concourse.bass as bass
import concourse.mybir as mybir
from concourse.bass_utils import run_bass_kernel_spmd
from contextlib import ExitStack

F32 = mybir.dt.float32
BF16 = mybir.dt.bfloat16
AF = mybir.ActivationFunctionType
ALU = mybir.AluOpType
AX = mybir.AxisListType

T = 4096
D = 1024
H = 16
HD = 64
NKV = 4
NT = T // 128
NEG = -3.0e38
DEPTH = 6
WINDOW = 256
LN8 = math.log(8.0)


def _split3(x):
    import ml_dtypes
    bf = ml_dtypes.bfloat16
    x = np.asarray(x, dtype=np.float32)
    hi = x.astype(bf).astype(np.float32)
    mid = (x - hi).astype(bf).astype(np.float32)
    lo = (x - hi - mid).astype(bf).astype(np.float32)
    return hi, mid, lo


def host_constants():
    c = {}
    c["ident"] = np.eye(128, dtype=np.float32)
    c["diagmask"] = np.triu(np.full((128, 128), NEG, dtype=np.float32), 1)
    slopes = 2.0 ** (-(np.arange(H, dtype=np.float64) / (H - 1) * 8.0))
    s_hi, s_mid, s_lo = _split3(slopes)
    qaug = np.zeros((6, H * 128), dtype=np.float32)
    for h in range(H):
        for a, v in enumerate([s_hi[h], s_hi[h], s_mid[h], s_mid[h], s_lo[h], s_lo[h]]):
            qaug[a, h * 128:(h + 1) * 128] = v
    c["qaug"] = qaug
    j = np.arange(T, dtype=np.float64)
    j_hi, j_lo, _ = _split3(j)
    c["kaug"] = np.stack([j_hi, j_lo, j_hi, j_lo, j_hi, j_lo]).astype(np.float32)
    half = np.arange(0, HD, 2, dtype=np.float64)
    inv_freq = 1.0 / 10000.0 ** (half / HD)
    t = np.arange(T, dtype=np.float64)
    freqs = np.concatenate([t[:, None] * inv_freq[None, :]] * 2, axis=-1)
    base_scale = (half + 0.4 * HD) / (1.4 * HD)
    power = (t - T // 2) / (WINDOW // 2)
    scale = np.concatenate([base_scale[None, :] ** power[:, None]] * 2, axis=-1)
    cos, sin = np.cos(freqs), np.sin(freqs)
    sgn = np.concatenate([-np.ones(32), np.ones(32)])[None, :]
    c["rcsq"] = (cos * scale * 8.0).astype(np.float32)
    c["rssq"] = (sin * scale * sgn * 8.0).astype(np.float32)
    c["rcsk"] = (cos / scale).astype(np.float32)
    c["rssk"] = (sin / scale * sgn).astype(np.float32)
    return c


def _tile_tok(a):
    Tt, F = a.shape
    return np.ascontiguousarray(a.reshape(Tt // 128, 128, F).transpose(1, 0, 2).reshape(128, -1))


def _fm(a):
    Din, F = a.shape
    return np.ascontiguousarray(a.reshape(Din // 128, 128, F).transpose(1, 0, 2).reshape(128, -1))


def _rep(v):
    v = np.asarray(v, dtype=np.float32)
    return np.ascontiguousarray(np.broadcast_to(v, (128, v.shape[0])))



class Sem:
    def __init__(self, es, nc, name):
        self.h = es.enter_context(nc.semaphore(name))
        self.n = 0


class G:
    pass


def EMIT(g, engine, fn, waits=(), inc=None):
    waits = tuple((s, v) for (s, v) in waits)
    val = None
    if inc is not None:
        sem, k = inc
        sem.n += k
        val = sem.n

        def run(e, fn=fn, waits=waits, sem=sem, k=k):
            for s, v in waits:
                e.wait_ge(s.h, v)
            fn(e).then_inc(sem.h, k)
    else:
        def run(e, fn=fn, waits=waits):
            for s, v in waits:
                e.wait_ge(s.h, v)
            fn(e)
    g.prog[engine].append(run)
    return val


class Arena:
    def __init__(self, nc, es, nbytes):
        self.t = es.enter_context(nc.sbuf_tensor("arena", [128, nbytes // 4], F32))
        self.nbytes = nbytes

    def view(self, off, p, cols, dtype):
        esz = 4 if dtype == F32 else 2
        assert off % 4 == 0 and (off + cols * esz) <= self.nbytes, (off, cols, dtype)
        base = self.t[0:p, off // 4: off // 4 + (cols * esz + 3) // 4]
        if dtype == F32:
            return base[:, :cols]
        v = base.bitcast(BF16)
        return v[:, :cols]


class Reg:
    def __init__(self, arena, start=0):
        self.a = arena
        self.off = start

    def alloc(self, p, cols, dtype, align=512):
        esz = 4 if dtype == F32 else 2
        self.off = (self.off + align - 1) // align * align
        v = self.a.view(self.off, p, cols, dtype)
        self.off += cols * esz
        return v



def build_nc(phases=("A", "B", "C"), debug=(), mini_n=None):
    nc = bass.Bass("TRN2", target_bir_lowering=False)
    g = G()
    g.nc = nc
    g.mini_n = mini_n
    g.debug = set(debug)
    g.prog = {e: [] for e in ["sync", "gpsimd", "tensor", "vector", "scalar"]}
    dr = {}
    g.dr = dr

    def din(name, shape):
        dr[name] = nc.dram_tensor(name, shape, F32, kind="ExternalInput").ap()

    din("x", [T, D])
    din("wq", [128, 8 * 1024]); din("wkv", [128, 8 * 512]); din("wo", [128, 8 * 1024])
    din("bq", [128, 1024]); din("bkv", [128, 512]); din("bo", [128, 1024])
    din("qn", [128, 1024]); din("kn", [128, 256])
    din("wqkv", [DEPTH, 128, 8 * 3072]); din("woin", [DEPTH, 128, 8 * 1024])
    din("w1", [DEPTH, 128, 22 * 2048]); din("w2", [DEPTH, 128, 22 * 1024])
    din("b1a", [DEPTH, 128, 22]); din("b1g", [DEPTH, 128, 22])
    din("b2", [DEPTH, 128, 8]); din("lng", [DEPTH, 128, 8]); din("lnb", [DEPTH, 128, 8])
    din("qsc", [DEPTH, 128, 1024]); din("ksc", [DEPTH, 128, 1024])
    din("ident", [128, 128]); din("diagmask", [128, 128]); din("fconsts", [128, 8])
    din("bndtab", [128, 8]); din("onesrow", [1, 128]); din("bmask", [128, 3 * 384])
    din("qaug", [6, H * 128]); din("kaug", [6, T])
    din("rcsq", [128, NT * 64]); din("rssq", [128, NT * 64])
    din("rcsk", [128, NT * 64]); din("rssk", [128, NT * 64])
    dr["s_scr"] = nc.dram_tensor("s_scr", [T, D], F32, kind="Internal").ap()
    if "B0" in phases:
        dr["r_init"] = nc.dram_tensor("r_init", [128, 8 * T], F32, kind="ExternalInput").ap()
    dr["r_scr"] = nc.dram_tensor("r_scr", [128, 8 * T], F32, kind="Internal").ap()
    dr["wbf"] = nc.dram_tensor("wbf", [DEPTH, 128, 8 * 3072 + 8 * 1024 + 22 * 2048 + 22 * 1024],
                               BF16, kind="Internal").ap()
    dr["out"] = nc.dram_tensor("out", [T, D], F32, kind="ExternalOutput").ap()
    g.dbg = {}
    for name in g.debug:
        shp = {"xn": [T, D], "q": [T, D], "kv": [T, 512], "thr": [T, H], "z": [T, H],
               "r0": [128, 8 * T], "rl": [DEPTH + 1, 128, 8 * T],
               "ra": [128, 8 * T], "b0": [6, 128, 1024], "c0": [4, 128, 1024],
               "f0": [4, 128, 4096]}[name]
        g.dbg[name] = nc.dram_tensor("dbg_" + name, shp, F32, kind="ExternalOutput").ap()

    with ExitStack() as es:
        _build(g, es, phases)
    return nc


def _build(g, es, phases):
    nc, dr = g.nc, g.dr
    sm = G()
    g.sm = sm
    for name in ["ld", "lg", "pe", "cv", "cs", "cg", "st", "sg", "lw", "lr"]:
        setattr(sm, name, Sem(es, nc, name))

    sb = G()
    g.sb = sb
    sb.ident = es.enter_context(nc.sbuf_tensor("sb_ident", [128, 128], BF16))
    sb.identf = es.enter_context(nc.sbuf_tensor("sb_identf", [128, 128], F32))
    sb.diagmask = es.enter_context(nc.sbuf_tensor("sb_diagmask", [128, 128], F32))
    sb.m8 = es.enter_context(nc.sbuf_tensor("m8", [128, 8], F32))
    sb.fc = es.enter_context(nc.sbuf_tensor("fc", [128, 8], F32))
    sb.thr = es.enter_context(nc.sbuf_tensor("thr", [128, 4], F32))
    sb.zrec = es.enter_context(nc.sbuf_tensor("zrec", [128, 4], F32))
    sb.stats = es.enter_context(nc.sbuf_tensor("stats", [128, 8], F32))
    sb.rss = es.enter_context(nc.sbuf_tensor("rss", [128, 24], F32))
    sb.onesb = es.enter_context(nc.sbuf_tensor("onesb", [128, 1], BF16))

    ar = Arena(nc, es, 206 * 1024)
    g.ar = ar

    A = G()
    g.A = A
    ra = Reg(ar)
    A.wq_hi = ra.alloc(128, 8 * 1024, BF16)
    A.wq_lo = ra.alloc(128, 8 * 1024, BF16)
    A.wkv_hi = ra.alloc(128, 8 * 512, BF16)
    A.wkv_lo = ra.alloc(128, 8 * 512, BF16)
    A.wo = ra.alloc(128, 8 * 1024, BF16)
    A.bq = ra.alloc(128, 1024, F32)
    A.bkv = ra.alloc(128, 512, F32)
    A.bo = ra.alloc(128, 1024, F32)
    A.qn = ra.alloc(128, 1024, F32)
    A.kn = ra.alloc(128, 256, F32)
    A.khl = ra.alloc(128, NKV * T, BF16)
    A.kaug = ra.alloc(6, T, BF16)
    A.vbuf = ra.alloc(128, NT * 256, BF16)
    A.lhsT1 = ra.alloc(128, H * 128, BF16)
    A.lhsT1x = ra.alloc(128, H * 128, BF16)
    A.qaug = ra.alloc(6, H * 128, BF16)
    A.S = ra.alloc(128, T, F32)
    A.P = ra.alloc(128, T, BF16)
    A.coll = ra.alloc(128, 512, F32)
    A.PT = ra.alloc(128, 1024, BF16)
    A.yfm = ra.alloc(128, 8 * 128, BF16)
    A.xt = ra.alloc(128, 1024, F32)
    A.xn = ra.alloc(128, 1024, F32)
    A.tmpf = ra.alloc(128, 1024, F32)
    A.stile = ra.alloc(128, 1024, F32)
    A.k2s = ra.alloc(128, 256, F32)
    A.hi_tm = ra.alloc(128, 1024, BF16)
    A.lo_tm = ra.alloc(128, 1024, BF16)
    A.hi_fm = ra.alloc(128, 1024, BF16)
    A.lo_fm = ra.alloc(128, 1024, BF16)
    A.kvsb = ra.alloc(128, 512, F32)
    A.khat = ra.alloc(128, 256, F32)
    A.k_hi = ra.alloc(128, 256, BF16)
    A.k_lo = ra.alloc(128, 256, BF16)
    g.A_end = ra.off
    assert ra.off <= ar.nbytes, ra.off

    ps = G()
    g.ps = ps
    ps.S = es.enter_context(nc.psum_tensor("psS", [128, 2048], F32))
    ps.tr = es.enter_context(nc.psum_tensor("pstr", [128, 1024], BF16))
    ps.proj = es.enter_context(nc.psum_tensor("psproj", [128, 1024], F32))
    ps.y = es.enter_context(nc.psum_tensor("psy", [128, 512], F32))

    g.war = {}

    blk = es.enter_context(nc.Block())

    if "B0" in phases:
        for dst, srcn in [(sb.ident[:], "ident"), (sb.identf[:], "ident"),
                          (sb.diagmask[:], "diagmask"), (sb.fc[:], "fconsts")]:
            EMIT(g, "gpsimd" if dst.dtype == BF16 else "sync",
                 lambda e, dst=dst, srcn=srcn: e.dma_start(out=dst, in_=dr[srcn][:]),
                 inc=(sm.lg, 16) if dst.dtype == BF16 else (sm.ld, 16))
        _emit_wbf_precast(g, layers=[0])
        g.proA = {"lg": sm.lg.n, "ld": sm.ld.n, "cv": sm.cv.n}
        _phaseB(g, mini=True)
    else:
        _prologue(g)
        if "A" in phases:
            for tau in range(NT):
                _phaseA_tau(g, tau)
        if "B" in phases:
            _phaseB(g)
        if "C" in phases:
            _phaseC(g)
    EMIT(g, "sync", lambda e: e.nop(), waits=[(sm.st, sm.st.n)])
    EMIT(g, "gpsimd", lambda e: e.nop(), waits=[(sm.sg, sm.sg.n)])

    @blk.sync
    def _(e):
        for f in g.prog["sync"]:
            f(e)

    @blk.gpsimd
    def _(e):
        for f in g.prog["gpsimd"]:
            f(e)

    @blk.tensor
    def _(e):
        for f in g.prog["tensor"]:
            f(e)

    @blk.vector
    def _(e):
        for f in g.prog["vector"]:
            f(e)

    @blk.scalar
    def _(e):
        for f in g.prog["scalar"]:
            f(e)



def _prologue(g):
    nc, sb, A, dr, sm = g.nc, g.sb, g.A, g.dr, g.sm

    for dst, src in [(sb.identf[:], dr["ident"]), (sb.diagmask[:], dr["diagmask"]),
                     (A.bq, dr["bq"]), (A.bkv, dr["bkv"]), (A.bo, dr["bo"]),
                     (A.qn, dr["qn"]), (A.kn, dr["kn"]), (sb.fc[:], dr["fconsts"])]:
        EMIT(g, "sync", lambda e, dst=dst, src=src: e.dma_start(out=dst, in_=src[:]),
             inc=(sm.ld, 16))
    for dst, src in [(sb.ident[:], dr["ident"]), (A.wq_hi, dr["wq"]),
                     (A.wkv_hi, dr["wkv"]), (A.wo, dr["wo"]),
                     (A.kaug, dr["kaug"]), (A.qaug, dr["qaug"])]:
        EMIT(g, "gpsimd", lambda e, dst=dst, src=src: e.dma_start(out=dst, in_=src[:]),
             inc=(sm.lg, 16))
    lg_hi = sm.lg.n

    for (lo, hi, src, ncols) in [(A.wq_lo, A.wq_hi, dr["wq"], 8192),
                                 (A.wkv_lo, A.wkv_hi, dr["wkv"], 4096)]:
        for c0 in range(0, ncols, 4096):
            w = min(4096, ncols - c0)
            ldv = EMIT(g, "sync", lambda e, c0=c0, w=w, src=src:
                       e.dma_start(out=A.S[:, :w], in_=src[:, c0:c0 + w]),
                       waits=W(g, "A.S"), inc=(sm.ld, 16))
            cvv = EMIT(g, "vector", lambda e, lo=lo, hi=hi, c0=c0, w=w:
                       e.tensor_tensor(out=lo[:, c0:c0 + w], in0=A.S[:, :w],
                                       in1=hi[:, c0:c0 + w], op=ALU.subtract),
                       waits=[(sm.ld, ldv), (sm.lg, lg_hi)], inc=(sm.cv, 1))
            MARK(g, "A.S", sm.cv, cvv)


    _emit_wbf_precast(g)
    g.proA = {"lg": sm.lg.n, "ld": sm.ld.n, "cv": sm.cv.n}


def _emit_wbf_precast(g, layers=range(DEPTH)):
    nc, sb, dr, sm = g.nc, g.sb, g.dr, g.sm
    WCWO = 8 * 3072
    WCW1 = WCWO + 8 * 1024
    WCW2 = WCW1 + 22 * 2048
    EMIT(g, "gpsimd", lambda e: e.dma_start(out=sb.onesb[:], in_=dr["fconsts"][:, 3:4],
                                            allow_slow_non_contiguous=True),
         inc=(sm.lg, 16))
    g.ob_rdy = sm.lg.n
    for l in layers:
        for (srcn, off, wd) in [("wqkv", 0, 8 * 3072), ("woin", WCWO, 8 * 1024),
                                ("w1", WCW1, 22 * 2048), ("w2", WCW2, 22 * 1024)]:
            EMIT(g, "gpsimd", lambda e, l=l, srcn=srcn, off=off, wd=wd: e.dma_start(
                out=dr["wbf"][l, :, off:off + wd], in_=dr[srcn][l]),
                inc=(sm.lg, 16))
    g.wbf_rdy = sm.lg.n


def W(g, *names):
    out = []
    fl = getattr(g, "war_floor", [])
    for n in names:
        out.extend(g.war.get(n, fl))
    return out


def MARK(g, name, sem, val, add=False):
    if add and name in g.war:
        g.war[name] = g.war[name] + [(sem, val)]
    else:
        g.war[name] = [(sem, val)]



def _phaseA_tau(g, tau):
    nc, sb, A, ps, dr, sm = g.nc, g.sb, g.A, g.ps, g.dr, g.sm
    L = (tau + 1) * 128
    r0 = tau * 128

    ld_x = EMIT(g, "sync", lambda e: e.dma_start(out=A.xt, in_=dr["x"][r0:r0 + 128, :]),
                waits=W(g, "A.xt") + [(sm.cv, g.proA["cv"])], inc=(sm.ld, 16))

    cv1 = EMIT(g, "vector", lambda e: e.reduce_sum(out=sb.stats[:, 0:1], in_=A.xt, axis=AX.X),
               waits=[(sm.ld, ld_x)] + W(g, "sb.stats"), inc=(sm.cv, 1))
    cs1 = EMIT(g, "scalar", lambda e: e.activation(out=A.stile, in_=A.xt, func=AF.Square,
                                                   accum_out=sb.stats[:, 1:2]),
               waits=[(sm.ld, ld_x)] + W(g, "A.stile", "sb.stats"), inc=(sm.cs, 1))
    cv2 = EMIT(g, "vector", lambda e: e.tensor_scalar(out=sb.stats[:, 2:3], in0=sb.stats[:, 0:1],
                                                      scalar1=1.0 / D, scalar2=None, op0=ALU.mult),
               waits=[(sm.cv, cv1)], inc=(sm.cv, 1))
    cv3 = EMIT(g, "vector", lambda e: e.tensor_scalar(out=sb.stats[:, 3:4], in0=sb.stats[:, 1:2],
                                                      scalar1=1.0 / D, scalar2=None, op0=ALU.mult),
               waits=[(sm.cs, cs1)], inc=(sm.cv, 1))
    cv4 = EMIT(g, "vector", lambda e: e.tensor_tensor(out=sb.stats[:, 4:5], in0=sb.stats[:, 2:3],
                                                      in1=sb.stats[:, 2:3], op=ALU.mult),
               waits=[(sm.cv, cv2)], inc=(sm.cv, 1))
    cv5 = EMIT(g, "vector", lambda e: e.tensor_tensor(out=sb.stats[:, 4:5], in0=sb.stats[:, 3:4],
                                                      in1=sb.stats[:, 4:5], op=ALU.subtract),
               waits=[(sm.cv, cv4), (sm.cv, cv3)], inc=(sm.cv, 1))
    cs2 = EMIT(g, "scalar", lambda e: e.activation(out=sb.stats[:, 5:6], in_=sb.stats[:, 4:5],
                                                   func=AF.Ln, bias=sb.fc[:, 0:1]),
               waits=[(sm.cv, cv5)], inc=(sm.cs, 1))
    cs3 = EMIT(g, "scalar", lambda e: e.activation(out=sb.stats[:, 6:7], in_=sb.stats[:, 5:6],
                                                   func=AF.Exp, scale=-0.5),
               waits=[(sm.cs, cs2)], inc=(sm.cs, 1))
    cv_xn = EMIT(g, "vector", lambda e: e.tensor_scalar(out=A.xn, in0=A.xt,
                                                        scalar1=sb.stats[:, 2:3],
                                                        scalar2=sb.stats[:, 6:7],
                                                        op0=ALU.subtract, op1=ALU.mult),
                 waits=[(sm.cs, cs3)] + W(g, "A.xn"), inc=(sm.cv, 1))
    MARK(g, "sb.stats", sm.cv, cv_xn)
    cv_hi = EMIT(g, "vector", lambda e: e.tensor_copy(out=A.hi_tm, in_=A.xn),
                 waits=[(sm.cv, cv_xn)] + W(g, "A.hi_tm"), inc=(sm.cv, 1))
    cv_t = EMIT(g, "vector", lambda e: e.tensor_tensor(out=A.tmpf, in0=A.xn, in1=A.hi_tm,
                                                       op=ALU.subtract),
                waits=[(sm.cv, cv_hi)] + W(g, "A.tmpf"), inc=(sm.cv, 1))
    cv_lo = EMIT(g, "vector", lambda e: e.tensor_copy(out=A.lo_tm, in_=A.tmpf),
                 waits=[(sm.cv, cv_t)] + W(g, "A.lo_tm"), inc=(sm.cv, 1))

    pe1 = EMIT(g, "tensor", lambda e: [e.transpose(ps.tr[:, c * 128:(c + 1) * 128],
                                                   A.hi_tm[:, c * 128:(c + 1) * 128],
                                                   sb.ident[:]) for c in range(8)][-1],
               waits=[(sm.cv, cv_hi), (sm.lg, g.proA["lg"])] + W(g, "ps.tr"), inc=(sm.pe, 1))
    cv_eh = EMIT(g, "vector", lambda e: e.tensor_copy(out=A.hi_fm, in_=ps.tr[:, :]),
                 waits=[(sm.pe, pe1)] + W(g, "A.hi_fm"), inc=(sm.cv, 1))
    pe2 = EMIT(g, "tensor", lambda e: [e.transpose(ps.tr[:, c * 128:(c + 1) * 128],
                                                   A.lo_tm[:, c * 128:(c + 1) * 128],
                                                   sb.ident[:]) for c in range(8)][-1],
               waits=[(sm.cv, cv_lo), (sm.cv, cv_eh)], inc=(sm.pe, 1))
    cv_el = EMIT(g, "vector", lambda e: e.tensor_copy(out=A.lo_fm, in_=ps.tr[:, :]),
                 waits=[(sm.pe, pe2)] + W(g, "A.lo_fm"), inc=(sm.cv, 1))
    MARK(g, "ps.tr", sm.cv, cv_el)
    MARK(g, "A.hi_tm", sm.pe, pe1)
    MARK(g, "A.lo_tm", sm.pe, pe2)
    cv_r = EMIT(g, "vector", lambda e: e.tensor_tensor(out=A.tmpf, in0=A.hi_fm, in1=A.lo_fm,
                                                       op=ALU.add),
                waits=[(sm.cv, cv_el), (sm.cv, cv_lo)], inc=(sm.cv, 1))
    st_r = EMIT(g, "sync", lambda e, tau=tau: e.dma_start(
        out=dr["r_scr"].rearrange("p (c t) -> p c t", c=8)[:, :, tau * 128:tau * 128 + 128],
        in_=A.tmpf.rearrange("p (c t) -> p c t", c=8)),
        waits=[(sm.cv, cv_r)], inc=(sm.st, 16))
    MARK(g, "A.tmpf", sm.st, st_r)

    def qproj(e):
        last = None
        for oc in range(2):
            for c in range(8):
                sl = ps.proj[:, oc * 512:(oc + 1) * 512]
                co = c * 1024 + oc * 512
                last = e.matmul(sl, A.hi_fm[:, c * 128:(c + 1) * 128],
                                A.wq_hi[:, co:co + 512], start=(c == 0), stop=False)
                last = e.matmul(sl, A.hi_fm[:, c * 128:(c + 1) * 128],
                                A.wq_lo[:, co:co + 512], start=False, stop=False)
                last = e.matmul(sl, A.lo_fm[:, c * 128:(c + 1) * 128],
                                A.wq_hi[:, co:co + 512], start=False, stop=(c == 7))
        return last
    pe_q = EMIT(g, "tensor", qproj,
                waits=[(sm.cv, cv_el), (sm.cv, g.proA["cv"])] + W(g, "ps.proj"),
                inc=(sm.pe, 1))
    cv_q = EMIT(g, "vector", lambda e: e.tensor_tensor(out=A.xt, in0=ps.proj[:, :], in1=A.bq,
                                                       op=ALU.add),
                waits=[(sm.pe, pe_q)], inc=(sm.cv, 1))

    def kvproj(e):
        last = None
        for c in range(8):
            sl = ps.proj[:, 0:512]
            last = e.matmul(sl, A.hi_fm[:, c * 128:(c + 1) * 128],
                            A.wkv_hi[:, c * 512:(c + 1) * 512], start=(c == 0), stop=False)
            last = e.matmul(sl, A.hi_fm[:, c * 128:(c + 1) * 128],
                            A.wkv_lo[:, c * 512:(c + 1) * 512], start=False, stop=False)
            last = e.matmul(sl, A.lo_fm[:, c * 128:(c + 1) * 128],
                            A.wkv_hi[:, c * 512:(c + 1) * 512], start=False, stop=(c == 7))
        return last
    pe_kv = EMIT(g, "tensor", kvproj, waits=[(sm.cv, cv_q)], inc=(sm.pe, 1))
    cv_kv = EMIT(g, "vector", lambda e: e.tensor_tensor(out=A.kvsb, in0=ps.proj[:, 0:512],
                                                        in1=A.bkv, op=ALU.add),
                 waits=[(sm.pe, pe_kv)] + W(g, "A.kvsb"), inc=(sm.cv, 1))
    MARK(g, "ps.proj", sm.cv, cv_kv)
    MARK(g, "A.hi_fm", sm.pe, pe_kv)
    MARK(g, "A.lo_fm", sm.pe, pe_kv)
    MARK(g, "A.xn", sm.cv, cv_lo)

    cs_q2 = EMIT(g, "scalar", lambda e: e.activation(out=A.stile, in_=A.xt, func=AF.Square),
                 waits=[(sm.cv, cv_q), (sm.cs, cs1)] + W(g, "A.stile"), inc=(sm.cs, 1))
    cs_k2 = EMIT(g, "scalar", lambda e: e.activation(out=A.k2s, in_=A.kvsb[:, 0:256],
                                                     func=AF.Square),
                 waits=[(sm.cv, cv_kv)] + W(g, "A.k2s"), inc=(sm.cs, 1))
    cv_ssq = EMIT(g, "vector", lambda e: e.tensor_reduce(
        out=sb.rss[:, 0:16], in_=A.stile.rearrange("p (h d) -> p h d", h=16),
        axis=AX.X, op=ALU.add), waits=[(sm.cs, cs_q2)] + W(g, "sb.rss"), inc=(sm.cv, 1))
    cv_ssk = EMIT(g, "vector", lambda e: e.tensor_reduce(
        out=sb.rss[:, 16:20], in_=A.k2s.rearrange("p (h d) -> p h d", h=4),
        axis=AX.X, op=ALU.add), waits=[(sm.cs, cs_k2)], inc=(sm.cv, 1))
    MARK(g, "A.stile", sm.cv, cv_ssq)
    MARK(g, "A.k2s", sm.cv, cv_ssk)
    cs_l1 = EMIT(g, "scalar", lambda e: e.activation(out=sb.rss[:, 0:16], in_=sb.rss[:, 0:16],
                                                     func=AF.Ln, scale=1.0 / 64, bias=sb.fc[:, 1:2]),
                 waits=[(sm.cv, cv_ssq)], inc=(sm.cs, 1))
    cs_e1 = EMIT(g, "scalar", lambda e: e.activation(out=sb.rss[:, 0:16], in_=sb.rss[:, 0:16],
                                                     func=AF.Exp, scale=-0.5, bias=sb.fc[:, 2:3]),
                 waits=[(sm.cs, cs_l1)], inc=(sm.cs, 1))
    cs_l2 = EMIT(g, "scalar", lambda e: e.activation(out=sb.rss[:, 16:20], in_=sb.rss[:, 16:20],
                                                     func=AF.Ln, scale=1.0 / 64, bias=sb.fc[:, 1:2]),
                 waits=[(sm.cv, cv_ssk), (sm.cs, cs_e1)], inc=(sm.cs, 1))
    cs_e2 = EMIT(g, "scalar", lambda e: e.activation(out=sb.rss[:, 16:20], in_=sb.rss[:, 16:20],
                                                     func=AF.Exp, scale=-0.5),
                 waits=[(sm.cs, cs_l2)], inc=(sm.cs, 1))
    cv_q1 = EMIT(g, "vector", lambda e: e.tensor_tensor(
        out=A.xn.rearrange("p (h d) -> p h d", h=16),
        in0=A.xt.rearrange("p (h d) -> p h d", h=16),
        in1=sb.rss[:, 0:16].unsqueeze(2).to_broadcast([128, 16, 64]), op=ALU.mult),
        waits=[(sm.cs, cs_e1), (sm.cv, cv_lo)], inc=(sm.cv, 1))
    cv_q2 = EMIT(g, "vector", lambda e: e.tensor_tensor(out=A.xn, in0=A.xn, in1=A.qn,
                                                        op=ALU.mult),
                 waits=[(sm.cv, cv_q1)], inc=(sm.cv, 1))
    MARK(g, "A.xt", sm.cv, cv_q1)
    cv_k1 = EMIT(g, "vector", lambda e: e.tensor_tensor(
        out=A.khat.rearrange("p (h d) -> p h d", h=4),
        in0=A.kvsb[:, 0:256].rearrange("p (h d) -> p h d", h=4),
        in1=sb.rss[:, 16:20].unsqueeze(2).to_broadcast([128, 4, 64]), op=ALU.mult),
        waits=[(sm.cs, cs_e2)] + W(g, "A.khat"), inc=(sm.cv, 1))
    cv_k2 = EMIT(g, "vector", lambda e: e.tensor_tensor(out=A.khat, in0=A.khat, in1=A.kn,
                                                        op=ALU.mult),
                 waits=[(sm.cv, cv_k1)], inc=(sm.cv, 1))
    MARK(g, "sb.rss", sm.cv, cv_k1)
    cv_qh = EMIT(g, "vector", lambda e: e.tensor_copy(out=A.hi_tm, in_=A.xn),
                 waits=[(sm.cv, cv_q2), (sm.pe, pe1)], inc=(sm.cv, 1))
    cv_qt = EMIT(g, "vector", lambda e: e.tensor_tensor(out=A.tmpf, in0=A.xn, in1=A.hi_tm,
                                                        op=ALU.subtract),
                 waits=[(sm.cv, cv_qh)] + W(g, "A.tmpf"), inc=(sm.cv, 1))
    cv_ql = EMIT(g, "vector", lambda e: e.tensor_copy(out=A.lo_tm, in_=A.tmpf),
                 waits=[(sm.cv, cv_qt), (sm.pe, pe2)], inc=(sm.cv, 1))
    MARK(g, "A.xn", sm.cv, cv_qt, add=("q" in g.debug))
    cv_kh = EMIT(g, "vector", lambda e: e.tensor_copy(out=A.k_hi, in_=A.khat),
                 waits=[(sm.cv, cv_k2)] + W(g, "A.k_hi"), inc=(sm.cv, 1))
    cv_kt = EMIT(g, "vector", lambda e: e.tensor_tensor(out=A.tmpf[:, 0:256], in0=A.khat,
                                                        in1=A.k_hi, op=ALU.subtract),
                 waits=[(sm.cv, cv_kh), (sm.cv, cv_ql)], inc=(sm.cv, 1))
    cv_kl = EMIT(g, "vector", lambda e: e.tensor_copy(out=A.k_lo, in_=A.tmpf[:, 0:256]),
                 waits=[(sm.cv, cv_kt)] + W(g, "A.k_lo"), inc=(sm.cv, 1))
    MARK(g, "A.khat", sm.cv, cv_kt, add=("kv" in g.debug))
    MARK(g, "A.tmpf", sm.cv, cv_kl)
    cv_v = EMIT(g, "vector", lambda e, tau=tau: e.tensor_copy(
        out=A.vbuf[:, tau * 256:(tau + 1) * 256], in_=A.kvsb[:, 256:512]),
        waits=[(sm.cv, cv_kv)], inc=(sm.cv, 1))
    MARK(g, "A.kvsb", sm.cv, cv_v, add=("kv" in g.debug))
    MARK(g, "A.kvsb", sm.cv, cv_k1, add=True)

    if "q" in g.debug:
        _sq = EMIT(g, "sync", lambda e, tau=tau: e.dma_start(
            out=g.dbg["q"][tau * 128:tau * 128 + 128, :], in_=A.xn),
            waits=[(sm.cv, cv_q2)], inc=(sm.st, 16))
        MARK(g, "A.xn", sm.st, _sq, add=True)
    if "kv" in g.debug:
        _sk = EMIT(g, "sync", lambda e, tau=tau: e.dma_start(
            out=g.dbg["kv"][tau * 128:tau * 128 + 128, 0:256], in_=A.khat),
            waits=[(sm.cv, cv_k2)], inc=(sm.st, 16))
        MARK(g, "A.khat", sm.st, _sk, add=True)
        _sk2 = EMIT(g, "sync", lambda e, tau=tau: e.dma_start(
            out=g.dbg["kv"][tau * 128:tau * 128 + 128, 256:512], in_=A.kvsb[:, 256:512]),
            waits=[(sm.cv, cv_kv)], inc=(sm.st, 16))
        MARK(g, "A.kvsb", sm.st, _sk2, add=True)
    def ktr(e):
        last = None
        for kvh in range(4):
            last = e.transpose(ps.tr[0:64, kvh * 128:kvh * 128 + 128],
                               A.k_hi[:, kvh * 64:(kvh + 1) * 64], sb.ident[:])
            last = e.transpose(ps.tr[64:128, kvh * 128:kvh * 128 + 128],
                               A.k_lo[:, kvh * 64:(kvh + 1) * 64], sb.ident[:])
        return last
    pe_k = EMIT(g, "tensor", ktr, waits=[(sm.cv, cv_kl)] + W(g, "ps.tr"), inc=(sm.pe, 1))
    cv_ke = EMIT(g, "vector", lambda e, tau=tau: e.tensor_copy(
        out=A.khl.rearrange("p (kv t) -> p kv t", kv=4)[:, :, tau * 128:tau * 128 + 128],
        in_=ps.tr[:, 0:512].rearrange("p (kv t) -> p kv t", kv=4)),
        waits=[(sm.pe, pe_k)], inc=(sm.cv, 1))
    MARK(g, "A.k_hi", sm.pe, pe_k)
    MARK(g, "A.k_lo", sm.pe, pe_k)

    def qtr_hi(e):
        last = None
        for h in range(16):
            pr = (h % 2) * 64
            last = e.transpose(ps.tr[pr:pr + 64, (h // 2) * 128:(h // 2) * 128 + 128],
                               A.hi_tm[:, h * 64:(h + 1) * 64], sb.ident[:])
        return last
    pe_qh = EMIT(g, "tensor", qtr_hi, waits=[(sm.cv, cv_qh), (sm.cv, cv_ke)], inc=(sm.pe, 1))

    def qev_hi(e):
        last = None
        for h in range(16):
            pr = (h % 2) * 64
            src = ps.tr[pr:pr + 64, (h // 2) * 128:(h // 2) * 128 + 128]
            last = e.tensor_copy(out=A.lhsT1[0:64, h * 128:(h + 1) * 128], in_=src)
            last = e.tensor_copy(out=A.lhsT1x[64:128, h * 128:(h + 1) * 128], in_=src)
        return last
    cv_qeh = EMIT(g, "vector", qev_hi, waits=[(sm.pe, pe_qh)] + W(g, "A.lhsT1.r"),
                  inc=(sm.cv, 1))

    def qtr_lo(e):
        last = None
        for h in range(16):
            pr = (h % 2) * 64
            last = e.transpose(ps.tr[pr:pr + 64, (h // 2) * 128:(h // 2) * 128 + 128],
                               A.lo_tm[:, h * 64:(h + 1) * 64], sb.ident[:])
        return last
    pe_ql = EMIT(g, "tensor", qtr_lo, waits=[(sm.cv, cv_ql), (sm.cv, cv_qeh)], inc=(sm.pe, 1))

    def qev_lo(e):
        last = None
        for h in range(16):
            pr = (h % 2) * 64
            src = ps.tr[pr:pr + 64, (h // 2) * 128:(h // 2) * 128 + 128]
            last = e.tensor_copy(out=A.lhsT1[64:128, h * 128:(h + 1) * 128], in_=src)
            last = e.tensor_copy(out=A.lhsT1x[0:64, h * 128:(h + 1) * 128], in_=src)
        return last
    cv_qel = EMIT(g, "vector", qev_lo, waits=[(sm.pe, pe_ql)], inc=(sm.cv, 1))
    MARK(g, "ps.tr", sm.cv, cv_qel)
    MARK(g, "A.hi_tm", sm.pe, pe_qh)
    MARK(g, "A.lo_tm", sm.pe, pe_ql)

    kready = max(cv_ke, cv_qel)
    for h in range(H):
        _phaseA_head(g, tau, h, kready)

    yw = g.war["A.yfm.w"]

    def oproj(e):
        last = None
        for oc in range(2):
            for c in range(8):
                last = e.matmul(ps.proj[:, oc * 512:(oc + 1) * 512],
                                A.yfm[:, c * 128:(c + 1) * 128],
                                A.wo[:, c * 1024 + oc * 512: c * 1024 + oc * 512 + 512],
                                start=(c == 0), stop=(c == 7))
        return last
    pe_s = EMIT(g, "tensor", oproj, waits=yw + W(g, "ps.proj"), inc=(sm.pe, 1))
    cv_s = EMIT(g, "vector", lambda e: e.tensor_tensor(out=A.stile, in0=ps.proj[:, :],
                                                       in1=A.bo, op=ALU.add),
                waits=[(sm.pe, pe_s)] + W(g, "A.stile"), inc=(sm.cv, 1))
    st_s = EMIT(g, "sync", lambda e, tau=tau: e.dma_start(
        out=dr["s_scr"][tau * 128:tau * 128 + 128, :], in_=A.stile),
        waits=[(sm.cv, cv_s)], inc=(sm.st, 16))
    MARK(g, "ps.proj", sm.cv, cv_s)
    MARK(g, "A.stile", sm.st, st_s)
    MARK(g, "A.yfm.r", sm.pe, pe_s)


def _phaseA_head(g, tau, h, kready):
    nc, sb, A, ps, dr, sm = g.nc, g.sb, g.A, g.ps, g.dr, g.sm
    L = (tau + 1) * 128
    kv = h // 4

    rounds = [(0, min(L, 2048))]
    if L > 2048:
        rounds.append((2048, L))
    cs_ev = None
    pe_sc = None
    for (c0, c1) in rounds:
        def smm(e, c0=c0, c1=c1, h=h, kv=kv):
            last = None
            for cc in range(c0, c1, 512):
                w = min(512, c1 - cc)
                sl = ps.S[:, cc - c0:cc - c0 + w]
                last = e.matmul(sl, A.lhsT1[:, h * 128:(h + 1) * 128],
                                A.khl[:, kv * T + cc: kv * T + cc + w],
                                start=True, stop=False)
                last = e.matmul(sl, A.lhsT1x[:, h * 128:(h + 1) * 128],
                                A.khl[:, kv * T + cc: kv * T + cc + w],
                                start=False, stop=False)
                last = e.matmul(sl, A.qaug[:, h * 128:(h + 1) * 128],
                                A.kaug[:, cc:cc + w], start=False, stop=True)
            return last
        pe_sc = EMIT(g, "tensor", smm, waits=[(sm.cv, kready)] + W(g, "ps.S"),
                     inc=(sm.pe, 1))
        cs_ev = EMIT(g, "scalar", lambda e, c0=c0, c1=c1: e.copy(out=A.S[:, c0:c1],
                                                                 in_=ps.S[:, 0:c1 - c0]),
                     waits=[(sm.pe, pe_sc)] + (W(g, "A.S") if c0 == 0 else []),
                     inc=(sm.cs, 1))
        MARK(g, "ps.S", sm.cs, cs_ev)
    MARK(g, "A.lhsT1.r", sm.pe, pe_sc)

    cv_dg = EMIT(g, "vector", lambda e, L=L: e.tensor_tensor(
        out=A.S[:, L - 128:L], in0=A.S[:, L - 128:L], in1=sb.diagmask[:], op=ALU.add),
        waits=[(sm.cs, cs_ev)], inc=(sm.cv, 1))

    if tau <= 3:
        if L < 512:
            EMIT(g, "vector", lambda e, L=L: e.memset(A.coll[:, L:512], NEG),
                 waits=W(g, "A.coll"), inc=(sm.cv, 1))
        cv_coll = EMIT(g, "vector", lambda e, L=L: e.tensor_copy(out=A.coll[:, 0:L],
                                                                 in_=A.S[:, 0:L]),
                       waits=[(sm.cv, cv_dg)] + W(g, "A.coll"), inc=(sm.cv, 1))
    else:
        def strips(e, L=L):
            S3 = A.S[:, 0:L].rearrange("p (a b) -> p a b", b=64)
            last = None
            for i in range(64):
                last = e.max(out=A.coll[:, i * 8:(i + 1) * 8], in_=S3[:, :, i])
            return last
        cv_coll = EMIT(g, "vector", strips, waits=[(sm.cv, cv_dg)] + W(g, "A.coll"),
                       inc=(sm.cv, 1))

    prev = cv_coll
    for r in range(8):
        cv_m = EMIT(g, "vector", lambda e: e.max(out=sb.m8[:, :], in_=A.coll[:, :]),
                    waits=[(sm.cv, prev)] + W(g, "sb.m8"), inc=(sm.cv, 1))
        if r == 0:
            EMIT(g, "vector", lambda e: e.tensor_scalar(
                out=sb.thr[:, 1:2], in0=sb.m8[:, 0:1], scalar1=-1.0, scalar2=None,
                op0=ALU.mult), waits=[(sm.cv, cv_m)] + W(g, "sb.thr"), inc=(sm.cv, 1))
        if r < 7:
            prev = EMIT(g, "vector", lambda e: e.match_replace(
                out=A.coll[:, :], in_to_replace=sb.m8[:, :], in_values=A.coll[:, :],
                imm_value=NEG), waits=[(sm.cv, cv_m)], inc=(sm.cv, 1))
            MARK(g, "sb.m8", sm.cv, prev)
        else:
            cv_th = EMIT(g, "vector", lambda e: e.tensor_copy(out=sb.thr[:, 0:1],
                                                              in_=sb.m8[:, 7:8]),
                         waits=[(sm.cv, cv_m)], inc=(sm.cv, 1))
            MARK(g, "sb.m8", sm.cv, cv_th)
            MARK(g, "A.coll", sm.cv, cv_th)
    cv_done = cv_th

    cs_exp = EMIT(g, "scalar", lambda e, L=L: e.activation(
        out=A.P[:, 0:L], in_=A.S[:, 0:L], func=AF.Exp, bias=sb.thr[:, 1:2]),
        waits=[(sm.cv, cv_done)] + W(g, "A.P"), inc=(sm.cs, 1))

    cv_mk = EMIT(g, "vector", lambda e, L=L: e.scalar_tensor_tensor(
        out=A.P[:, 0:L], in0=A.S[:, 0:L], scalar=sb.thr[:, 0:1], in1=A.P[:, 0:L],
        op0=ALU.is_ge, op1=ALU.mult, accum_out=sb.zrec[:, 0:1]),
        waits=[(sm.cs, cs_exp)] + W(g, "sb.zrec"), inc=(sm.cv, 1))
    MARK(g, "A.S", sm.cv, cv_mk)
    MARK(g, "sb.thr", sm.cv, cv_mk)
    if "thr" in g.debug:
        _stv = EMIT(g, "sync", lambda e, tau=tau, h=h: e.dma_start(
            out=g.dbg["thr"][tau * 128:tau * 128 + 128, h:h + 1], in_=sb.thr[:, 0:1],
            allow_slow_non_contiguous=True),
            waits=[(sm.cv, cv_done)], inc=(sm.st, 16))
        MARK(g, "sb.thr", sm.st, _stv, add=True)
    cv_rz = EMIT(g, "vector", lambda e: e.reciprocal(out=sb.zrec[:, 1:2],
                                                     in_=sb.zrec[:, 0:1]),
                 waits=[(sm.cv, cv_mk)], inc=(sm.cv, 1))
    cv_pn = EMIT(g, "vector", lambda e, L=L: e.tensor_scalar(
        out=A.P[:, 0:L], in0=A.P[:, 0:L], scalar1=sb.zrec[:, 1:2], scalar2=None,
        op0=ALU.mult), waits=[(sm.cv, cv_rz)], inc=(sm.cv, 1))
    MARK(g, "sb.zrec", sm.cv, cv_pn)
    if "z" in g.debug:
        _stz = EMIT(g, "sync", lambda e, tau=tau, h=h: e.dma_start(
            out=g.dbg["z"][tau * 128:tau * 128 + 128, h:h + 1], in_=sb.zrec[:, 0:1],
            allow_slow_non_contiguous=True),
            waits=[(sm.cv, cv_mk)], inc=(sm.st, 16))
        MARK(g, "sb.zrec", sm.st, _stz, add=True)

    ntile = L // 128
    pe_y = None
    pe_pt = None
    for b0 in range(0, ntile, 8):
        b1 = min(ntile, b0 + 8)

        def ptr(e, b0=b0, b1=b1):
            last = None
            for i in range(b0, b1):
                last = e.transpose(ps.tr[:, (i - b0) * 128:(i - b0 + 1) * 128],
                                   A.P[:, i * 128:(i + 1) * 128], sb.ident[:])
            return last
        pe_pt = EMIT(g, "tensor", ptr, waits=[(sm.cv, cv_pn)] + W(g, "ps.tr"),
                     inc=(sm.pe, 1))
        cv_pt = EMIT(g, "vector", lambda e, b0=b0, b1=b1: e.tensor_copy(
            out=A.PT[:, 0:(b1 - b0) * 128], in_=ps.tr[:, 0:(b1 - b0) * 128]),
            waits=[(sm.pe, pe_pt)] + W(g, "A.PT"), inc=(sm.cv, 1))
        MARK(g, "ps.tr", sm.cv, cv_pt)

        def ymm(e, b0=b0, b1=b1, ntile=ntile, kv=kv):
            last = None
            for i in range(b0, b1):
                last = e.matmul(ps.y[0:64, 0:128],
                                A.vbuf[:, i * 256 + kv * 64:i * 256 + kv * 64 + 64],
                                A.PT[:, (i - b0) * 128:(i - b0 + 1) * 128],
                                start=(i == 0), stop=(i == ntile - 1))
            return last
        pe_y = EMIT(g, "tensor", ymm,
                    waits=[(sm.cv, cv_pt)] + (W(g, "ps.y") if b0 == 0 else []),
                    inc=(sm.pe, 1))
        MARK(g, "A.PT", sm.pe, pe_y)
    MARK(g, "A.P", sm.pe, pe_pt)
    cv_y = EMIT(g, "vector", lambda e, h=h: e.tensor_copy(
        out=A.yfm[(h % 2) * 64:(h % 2) * 64 + 64, (h // 2) * 128:(h // 2) * 128 + 128],
        in_=ps.y[0:64, 0:128]),
        waits=[(sm.pe, pe_y)] + (W(g, "A.yfm.r") if h == 0 else []), inc=(sm.cv, 1))
    MARK(g, "ps.y", sm.cv, cv_y)
    MARK(g, "A.PT", sm.pe, pe_y)
    MARK(g, "A.yfm.w", sm.cv, cv_y)

def _phaseB(g, mini=False):
    nc, sb, dr, sm, ar = g.nc, g.sb, g.dr, g.sm, g.ar
    B = G()
    g.B = B
    rb = Reg(ar)
    B.rcsq = rb.alloc(128, NT * 64, F32)
    B.rssq = rb.alloc(128, NT * 64, F32)
    B.rcsk = rb.alloc(128, NT * 64, F32)
    B.rssk = rb.alloc(128, NT * 64, F32)
    B.qsc = rb.alloc(128, 1024, F32)
    B.ksc = rb.alloc(128, 1024, F32)
    B.lvec = rb.alloc(128, 72, F32)
    B.bndtab = rb.alloc(128, 8, F32)
    B.bmask = rb.alloc(128, 3 * 384, BF16)
    B.onesrow = rb.alloc(1, 128, F32)
    B.rtile = rb.alloc(128, 8 * 1025, F32)
    B.bnd = rb.alloc(128, 8, F32)
    common_end = rb.off
    r1 = Reg(ar, common_end)
    B.wqkv = r1.alloc(128, 8 * 3072, BF16)
    B.kTring = r1.alloc(128, 8 * 1280, BF16)
    B.vring = r1.alloc(128, 10 * 1024, BF16)
    B.qt = r1.alloc(128, 1024, F32)
    B.kt = r1.alloc(128, 1024, F32)
    _woff = (r1.off + 511) // 512 * 512
    B.t0 = r1.alloc(128, 1024, F32)
    B.t1 = r1.alloc(128, 1024, F32)
    B.t2 = B.qt
    B.ainT = r1.alloc(128, 8 * 128, BF16)
    B.qhb = r1.alloc(128, 1024, BF16)
    B.khb = r1.alloc(128, 1024, BF16)
    B.qT = r1.alloc(128, 8 * 128, BF16)
    B.woin = ar.view(_woff, 128, 8 * 1024, BF16)
    assert r1.off - _woff >= 16 * 1024
    B.Ssb = [r1.alloc(128, 384, F32), r1.alloc(128, 384, F32)]
    B.Pb = [r1.alloc(128, 384, BF16), r1.alloc(128, 384, BF16)]
    B.PT = [r1.alloc(128, 384, BF16), r1.alloc(128, 384, BF16)]
    B.yfm = r1.alloc(128, 8 * 512, BF16)
    B.zr = r1.alloc(128, 8, F32)
    b1_end = r1.off
    r2 = Reg(ar, common_end)
    B.w1buf = [r2.alloc(128, 2048, BF16), r2.alloc(128, 2048, BF16)]
    B.w2buf = [r2.alloc(128, 1024, BF16), r2.alloc(128, 1024, BF16)]
    B.xnf = r2.alloc(128, 8 * 1024, BF16)
    B.hbuf = [r2.alloc(128, 22 * 512, BF16), r2.alloc(128, 22 * 512, BF16)]
    B.rbf = r2.alloc(128, 8 * 1025, BF16)
    B.xsc = r2.alloc(128, 1024, F32)
    B.gateg = [r2.alloc(128, 512, BF16), r2.alloc(128, 512, BF16)]
    B.mub = r2.alloc(128, 1024, BF16)
    B.rsb = r2.alloc(128, 1024, F32)
    B.st2 = r2.alloc(1, 2 * 1024, F32)
    b2_end = r2.off
    assert max(b1_end, b2_end) <= ar.nbytes, (b1_end, b2_end, ar.nbytes)

    bar = [(sm.pe, sm.pe.n), (sm.cv, sm.cv.n), (sm.cs, sm.cs.n),
           (sm.st, sm.st.n), (sm.sg, sm.sg.n), (sm.cg, sm.cg.n), (sm.ld, sm.ld.n),
           (sm.lg, sm.lg.n), (sm.lw, sm.lw.n), (sm.lr, sm.lr.n)]
    g.war_floor = bar
    for dst, srcn in [(B.bndtab, "bndtab"), (B.onesrow, "onesrow"),
                      (B.rcsq, "rcsq"), (B.rssq, "rssq"), (B.rcsk, "rcsk"),
                      (B.rssk, "rssk")]:
        EMIT(g, "sync", lambda e, dst=dst, srcn=srcn: e.dma_start(out=dst, in_=dr[srcn][:]),
             waits=bar, inc=(sm.ld, 16))
    EMIT(g, "gpsimd", lambda e: e.dma_start(out=B.bmask, in_=dr["bmask"][:]),
         waits=bar, inc=(sm.lg, 16))
    g.Bld0 = sm.ld.n
    g.mini = mini
    if mini:
        _phaseB_layer(g, 0)
        if "rl" in g.debug:
            EMIT(g, "sync", lambda e: e.dma_start(out=g.dbg["rl"][0], in_=dr["r_scr"][:]),
                 waits=[(sm.st, sm.st.n), (sm.cv, sm.cv.n)], inc=(sm.ld, 16))
        return
    if "rl" in g.debug:
        EMIT(g, "sync", lambda e: e.dma_start(out=g.dbg["rl"][0], in_=dr["r_scr"][:]),
             waits=[(sm.st, sm.st.n), (sm.cv, sm.cv.n)], inc=(sm.ld, 16))
    for l in range(DEPTH):
        _phaseB_layer(g, l)
        if "rl" in g.debug:
            EMIT(g, "sync", lambda e, l=l: e.dma_start(out=g.dbg["rl"][l + 1], in_=dr["r_scr"][:]),
                 waits=[(sm.st, sm.st.n), (sm.cv, sm.cv.n)], inc=(sm.ld, 16))


def _phaseB_layer(g, l):
    nc, sb, dr, sm, B = g.nc, g.sb, g.dr, g.sm, g.B
    WCWO = 8 * 3072
    WCW1 = 8 * 3072 + 8 * 1024
    WCW2 = WCW1 + 22 * 2048
    for dst, srcn in [(B.lvec[:, 0:22], "b1a"), (B.lvec[:, 22:44], "b1g"),
                      (B.lvec[:, 44:52], "b2"), (B.lvec[:, 52:60], "lng"),
                      (B.lvec[:, 60:68], "lnb"), (B.qsc, "qsc"), (B.ksc, "ksc")]:
        EMIT(g, "sync", lambda e, dst=dst, srcn=srcn: e.dma_start(out=dst, in_=dr[srcn][l]),
             waits=W(g, "B.lconsts") + [(sm.ld, g.Bld0)], inc=(sm.ld, 16))
    g.ld_lv = sm.ld.n
    ldq = EMIT(g, "sync", lambda e: e.dma_start(
        out=B.wqkv, in_=dr["wbf"][l, :, 0:8 * 3072]),
        waits=W(g, "B.wqkv") + [(sm.lg, g.wbf_rdy)], inc=(sm.lw, 16))
    g.ld_wqkv = ldq
    for wb in range(getattr(g, "mini_n", None) or (1 if getattr(g, "mini", False) else 4)):
        _phaseB_attn_wb(g, l, wb, WCWO)
    if "ra" in g.debug and l == 0:
        EMIT(g, "sync", lambda e: e.dma_start(out=g.dbg["ra"], in_=dr["r_scr"][:]),
             waits=[(sm.st, sm.st.n), (sm.cv, sm.cv.n), (sm.cs, sm.cs.n)], inc=(sm.ld, 16))
        MARK(g, "B.lconsts", sm.cv, sm.cv.n)
    for pc in range(getattr(g, "mini_n", None) or (1 if getattr(g, "mini", False) else 4)):
        _phaseB_ffn_pair(g, l, pc, WCW1)
    MARK(g, "B.lconsts", sm.cv, sm.cv.n)
    MARK(g, "B.lconsts", sm.cs, sm.cs.n, add=True)
    MARK(g, "B.lconsts", sm.cg, sm.cg.n, add=True)


def _load_rtile(g, t0, ncols, first_seg):
    nc, dr, sm, B = g.nc, g.dr, g.sm, g.B
    rv = B.rtile.rearrange("p (c t) -> p c t", c=8)
    rsrc = dr["r_init"] if getattr(g, "mini", False) else dr["r_scr"]
    ldr = EMIT(g, "sync", lambda e: e.dma_start(
        out=rv[:, :, 1:1 + ncols],
        in_=rsrc.rearrange("p (c t) -> p c t", c=8)[:, :, t0:t0 + ncols]),
        waits=W(g, "B.rtile"), inc=(sm.lr, 16))
    if first_seg:
        cvz = EMIT(g, "vector", lambda e: e.memset(rv[:, :, 0:1], 0.0),
                   waits=[(sm.lr, ldr)], inc=(sm.cv, 1))
    else:
        cvz = EMIT(g, "vector", lambda e: e.tensor_copy(out=rv[:, :, 0:1], in_=B.bnd.unsqueeze(2)),
                   waits=[(sm.lr, ldr)] + W(g, "B.bnd"), inc=(sm.cv, 1))
    r_ready = [(sm.lr, ldr), (sm.cv, cvz)]
    cvb = EMIT(g, "vector", lambda e: e.tensor_copy(out=B.bnd.unsqueeze(2),
                                                    in_=rv[:, :, ncols:ncols + 1]),
               waits=r_ready, inc=(sm.cv, 1))
    MARK(g, "B.bnd", sm.cv, cvb)
    return rv, r_ready


def _phaseB_attn_wb(g, l, wb, WCWO):
    nc, sb, dr, sm, B, ps = g.nc, g.sb, g.dr, g.sm, g.B, g.ps
    t0 = wb * 1024
    rv, r_ready = _load_rtile(g, t0, 1024, wb == 0)
    for ti in range(8):
        _phaseB_tile(g, l, wb, ti, rv, r_ready)
        if ti == 3 or ti == 7:
            _phaseB_oproj(g, l, wb, ti // 4, rv, WCWO, r_ready)
    st_r = EMIT(g, "sync", lambda e: e.dma_start(
        out=dr["r_scr"].rearrange("p (c t) -> p c t", c=8)[:, :, t0:t0 + 1024],
        in_=rv[:, :, 1:1025]),
        waits=W(g, "B.rtile.w"), inc=(sm.st, 16))
    MARK(g, "B.rtile", sm.st, st_r)


def _phaseB_tile(g, l, wb, ti, rv, r_ready):
    nc, sb, dr, sm, B, ps = g.nc, g.sb, g.dr, g.sm, g.B, g.ps
    gt = wb * 8 + ti
    slot = gt % 10
    c0 = ti * 128

    at = B.ainT.rearrange("p (c t) -> p c t", c=8)
    cva = EMIT(g, "vector", lambda e: e.tensor_copy(out=at[:, 0:4, :], in_=rv[:, 0:4, 1 + c0:1 + c0 + 128]),
               waits=r_ready + W(g, "B.ainT"), inc=(sm.cv, 1))
    cva2 = EMIT(g, "vector", lambda e: e.tensor_copy(out=at[:, 4:8, :], in_=rv[:, 4:8, c0:c0 + 128]),
                waits=r_ready, inc=(sm.cv, 1))
    a_rdy = [(sm.cv, cva), (sm.cv, cva2)]

    def proj(e, o0):
        last = None
        for oc in range(2):
            sl = ps.proj[:, oc * 512:(oc + 1) * 512]
            for c in range(8):
                last = e.matmul(sl, at[:, c, :],
                                B.wqkv[:, c * 3072 + (o0 + oc) * 512: c * 3072 + (o0 + oc) * 512 + 512],
                                start=(c == 0), stop=(c == 7))
        return last
    pe_q = EMIT(g, "tensor", lambda e: proj(e, 0),
                waits=a_rdy + W(g, "ps.proj") + [(sm.lw, g.ld_wqkv)], inc=(sm.pe, 1))
    cv_q = EMIT(g, "scalar", lambda e: e.copy(out=B.qt, in_=ps.proj[:, :]),
                waits=[(sm.pe, pe_q)] + W(g, "B.qt") + W(g, "B.t2"), inc=(sm.cs, 1))
    MARK(g, "ps.proj", sm.cs, cv_q)
    pe_k = EMIT(g, "tensor", lambda e: proj(e, 2),
                waits=[(sm.cs, cv_q)], inc=(sm.pe, 1))
    cv_k = EMIT(g, "scalar", lambda e: e.copy(out=B.kt, in_=ps.proj[:, :]),
                waits=[(sm.pe, pe_k)] + W(g, "B.kt"), inc=(sm.cs, 1))
    MARK(g, "ps.proj", sm.cs, cv_k)
    pe_v = EMIT(g, "tensor", lambda e: proj(e, 4),
                waits=[(sm.cs, cv_k)], inc=(sm.pe, 1))
    cv_v = EMIT(g, "scalar", lambda e: e.copy(
        out=B.vring[:, slot * 1024:(slot + 1) * 1024], in_=ps.proj[:, :]),
        waits=[(sm.pe, pe_v)] + W(g, "B.vring%d" % slot), inc=(sm.cs, 1))
    MARK(g, "ps.proj", sm.cs, cv_v)

    def norm_rot(which, src, scl, cs_t, ss_t, outb, prev):
        sqv = EMIT(g, "scalar", lambda e: e.activation(out=B.t0, in_=src, func=AF.Square),
                   waits=[(sm.cs, prev)] + W(g, "B.t0"), inc=(sm.cs, 1))
        ssv = EMIT(g, "vector", lambda e: e.tensor_reduce(
            out=sb.rss[:, 0:16], in_=B.t0.rearrange("p (h d) -> p h d", h=16),
            axis=AX.X, op=ALU.add), waits=[(sm.cs, sqv)] + W(g, "sb.rss"), inc=(sm.cv, 1))
        MARK(g, "B.t0", sm.cv, ssv)
        cl = EMIT(g, "vector", lambda e: e.tensor_scalar(
            out=sb.rss[:, 0:16], in0=sb.rss[:, 0:16], scalar1=1e-24, scalar2=None,
            op0=ALU.max), waits=[(sm.cv, ssv)], inc=(sm.cv, 1))
        l1 = EMIT(g, "scalar", lambda e: e.activation(out=sb.rss[:, 0:16], in_=sb.rss[:, 0:16],
                                                      func=AF.Ln),
                  waits=[(sm.cv, cl)], inc=(sm.cs, 1))
        l2 = EMIT(g, "scalar", lambda e: e.activation(out=sb.rss[:, 0:16], in_=sb.rss[:, 0:16],
                                                      func=AF.Exp, scale=-0.5),
                  waits=[(sm.cs, l1)], inc=(sm.cs, 1))
        MARK(g, "sb.rss", sm.cs, l2)
        m1 = EMIT(g, "vector", lambda e: e.tensor_tensor(
            out=B.t1.rearrange("p (h d) -> p h d", h=16),
            in0=src.rearrange("p (h d) -> p h d", h=16),
            in1=sb.rss[:, 0:16].unsqueeze(2).to_broadcast([128, 16, 64]), op=ALU.mult),
            waits=[(sm.cs, l2)] + W(g, "B.t1"), inc=(sm.cv, 1))
        MARK(g, "B.qt.rd", sm.cv, m1)
        m2 = EMIT(g, "gpsimd", lambda e: e.tensor_tensor(out=B.t1, in0=B.t1, in1=scl,
                                                         op=ALU.mult),
                  waits=[(sm.cv, m1), (sm.ld, g.ld_lv)], inc=(sm.cg, 1))
        csl = cs_t[:, gt * 64:(gt + 1) * 64]
        ssl = ss_t[:, gt * 64:(gt + 1) * 64]
        r1v = EMIT(g, "vector", lambda e: e.tensor_tensor(
            out=B.t2.rearrange("p (h d) -> p h d", h=16),
            in0=B.t1.rearrange("p (h d) -> p h d", h=16),
            in1=csl.unsqueeze(1).to_broadcast([128, 16, 64]), op=ALU.mult),
            waits=[(sm.cg, m2)] + W(g, "B.t2") + W(g, "B.qt.rd"), inc=(sm.cv, 1))
        t1v = B.t1.rearrange("p (h s d) -> p h s d", h=16, s=2)
        t0v = B.t0.rearrange("p (h s d) -> p h s d", h=16, s=2)
        r2v = EMIT(g, "vector", lambda e: e.tensor_tensor(
            out=t0v[:, :, 0, :], in0=t1v[:, :, 1, :],
            in1=ssl[:, 0:32].unsqueeze(1).to_broadcast([128, 16, 32]), op=ALU.mult),
            waits=[(sm.cg, m2), (sm.cv, ssv)] + W(g, "B.t0"), inc=(sm.cv, 1))
        r3v = EMIT(g, "vector", lambda e: e.tensor_tensor(
            out=t0v[:, :, 1, :], in0=t1v[:, :, 0, :],
            in1=ssl[:, 32:64].unsqueeze(1).to_broadcast([128, 16, 32]), op=ALU.mult),
            waits=[(sm.cg, m2)], inc=(sm.cv, 1))
        fin = EMIT(g, "vector", lambda e: e.tensor_tensor(out=outb, in0=B.t2, in1=B.t0,
                                                          op=ALU.add),
                   waits=[(sm.cv, r1v), (sm.cv, r2v), (sm.cv, r3v)] + W(g, "B." + which),
                   inc=(sm.cv, 1))
        MARK(g, "B.t0", sm.cv, fin)
        MARK(g, "B.t1", sm.cv, max(r1v, r2v, r3v))
        MARK(g, "B.t2", sm.cv, fin)
        return fin
    fq = norm_rot("qhb", B.qt, B.qsc, B.rcsq, B.rssq, B.qhb, cv_q)
    MARK(g, "B.qt", sm.cv, fq)
    fk = norm_rot("khb", B.kt, B.ksc, B.rcsk, B.rssk, B.khb, cv_k)
    MARK(g, "B.kt", sm.cv, fk)
    if "b0" in g.debug and l == 0 and gt == 0:
        for si, buf in [(0, B.qhb), (1, B.khb)]:
            _sv = EMIT(g, "gpsimd", lambda e, si=si, buf=buf: e.dma_start(
                out=g.dbg["b0"][si], in_=buf),
                waits=[(sm.cv, fq), (sm.cv, fk)], inc=(sm.sg, 16))
        MARK(g, "B.qhb", sm.sg, sm.sg.n, add=True)
        MARK(g, "B.khb", sm.sg, sm.sg.n, add=True)
        _sv = EMIT(g, "gpsimd", lambda e: e.dma_start(
            out=g.dbg["b0"][2], in_=B.vring[:, 0:1024]),
            waits=[(sm.cs, cv_v)], inc=(sm.sg, 16))
        MARK(g, "B.vring0", sm.sg, _sv)

    trv = ps.tr.bitcast(BF16)
    def qtr(e):
        last = None
        for h in range(16):
            pr = (h % 2) * 64
            last = e.transpose(trv[pr:pr + 64, (h // 2) * 128:(h // 2) * 128 + 128],
                               B.qhb[:, h * 64:(h + 1) * 64], sb.ident[:])
        return last
    pe_qt = EMIT(g, "tensor", qtr, waits=[(sm.cv, fq)] + W(g, "ps.tr"), inc=(sm.pe, 1))
    cv_qt = EMIT(g, "scalar", lambda e: e.copy(out=B.qT, in_=trv[:, 0:1024]),
                 waits=[(sm.pe, pe_qt)] + W(g, "B.qT"), inc=(sm.cs, 1))
    MARK(g, "ps.tr", sm.cs, cv_qt)

    def ktr(e):
        last = None
        for h in range(16):
            pr = (h % 2) * 64
            last = e.transpose(trv[pr:pr + 64, (h // 2) * 128:(h // 2) * 128 + 128],
                               B.khb[:, h * 64:(h + 1) * 64], sb.ident[:])
        return last
    pe_kt = EMIT(g, "tensor", ktr, waits=[(sm.cv, fk), (sm.cs, cv_qt)], inc=(sm.pe, 1))
    kr = B.kTring.rearrange("p (b s) -> p b s", b=8)
    cv_kt = EMIT(g, "scalar", lambda e: e.copy(
        out=kr[:, :, slot * 128:(slot + 1) * 128],
        in_=trv[:, 0:1024].rearrange("p (b s) -> p b s", b=8)),
        waits=[(sm.pe, pe_kt)] + W(g, "B.kring%d" % slot), inc=(sm.cs, 1))
    MARK(g, "ps.tr", sm.cs, cv_kt)
    MARK(g, "B.qhb", sm.pe, pe_qt)
    MARK(g, "B.khb", sm.pe, pe_kt)

    nw = min(3, gt + 1)
    var = min(gt, 2)
    slots = [(gt - nw + 1 + j) % 10 for j in range(nw)]
    runs = []
    run = [slots[0]]
    for s_ in slots[1:]:
        if s_ == run[-1] + 1:
            run.append(s_)
        else:
            runs.append(run)
            run = [s_]
    runs.append(run)
    kready = [(sm.cs, cv_kt), (sm.cs, cv_qt)]
    vready = [(sm.cs, cv_v)]
    sc = {}
    pns = {}
    for h in range(4):
        sc[h] = _phaseB_head_scores(g, h, runs, kready)
    for k in range(H + 2):
        if k >= 2:
            _phaseB_head_y(g, ti, k - 2, nw, slots, pns[k - 2], vready)
        if k < H:
            pns[k] = _phaseB_head_soft(g, l, gt, k, nw, var, sc[k])
        if k + 4 < H:
            sc[k + 4] = _phaseB_head_scores(g, k + 4, runs, kready)


def _phaseB_head_scores(g, h, runs, kready):
    nc, sb, sm, B, ps = g.nc, g.sb, g.sm, g.B, g.ps
    pr = (h % 2) * 64
    blk = h // 2
    kr = B.kTring.rearrange("p (b s) -> p b s", b=8)
    psl = (h % 4) * 512

    def smm(e):
        last = None
        col = 0
        for run in runs:
            w = len(run) * 128
            last = e.matmul(ps.S[:, psl + col: psl + col + w],
                            B.qT[pr:pr + 64, blk * 128:(blk + 1) * 128],
                            kr[pr:pr + 64, blk, run[0] * 128:run[0] * 128 + w],
                            start=True, stop=True)
            col += w
        return last
    return EMIT(g, "tensor", smm, waits=kready + W(g, "ps.S%d" % (h % 4)), inc=(sm.pe, 1))


def _phaseB_head_soft(g, l, gt, h, nw, var, pe_s):
    nc, sb, sm, B, ps = g.nc, g.sb, g.sm, g.B, g.ps
    pp = h % 2
    psl = (h % 4) * 512
    Pb = B.Pb[pp]
    Ssb = B.Ssb[pp]
    zr = B.zr[:, pp * 4:pp * 4 + 4]
    cv_ev = EMIT(g, "vector", lambda e: e.tensor_tensor(
        out=Ssb[:, 0:nw * 128], in0=ps.S[:, psl:psl + nw * 128],
        in1=B.bmask[:, var * 384: var * 384 + nw * 128], op=ALU.add),
        waits=[(sm.pe, pe_s)] + W(g, "B.Ssb%d" % pp), inc=(sm.cv, 1))
    MARK(g, "ps.S%d" % (h % 4), sm.cv, cv_ev)
    cv_mx = EMIT(g, "vector", lambda e: e.tensor_reduce(
        out=zr[:, 3:4], in_=Ssb[:, 0:nw * 128], axis=AX.X, op=ALU.max, negate=True),
        waits=[(sm.cv, cv_ev)] + W(g, "B.zrA%d" % pp), inc=(sm.cv, 1))
    cs_e = EMIT(g, "scalar", lambda e: e.activation(
        out=Pb[:, 0:nw * 128], in_=Ssb[:, 0:nw * 128], func=AF.Exp,
        bias=zr[:, 3:4], accum_out=zr[:, 0:1]),
        waits=[(sm.cv, cv_mx)] + W(g, "B.Pb%d" % pp) + W(g, "B.zrB%d" % pp),
        inc=(sm.cs, 1))
    MARK(g, "B.Ssb%d" % pp, sm.cs, cs_e)
    MARK(g, "B.zrA%d" % pp, sm.cs, cs_e)
    cv_rz = EMIT(g, "vector", lambda e: e.reciprocal(out=zr[:, 1:2], in_=zr[:, 0:1]),
                 waits=[(sm.cs, cs_e)], inc=(sm.cv, 1))
    cv_pn = EMIT(g, "vector", lambda e: e.tensor_scalar(
        out=Pb[:, 0:nw * 128], in0=Pb[:, 0:nw * 128], scalar1=zr[:, 1:2],
        scalar2=None, op0=ALU.mult),
        waits=[(sm.cv, cv_rz)], inc=(sm.cv, 1))
    MARK(g, "B.zrB%d" % pp, sm.cv, cv_pn)
    if "b0" in g.debug and l == 0 and gt == 0 and h == 0:
        _s2 = EMIT(g, "gpsimd", lambda e: e.dma_start(
            out=g.dbg["b0"][4, :, 0:nw * 128], in_=Pb[:, 0:nw * 128]),
            waits=[(sm.cv, cv_pn)], inc=(sm.sg, 16))
        MARK(g, "B.Pb%d" % pp, sm.sg, _s2, add=True)
    return cv_pn


def _phaseB_head_y(g, ti, h, nw, slots, cv_pn, vready):
    nc, sb, sm, B, ps = g.nc, g.sb, g.sm, g.B, g.ps
    pr = (h % 2) * 64
    blk = h // 2
    pp = h % 2
    Pb = B.Pb[pp]
    PT = B.PT[pp]
    trv = ps.tr.bitcast(BF16)

    def ptr(e):
        last = None
        for j in range(nw):
            last = e.transpose(trv[:, j * 128:(j + 1) * 128],
                               Pb[:, j * 128:(j + 1) * 128], sb.ident[:])
        return last
    pe_p = EMIT(g, "tensor", ptr, waits=[(sm.cv, cv_pn)] + W(g, "ps.tr"), inc=(sm.pe, 1))
    MARK(g, "B.Pb%d" % pp, sm.pe, pe_p)
    cs_pt = EMIT(g, "scalar", lambda e: e.copy(out=PT[:, 0:nw * 128],
                                               in_=trv[:, 0:nw * 128]),
                 waits=[(sm.pe, pe_p)] + W(g, "B.PT%d" % pp), inc=(sm.cs, 1))
    MARK(g, "ps.tr", sm.cs, cs_pt)

    def ymm(e):
        last = None
        for j in range(nw):
            last = e.matmul(ps.y[0:64, 0:128],
                            B.vring[:, slots[j] * 1024 + h * 64: slots[j] * 1024 + h * 64 + 64],
                            PT[:, j * 128:(j + 1) * 128],
                            start=(j == 0), stop=(j == nw - 1))
        return last
    pe_y = EMIT(g, "tensor", ymm, waits=[(sm.cs, cs_pt)] + vready + W(g, "ps.y"),
                inc=(sm.pe, 1))
    MARK(g, "B.PT%d" % pp, sm.pe, pe_y)
    cs_y = EMIT(g, "scalar", lambda e: e.copy(
        out=B.yfm[pr:pr + 64, blk * 512 + (ti % 4) * 128: blk * 512 + (ti % 4) * 128 + 128],
        in_=ps.y[0:64, 0:128]),
        waits=[(sm.pe, pe_y)] + (W(g, "B.yfm.r") if (ti % 4 == 0 and h == 0) else []),
        inc=(sm.cs, 1))
    MARK(g, "ps.y", sm.cs, cs_y)
    MARK(g, "B.yfm.w", sm.cs, cs_y)


def _phaseB_oproj(g, l, wb, half, rv, WCWO, r_ready):
    nc, sb, dr, sm, B, ps = g.nc, g.sb, g.dr, g.sm, g.B, g.ps
    yw = W(g, "B.yfm.w")
    tbase = 1 + half * 512
    ldo = EMIT(g, "sync", lambda e: e.dma_start(
        out=B.woin, in_=dr["wbf"][l, :, WCWO:WCWO + 8 * 1024]),
        waits=W(g, "B.t0", "B.t1", "B.ainT", "B.qhb", "B.khb", "B.qT", "B.t2")
        + [(sm.cv, sm.cv.n), (sm.cs, sm.cs.n), (sm.cg, sm.cg.n)], inc=(sm.lw, 16))
    MARK(g, "B.t0", sm.lw, ldo); MARK(g, "B.t1", sm.lw, ldo)
    MARK(g, "B.ainT", sm.lw, ldo); MARK(g, "B.qhb", sm.lw, ldo)
    MARK(g, "B.khb", sm.lw, ldo); MARK(g, "B.qT", sm.lw, ldo)
    MARK(g, "B.t2", sm.lw, ldo)
    for dc in range(8):
        def mmdc(e, dc=dc):
            last = None
            for ic in range(8):
                last = e.matmul(ps.proj[:, 0:512],
                                B.woin[:, ic * 1024 + dc * 128: ic * 1024 + dc * 128 + 128],
                                B.yfm[:, ic * 512:(ic + 1) * 512],
                                start=(ic == 0), stop=(ic == 7))
            return last
        pe_o = EMIT(g, "tensor", mmdc,
                    waits=yw + W(g, "ps.proj") + [(sm.lw, ldo)], inc=(sm.pe, 1))
        cv_o = EMIT(g, "vector", lambda e, dc=dc: e.tensor_tensor(
            out=rv[:, dc, tbase:tbase + 512], in0=ps.proj[:, 0:512],
            in1=rv[:, dc, tbase:tbase + 512], op=ALU.add),
            waits=[(sm.pe, pe_o)] + r_ready + W(g, "B.bnd"), inc=(sm.cv, 1))
        MARK(g, "ps.proj", sm.cv, cv_o)
        MARK(g, "B.rtile.w", sm.cv, cv_o, add=(not (wb == 0 and half == 0 and dc == 0)))
    for nmw in ["B.t0", "B.t1", "B.ainT", "B.qhb", "B.khb", "B.qT", "B.t2"]:
        MARK(g, nmw, sm.pe, sm.pe.n)
    MARK(g, "B.yfm.r", sm.pe, sm.pe.n)


def _phaseB_ffn_pair(g, l, pc, WCW1):
    nc, sb, dr, sm, B, ps = g.nc, g.sb, g.dr, g.sm, g.B, g.ps
    t0 = pc * 1024
    rv, r_ready = _load_rtile(g, t0, 1024, pc == 0)
    rsrc = dr["r_init"] if getattr(g, "mini", False) else dr["r_scr"]
    rb3 = B.rbf.rearrange("p (c t) -> p c t", c=8)
    ldrb = EMIT(g, "gpsimd", lambda e: e.dma_start(
        out=rb3[:, :, 1:1025],
        in_=rsrc.rearrange("p (c t) -> p c t", c=8)[:, :, t0:t0 + 1024]),
        waits=W(g, "B.rbf") + W(g, "B.rtile") + [(sm.sg, sm.sg.n)], inc=(sm.lg, 16))
    cvzb = EMIT(g, "vector", lambda e: e.tensor_copy(out=rb3[:, :, 0:1], in_=rv[:, :, 0:1]),
                waits=[(sm.lg, ldrb)] + r_ready, inc=(sm.cv, 1))
    rb_rdy = [(sm.lg, ldrb), (sm.cv, cvzb)]

    def shb(dc, h2):
        if dc < 4:
            return rb3[:, dc, 1 + h2 * 512:513 + h2 * 512]
        return rb3[:, dc, h2 * 512:512 + h2 * 512]

    def sh(dc):
        if dc < 4:
            return rv[:, dc, 1:1025]
        return rv[:, dc, 0:1024]

    sslots = [ps.proj[0:1, 0:512], ps.proj[0:1, 512:1024],
              ps.y[0:1, 0:512], ps.tr[:].bitcast(F32)[0:1, 0:512]]

    def mmsum(e):
        last = None
        for h2 in range(2):
            for dc in range(8):
                last = e.matmul(sslots[h2], sb.onesb[:, 0:1], shb(dc, h2),
                                start=(dc == 0), stop=(dc == 7))
        return last
    pe_su = EMIT(g, "tensor", mmsum, waits=rb_rdy + W(g, "ps.proj") + [(sm.lg, g.ob_rdy)],
                 inc=(sm.pe, 1))
    cs_sq = EMIT(g, "scalar", lambda e: e.activation(out=B.rbf, in_=B.rbf, func=AF.Square),
                 waits=[(sm.pe, pe_su)], inc=(sm.cs, 1))

    def mmssq(e):
        last = None
        for h2 in range(2):
            for dc in range(8):
                last = e.matmul(sslots[2 + h2], sb.onesb[:, 0:1], shb(dc, h2),
                                start=(dc == 0), stop=(dc == 7))
        return last
    pe_sq = EMIT(g, "tensor", mmssq, waits=[(sm.cs, cs_sq)] + W(g, "ps.y", "ps.tr"),
                 inc=(sm.pe, 1))
    MARK(g, "B.rbf", sm.pe, pe_sq)
    cs_st = EMIT(g, "scalar", lambda e: [
        e.copy(out=B.st2[0:1, 0:512], in_=sslots[0]),
        e.copy(out=B.st2[0:1, 512:1024], in_=sslots[1]),
        e.copy(out=B.st2[0:1, 1024:1536], in_=sslots[2]),
        e.copy(out=B.st2[0:1, 1536:2048], in_=sslots[3])][-1],
        waits=[(sm.pe, pe_sq)] + W(g, "B.st2"), inc=(sm.cs, 1))
    MARK(g, "ps.y", sm.cs, cs_st)
    MARK(g, "ps.tr", sm.cs, cs_st)
    c12 = EMIT(g, "vector", lambda e: e.tensor_scalar(
        out=B.st2[0:1, 0:2048], in0=B.st2[0:1, 0:2048], scalar1=1.0 / D, scalar2=None,
        op0=ALU.mult), waits=[(sm.cs, cs_st)], inc=(sm.cv, 1))
    pe_b = EMIT(g, "tensor", lambda e: [
        e.matmul(ps.proj[:, 0:512], B.onesrow[0:1, :], B.st2[0:1, 0:512],
                 start=True, stop=True),
        e.matmul(ps.proj[:, 512:1024], B.onesrow[0:1, :], B.st2[0:1, 512:1024],
                 start=True, stop=True)][-1],
        waits=[(sm.cv, c12), (sm.cs, cs_st)] + W(g, "ps.proj"), inc=(sm.pe, 1))
    cs_mb = EMIT(g, "scalar", lambda e: e.copy(out=B.mub, in_=ps.proj[:, 0:1024]),
                 waits=[(sm.pe, pe_b)] + W(g, "B.mub"), inc=(sm.cs, 1))
    c3 = EMIT(g, "vector", lambda e: e.tensor_tensor(
        out=B.st2[0:1, 0:1024], in0=B.st2[0:1, 0:1024], in1=B.st2[0:1, 0:1024],
        op=ALU.mult), waits=[(sm.pe, pe_b)], inc=(sm.cv, 1))
    c4 = EMIT(g, "vector", lambda e: e.tensor_tensor(
        out=B.st2[0:1, 1024:2048], in0=B.st2[0:1, 1024:2048], in1=B.st2[0:1, 0:1024],
        op=ALU.subtract), waits=[(sm.cv, c3)], inc=(sm.cv, 1))
    c5 = EMIT(g, "scalar", lambda e: e.activation(out=B.st2[0:1, 1024:2048],
                                                  in_=B.st2[0:1, 1024:2048],
                                                  func=AF.Ln, bias=sb.fc[0:1, 0:1]),
              waits=[(sm.cv, c4)], inc=(sm.cs, 1))
    c6 = EMIT(g, "scalar", lambda e: e.activation(out=B.st2[0:1, 1024:2048],
                                                  in_=B.st2[0:1, 1024:2048],
                                                  func=AF.Exp, scale=-0.5),
              waits=[(sm.cs, c5)], inc=(sm.cs, 1))
    pe_b2 = EMIT(g, "tensor", lambda e: [
        e.matmul(ps.proj[:, 0:512], B.onesrow[0:1, :], B.st2[0:1, 1024:1536],
                 start=True, stop=True),
        e.matmul(ps.proj[:, 512:1024], B.onesrow[0:1, :], B.st2[0:1, 1536:2048],
                 start=True, stop=True)][-1],
        waits=[(sm.cs, c6), (sm.cs, cs_mb)], inc=(sm.pe, 1))
    cs_rb = EMIT(g, "scalar", lambda e: e.copy(out=B.rsb, in_=ps.proj[:, 0:1024]),
                 waits=[(sm.pe, pe_b2)] + W(g, "B.rsb"), inc=(sm.cs, 1))
    MARK(g, "ps.proj", sm.cs, cs_rb)
    MARK(g, "B.st2", sm.pe, pe_b2)

    xv = B.xnf.rearrange("p (c t) -> p c t", c=8)
    cv_xn = None
    for dc in range(8):
        s1 = EMIT(g, "gpsimd", lambda e, dc=dc: e.tensor_tensor(
            out=B.xsc, in0=sh(dc), in1=B.mub, op=ALU.subtract),
            waits=[(sm.cs, cs_mb)] + r_ready + W(g, "B.xsc"), inc=(sm.cg, 1))
        s2 = EMIT(g, "gpsimd", lambda e, dc=dc: e.tensor_tensor(
            out=B.xsc, in0=B.xsc, in1=B.rsb, op=ALU.mult),
            waits=[(sm.cs, cs_rb)], inc=(sm.cg, 1))
        cv_xn = EMIT(g, "vector", lambda e, dc=dc: e.tensor_scalar(
            out=xv[:, dc, :], in0=B.xsc, scalar1=B.lvec[:, 52 + dc:53 + dc],
            scalar2=B.lvec[:, 60 + dc:61 + dc], op0=ALU.mult, op1=ALU.add),
            waits=[(sm.cg, s2), (sm.ld, g.ld_lv)] + (W(g, "B.xnf") if dc == 0 else []),
            inc=(sm.cv, 1))
        MARK(g, "B.xsc", sm.cv, cv_xn)
    xn_rdy = [(sm.cv, cv_xn)]
    MARK(g, "B.mub", sm.cg, sm.cg.n)
    MARK(g, "B.rsb", sm.cg, sm.cg.n)

    if "f0" in g.debug and l == 0 and pc == 0:
        _f0 = EMIT(g, "gpsimd", lambda e: e.dma_start(out=g.dbg["f0"][0].bitcast(BF16),
                                                      in_=B.xnf),
                   waits=[(sm.cv, cv_xn)], inc=(sm.sg, 16))
        MARK(g, "B.xnf", sm.sg, _f0, add=True)
        _f1 = EMIT(g, "sync", lambda e: e.dma_start(out=g.dbg["f0"][1, :, 0:512].bitcast(BF16),
                                                    in_=B.mub),
                   waits=[(sm.cs, cs_mb)], inc=(sm.st, 16))
        MARK(g, "B.mub", sm.st, _f1, add=True)
        _f2 = EMIT(g, "sync", lambda e: e.dma_start(out=g.dbg["f0"][1, :, 1024:2048], in_=B.rsb),
                   waits=[(sm.cs, cs_rb)], inc=(sm.st, 16))
        MARK(g, "B.rsb", sm.st, _f2, add=True)

    h_last = [None, None]
    for m in range(22):
        wdt = min(128, 2730 - m * 128)
        buf = B.w1buf[m % 2]
        ldw = EMIT(g, "sync", lambda e, m=m, buf=buf: e.dma_start(
            out=buf, in_=dr["wbf"][l, :, WCW1 + m * 2048:WCW1 + (m + 1) * 2048]),
            waits=W(g, "B.w1buf%d" % (m % 2)) + [(sm.lg, g.wbf_rdy)], inc=(sm.lw, 16))
        for h2 in range(2):
            u = m * 2 + h2
            pb = (u % 2) * 1024

            def amm(e, m=m, wdt=wdt, buf=buf, h2=h2, pb=pb):
                last = None
                for dc in range(8):
                    last = e.matmul(ps.S[0:wdt, pb:pb + 512],
                                    buf[:, dc * wdt: (dc + 1) * wdt],
                                    B.xnf[:, dc * 1024 + h2 * 512:dc * 1024 + h2 * 512 + 512],
                                    start=(dc == 0), stop=(dc == 7))
                for dc in range(8):
                    last = e.matmul(ps.S[0:wdt, pb + 512:pb + 1024],
                                    buf[:, 8 * wdt + dc * wdt: 8 * wdt + (dc + 1) * wdt],
                                    B.xnf[:, dc * 1024 + h2 * 512:dc * 1024 + h2 * 512 + 512],
                                    start=(dc == 0), stop=(dc == 7))
                return last
            pe_a = EMIT(g, "tensor", amm,
                        waits=xn_rdy + [(sm.lw, ldw)] + W(g, "ps.Sab%d" % (u % 2)),
                        inc=(sm.pe, 1))
            cs_g = EMIT(g, "scalar", lambda e, m=m, wdt=wdt, h2=h2, pb=pb: e.activation(
                out=B.gateg[h2][0:wdt, :], in_=ps.S[0:wdt, pb + 512:pb + 1024], func=AF.Gelu,
                bias=B.lvec[0:wdt, 22 + m:23 + m]),
                waits=[(sm.pe, pe_a)] + W(g, "B.gateg%d" % h2), inc=(sm.cs, 1))
            cv_h = EMIT(g, "vector", lambda e, m=m, wdt=wdt, h2=h2, pb=pb: e.scalar_tensor_tensor(
                out=B.hbuf[h2][0:wdt, m * 512:(m + 1) * 512], in0=ps.S[0:wdt, pb:pb + 512],
                scalar=B.lvec[0:wdt, m:m + 1], in1=B.gateg[h2][0:wdt, :],
                op0=ALU.add, op1=ALU.mult),
                waits=[(sm.cs, cs_g)] + (W(g, "B.hbuf%d" % h2) if m == 0 else []),
                inc=(sm.cv, 1))
            MARK(g, "B.gateg%d" % h2, sm.cv, cv_h)
            MARK(g, "ps.Sab%d" % (u % 2), sm.cv, cv_h)
            h_last[h2] = cv_h
        MARK(g, "B.w1buf%d" % (m % 2), sm.pe, pe_a)
    if "f0" in g.debug and l == 0 and pc == 0:
        _f3 = EMIT(g, "gpsimd", lambda e: e.dma_start(out=g.dbg["f0"][2].bitcast(BF16),
                                                      in_=B.hbuf[0][:, 0:8192]),
                   waits=[(sm.cv, h_last[0])], inc=(sm.sg, 16))
        MARK(g, "B.hbuf0", sm.sg, sm.sg.n, add=True)

    banks = [ps.S[:, 0:512], ps.S[:, 512:1024], ps.S[:, 1024:1536], ps.S[:, 1536:2048],
             ps.proj[:, 0:512], ps.proj[:, 512:1024], ps.y[:, 0:512],
             ps.tr[:].bitcast(F32)[:, 0:512]]
    WCW2 = WCW1 + 22 * 2048
    for h2 in range(2):
        tbase = 1 + h2 * 512
        pe_w2 = None
        for m in range(22):
            wdt = min(128, 2730 - m * 128)
            buf = B.w2buf[m % 2]
            ldw2 = EMIT(g, "sync", lambda e, m=m, buf=buf, h2=h2: e.dma_start(
                out=buf, in_=dr["wbf"][l, :, WCW2 + m * 1024:WCW2 + (m + 1) * 1024]),
                waits=W(g, "B.w2buf%d" % (m % 2)) + [(sm.lg, g.wbf_rdy)], inc=(sm.lw, 16))

            def w2mm(e, m=m, wdt=wdt, buf=buf, h2=h2):
                last = None
                for dco in range(8):
                    last = e.matmul(banks[dco][0:128, :] if wdt == 128 else banks[dco],
                                    buf[0:wdt, dco * 128:(dco + 1) * 128],
                                    B.hbuf[h2][0:wdt, m * 512:(m + 1) * 512],
                                    start=(m == 0), stop=(m == 21))
                return last
            pe_w2 = EMIT(g, "tensor", w2mm,
                         waits=[(sm.cv, h_last[0]), (sm.cv, h_last[1]), (sm.lw, ldw2)]
                         + (W(g, "ps.w2") if m == 0 else []), inc=(sm.pe, 1))
            MARK(g, "B.w2buf%d" % (m % 2), sm.pe, pe_w2)
        MARK(g, "B.hbuf%d" % h2, sm.pe, pe_w2)
        for dco in range(8):
            cv_o = EMIT(g, "vector", lambda e, dco=dco, tbase=tbase: e.scalar_tensor_tensor(
                out=rv[:, dco, tbase:tbase + 512], in0=banks[dco],
                scalar=B.lvec[:, 44 + dco:45 + dco], in1=rv[:, dco, tbase:tbase + 512],
                op0=ALU.add, op1=ALU.add),
                waits=[(sm.pe, pe_w2)] + r_ready, inc=(sm.cv, 1))
            MARK(g, "B.rtile.w", sm.cv, cv_o, add=(dco > 0 or h2 > 0))
            if h2 == 1 and dco == 5:
                MARK(g, "ps.proj", sm.cv, cv_o)
            if h2 == 1 and dco == 6:
                MARK(g, "ps.y", sm.cv, cv_o)
            if dco == 7:
                MARK(g, "ps.w2", sm.cv, cv_o)
                if h2 == 1:
                    MARK(g, "ps.tr", sm.cv, cv_o)
    st_r = EMIT(g, "sync", lambda e: e.dma_start(
        out=dr["r_scr"].rearrange("p (c t) -> p c t", c=8)[:, :, t0:t0 + 1024],
        in_=rv[:, :, 1:1025]),
        waits=W(g, "B.rtile.w"), inc=(sm.st, 16))
    MARK(g, "B.rtile", sm.st, st_r)


def _phaseC(g):
    nc, sb, dr, sm, ar = g.nc, g.sb, g.dr, g.sm, g.ar
    ps = g.ps
    C = G()
    rc = Reg(ar)
    C.xt = rc.alloc(128, 1024, F32)
    C.st = rc.alloc(128, 1024, F32)
    C.rt = rc.alloc(128, 1024, F32)
    C.gt = rc.alloc(128, 1024, F32)
    C.ot = rc.alloc(128, 1024, F32)
    bar = [(sm.pe, sm.pe.n), (sm.cv, sm.cv.n), (sm.cs, sm.cs.n),
           (sm.st, sm.st.n), (sm.sg, sm.sg.n), (sm.cg, sm.cg.n), (sm.ld, sm.ld.n),
           (sm.lg, sm.lg.n), (sm.lw, sm.lw.n), (sm.lr, sm.lr.n)]
    g.war_floor = bar
    first = True
    for tau in range(NT):
        t0 = tau * 128
        wts = bar if first else []
        ld_x = EMIT(g, "sync", lambda e, t0=t0: e.dma_start(
            out=C.xt, in_=dr["x"][t0:t0 + 128, :]),
            waits=wts + W(g, "C.xt"), inc=(sm.ld, 16))
        ld_s = EMIT(g, "sync", lambda e, t0=t0: e.dma_start(
            out=C.st, in_=dr["s_scr"][t0:t0 + 128, :]),
            waits=wts + W(g, "C.st"), inc=(sm.ld, 16))
        ld_r = EMIT(g, "sync", lambda e, tau=tau: e.dma_start(
            out=C.rt.rearrange("p (c t) -> p c t", c=8),
            in_=dr["r_scr"].rearrange("p (c t) -> p c t", c=8)[:, :, tau * 128:tau * 128 + 128]),
            waits=wts + W(g, "C.rt"), inc=(sm.ld, 16))
        first = False

        def rtr(e):
            last = None
            for c in range(8):
                last = e.transpose(ps.proj[:, c * 128:(c + 1) * 128],
                                   C.rt[:, c * 128:(c + 1) * 128], sb.identf[:])
            return last
        pe_t = EMIT(g, "tensor", rtr, waits=[(sm.ld, ld_r)] + wts + W(g, "ps.proj"),
                    inc=(sm.pe, 1))
        cs_g = EMIT(g, "scalar", lambda e: e.activation(out=C.gt, in_=ps.proj[:, :],
                                                        func=AF.Silu),
                    waits=[(sm.pe, pe_t)] + wts + W(g, "C.gt"), inc=(sm.cs, 1))
        if "c0" in g.debug and tau == 0:
            _c0 = EMIT(g, "sync", lambda e: e.dma_start(out=g.dbg["c0"][0], in_=C.rt),
                       waits=[(sm.ld, ld_r)], inc=(sm.st, 16))
            MARK(g, "C.rt", sm.st, _c0, add=True)
            _c1 = EMIT(g, "sync", lambda e: e.dma_start(out=g.dbg["c0"][1], in_=C.gt),
                       waits=[(sm.cs, cs_g)], inc=(sm.st, 16))
            MARK(g, "C.gt", sm.st, _c1, add=True)
            _c2 = EMIT(g, "sync", lambda e: e.dma_start(out=g.dbg["c0"][2], in_=C.st),
                       waits=[(sm.ld, ld_s)], inc=(sm.st, 16))
            MARK(g, "C.st", sm.st, _c2, add=True)
        MARK(g, "ps.proj", sm.cs, cs_g)
        MARK(g, "C.rt", sm.pe, pe_t)
        cv_m = EMIT(g, "vector", lambda e: e.tensor_tensor(out=C.gt, in0=C.st, in1=C.gt,
                                                           op=ALU.mult),
                    waits=[(sm.cs, cs_g), (sm.ld, ld_s)] + wts, inc=(sm.cv, 1))
        MARK(g, "C.st", sm.cv, cv_m)
        cv_o = EMIT(g, "vector", lambda e: e.tensor_tensor(out=C.ot, in0=C.xt, in1=C.gt,
                                                           op=ALU.add),
                    waits=[(sm.cv, cv_m), (sm.ld, ld_x)] + W(g, "C.ot"), inc=(sm.cv, 1))
        MARK(g, "C.xt", sm.cv, cv_o)
        MARK(g, "C.gt", sm.cv, cv_o)
        st_o = EMIT(g, "sync", lambda e, t0=t0: e.dma_start(
            out=dr["out"][t0:t0 + 128, :], in_=C.ot),
            waits=[(sm.cv, cv_o)], inc=(sm.st, 16))
        MARK(g, "C.ot", sm.st, st_o)



def _prep_in_maps(inputs):
    c = host_constants()
    base = {}
    base["wq"] = _fm(np.asarray(inputs["L0_attn_wq"], dtype=np.float32))
    base["wkv"] = _fm(np.asarray(inputs["L0_attn_wkv"], dtype=np.float32))
    base["wo"] = _fm(np.asarray(inputs["L0_attn_wo"], dtype=np.float32))
    base["bq"] = _rep(inputs["L0_attn_bq"])
    base["bkv"] = _rep(inputs["L0_attn_bkv"])
    base["bo"] = _rep(inputs["L0_attn_bo"])
    base["qn"] = _rep(np.tile(np.asarray(inputs["L0_attn_qn_w"], dtype=np.float32), H))
    base["kn"] = _rep(np.tile(np.asarray(inputs["L0_attn_kn_w"], dtype=np.float32), NKV))
    Wqkv = np.asarray(inputs["L0_ffn_Wqkv"], dtype=np.float32)
    Woin = np.asarray(inputs["L0_ffn_Wo"], dtype=np.float32)
    W1 = np.asarray(inputs["L0_ffn_W1"], dtype=np.float32)
    W2 = np.asarray(inputs["L0_ffn_W2"], dtype=np.float32)
    base["wqkv"] = np.stack([_fm(Wqkv[l]) for l in range(DEPTH)])
    base["woin"] = np.stack([_fm(Woin[l]) for l in range(DEPTH)])
    w1l = np.zeros((DEPTH, 128, 22 * 2048), dtype=np.float32)
    for l in range(DEPTH):
        for m in range(22):
            wdt = min(128, 2730 - m * 128)
            a = W1[l][:, m * 128:m * 128 + wdt]
            gt = W1[l][:, 2730 + m * 128:2730 + m * 128 + wdt]
            w1l[l][:, m * 2048:m * 2048 + 8 * wdt] = \
                a.reshape(8, 128, wdt).transpose(1, 0, 2).reshape(128, 8 * wdt)
            w1l[l][:, m * 2048 + 8 * wdt:m * 2048 + 16 * wdt] = \
                gt.reshape(8, 128, wdt).transpose(1, 0, 2).reshape(128, 8 * wdt)
    base["w1"] = w1l
    w2l = np.zeros((DEPTH, 128, 22 * 1024), dtype=np.float32)
    for l in range(DEPTH):
        for m in range(22):
            wdt = min(128, 2730 - m * 128)
            w2l[l][:wdt, m * 1024:(m + 1) * 1024] = W2[l][m * 128:m * 128 + wdt, :]
    base["w2"] = w2l
    b1 = np.asarray(inputs["L0_ffn_b1"], dtype=np.float32)
    b1a = np.zeros((DEPTH, 128, 22), dtype=np.float32)
    b1g = np.zeros((DEPTH, 128, 22), dtype=np.float32)
    for l in range(DEPTH):
        for m in range(22):
            wdt = min(128, 2730 - m * 128)
            b1a[l, :wdt, m] = b1[l, m * 128:m * 128 + wdt]
            b1g[l, :wdt, m] = b1[l, 2730 + m * 128:2730 + m * 128 + wdt]
    base["b1a"] = b1a
    base["b1g"] = b1g
    base["b2"] = np.stack([np.asarray(inputs["L0_ffn_b2"], dtype=np.float32)[l].reshape(8, 128).T
                           for l in range(DEPTH)])
    base["lng"] = np.stack([np.asarray(inputs["L0_ffn_ln_g"], dtype=np.float32)[l].reshape(8, 128).T
                            for l in range(DEPTH)])
    base["lnb"] = np.stack([np.asarray(inputs["L0_ffn_ln_b"], dtype=np.float32)[l].reshape(8, 128).T
                            for l in range(DEPTH)])
    base["qsc"] = np.stack([_rep(np.tile(np.asarray(inputs["L0_ffn_q_scale"], dtype=np.float32)[l], H))
                            for l in range(DEPTH)])
    base["ksc"] = np.stack([_rep(np.tile(np.asarray(inputs["L0_ffn_k_scale"], dtype=np.float32)[l], H))
                            for l in range(DEPTH)])
    fc = np.zeros((128, 8), dtype=np.float32)
    fc[:, 0] = 1e-5; fc[:, 1] = 1e-8; fc[:, 2] = -LN8; fc[:, 3] = 1.0
    base["fconsts"] = fc
    p_ = np.arange(128, dtype=np.float32)
    bt = np.zeros((128, 8), dtype=np.float32)
    bt[:, 0] = 0.0;       bt[:, 3] = p_ + 1
    bt[:, 1] = 0.0;       bt[:, 4] = p_ + 129
    bt[:, 2] = p_;        bt[:, 5] = p_ + 257
    base["bndtab"] = bt
    base["onesrow"] = np.ones((1, 128), dtype=np.float32)
    bm = np.zeros((128, 3 * 384), dtype=np.float32)
    cc = np.arange(384)[None, :]
    pp = p_[:, None]
    bm[:, 0:128] = np.where(cc[:, :128] <= pp, 0.0, NEG)
    bm[:, 384:384 + 256] = np.where(cc[:, :256] <= 128 + pp, 0.0, NEG)
    bm[:, 768:768 + 384] = np.where((cc >= pp) & (cc <= pp + 256), 0.0, NEG)
    base["bmask"] = bm
    for k in ["ident", "diagmask", "qaug", "kaug"]:
        base[k] = c[k]
    for k in ["rcsq", "rssq", "rcsk", "rssk"]:
        base[k] = _tile_tok(c[k])
    x = np.asarray(inputs["L0_attn_x"], dtype=np.float32)
    return [dict(base, x=np.ascontiguousarray(x[b])) for b in range(8)]


def kernel(**inputs):
    in_maps = _prep_in_maps(inputs)
    nc = build_nc()
    res = run_bass_kernel_spmd(nc, in_maps, core_ids=list(range(8)))
    return np.stack([np.asarray(res.results[b]["out"]) for b in range(8)]).astype(np.float32)


# version 09 (final)
# speedup vs baseline: 1.0087x; 1.0087x over previous
import math
import numpy as np
import concourse.bass as bass
import concourse.mybir as mybir
from concourse.bass_utils import run_bass_kernel_spmd
from contextlib import ExitStack

F32 = mybir.dt.float32
BF16 = mybir.dt.bfloat16
AF = mybir.ActivationFunctionType
ALU = mybir.AluOpType
AX = mybir.AxisListType

T = 4096
D = 1024
H = 16
HD = 64
NKV = 4
NT = T // 128
NEG = -3.0e38
DEPTH = 6
WINDOW = 256
LN8 = math.log(8.0)


def _split3(x):
    import ml_dtypes
    bf = ml_dtypes.bfloat16
    x = np.asarray(x, dtype=np.float32)
    hi = x.astype(bf).astype(np.float32)
    mid = (x - hi).astype(bf).astype(np.float32)
    lo = (x - hi - mid).astype(bf).astype(np.float32)
    return hi, mid, lo


def host_constants():
    c = {}
    c["ident"] = np.eye(128, dtype=np.float32)
    c["diagmask"] = np.triu(np.full((128, 128), NEG, dtype=np.float32), 1)
    slopes = 2.0 ** (-(np.arange(H, dtype=np.float64) / (H - 1) * 8.0))
    s_hi, s_mid, s_lo = _split3(slopes)
    qaug = np.zeros((6, H * 128), dtype=np.float32)
    for h in range(H):
        for a, v in enumerate([s_hi[h], s_hi[h], s_mid[h], s_mid[h], s_lo[h], s_lo[h]]):
            qaug[a, h * 128:(h + 1) * 128] = v
    c["qaug"] = qaug
    j = np.arange(T, dtype=np.float64)
    j_hi, j_lo, _ = _split3(j)
    c["kaug"] = np.stack([j_hi, j_lo, j_hi, j_lo, j_hi, j_lo]).astype(np.float32)
    half = np.arange(0, HD, 2, dtype=np.float64)
    inv_freq = 1.0 / 10000.0 ** (half / HD)
    t = np.arange(T, dtype=np.float64)
    freqs = np.concatenate([t[:, None] * inv_freq[None, :]] * 2, axis=-1)
    base_scale = (half + 0.4 * HD) / (1.4 * HD)
    power = (t - T // 2) / (WINDOW // 2)
    scale = np.concatenate([base_scale[None, :] ** power[:, None]] * 2, axis=-1)
    cos, sin = np.cos(freqs), np.sin(freqs)
    sgn = np.concatenate([-np.ones(32), np.ones(32)])[None, :]
    c["rcsq"] = (cos * scale * 8.0).astype(np.float32)
    c["rssq"] = (sin * scale * sgn * 8.0).astype(np.float32)
    c["rcsk"] = (cos / scale).astype(np.float32)
    c["rssk"] = (sin / scale * sgn).astype(np.float32)
    return c


def _tile_tok(a):
    Tt, F = a.shape
    return np.ascontiguousarray(a.reshape(Tt // 128, 128, F).transpose(1, 0, 2).reshape(128, -1))


def _fm(a):
    Din, F = a.shape
    return np.ascontiguousarray(a.reshape(Din // 128, 128, F).transpose(1, 0, 2).reshape(128, -1))


def _rep(v):
    v = np.asarray(v, dtype=np.float32)
    return np.ascontiguousarray(np.broadcast_to(v, (128, v.shape[0])))



class Sem:
    def __init__(self, es, nc, name):
        self.h = es.enter_context(nc.semaphore(name))
        self.n = 0


class G:
    pass


def EMIT(g, engine, fn, waits=(), inc=None):
    waits = tuple((s, v) for (s, v) in waits)
    val = None
    if inc is not None:
        sem, k = inc
        sem.n += k
        val = sem.n

        def run(e, fn=fn, waits=waits, sem=sem, k=k):
            for s, v in waits:
                e.wait_ge(s.h, v)
            fn(e).then_inc(sem.h, k)
    else:
        def run(e, fn=fn, waits=waits):
            for s, v in waits:
                e.wait_ge(s.h, v)
            fn(e)
    g.prog[engine].append(run)
    return val


class Arena:
    def __init__(self, nc, es, nbytes):
        self.t = es.enter_context(nc.sbuf_tensor("arena", [128, nbytes // 4], F32))
        self.nbytes = nbytes

    def view(self, off, p, cols, dtype):
        esz = 4 if dtype == F32 else 2
        assert off % 4 == 0 and (off + cols * esz) <= self.nbytes, (off, cols, dtype)
        base = self.t[0:p, off // 4: off // 4 + (cols * esz + 3) // 4]
        if dtype == F32:
            return base[:, :cols]
        v = base.bitcast(BF16)
        return v[:, :cols]


class Reg:
    def __init__(self, arena, start=0):
        self.a = arena
        self.off = start

    def alloc(self, p, cols, dtype, align=512):
        esz = 4 if dtype == F32 else 2
        self.off = (self.off + align - 1) // align * align
        v = self.a.view(self.off, p, cols, dtype)
        self.off += cols * esz
        return v



def build_nc(phases=("A", "B", "C"), debug=(), mini_n=None):
    nc = bass.Bass("TRN2", target_bir_lowering=False)
    g = G()
    g.nc = nc
    g.mini_n = mini_n
    g.debug = set(debug)
    g.prog = {e: [] for e in ["sync", "gpsimd", "tensor", "vector", "scalar"]}
    dr = {}
    g.dr = dr

    def din(name, shape):
        dr[name] = nc.dram_tensor(name, shape, F32, kind="ExternalInput").ap()

    din("x", [T, D])
    din("wq", [128, 8 * 1024]); din("wkv", [128, 8 * 512]); din("wo", [128, 8 * 1024])
    din("bq", [128, 1024]); din("bkv", [128, 512]); din("bo", [128, 1024])
    din("qn", [128, 1024]); din("kn", [128, 256])
    din("wqkv", [DEPTH, 128, 8 * 3072]); din("woin", [DEPTH, 128, 8 * 1024])
    din("w1", [DEPTH, 128, 22 * 2048]); din("w2", [DEPTH, 128, 22 * 1024])
    din("b1a", [DEPTH, 128, 22]); din("b1g", [DEPTH, 128, 22])
    din("b2", [DEPTH, 128, 8]); din("lng", [DEPTH, 128, 8]); din("lnb", [DEPTH, 128, 8])
    din("qsc", [DEPTH, 128, 1024]); din("ksc", [DEPTH, 128, 1024])
    din("ident", [128, 128]); din("diagmask", [128, 128]); din("fconsts", [128, 8])
    din("bndtab", [128, 8]); din("onesrow", [1, 128]); din("bmask", [128, 3 * 384])
    din("qaug", [6, H * 128]); din("kaug", [6, T])
    din("rcsq", [128, NT * 64]); din("rssq", [128, NT * 64])
    din("rcsk", [128, NT * 64]); din("rssk", [128, NT * 64])
    dr["s_scr"] = nc.dram_tensor("s_scr", [T, D], F32, kind="Internal").ap()
    if "B0" in phases:
        dr["r_init"] = nc.dram_tensor("r_init", [128, 8 * T], F32, kind="ExternalInput").ap()
    dr["r_scr"] = nc.dram_tensor("r_scr", [128, 8 * T], F32, kind="Internal").ap()
    dr["wbf"] = nc.dram_tensor("wbf", [DEPTH, 128, 8 * 3072 + 8 * 1024 + 22 * 2048 + 22 * 1024],
                               BF16, kind="Internal").ap()
    dr["out"] = nc.dram_tensor("out", [T, D], F32, kind="ExternalOutput").ap()
    g.dbg = {}
    for name in g.debug:
        shp = {"xn": [T, D], "q": [T, D], "kv": [T, 512], "thr": [T, H], "z": [T, H],
               "r0": [128, 8 * T], "rl": [DEPTH + 1, 128, 8 * T],
               "ra": [128, 8 * T], "b0": [6, 128, 1024], "c0": [4, 128, 1024],
               "f0": [4, 128, 4096]}[name]
        g.dbg[name] = nc.dram_tensor("dbg_" + name, shp, F32, kind="ExternalOutput").ap()

    with ExitStack() as es:
        _build(g, es, phases)
    return nc


def _build(g, es, phases):
    nc, dr = g.nc, g.dr
    sm = G()
    g.sm = sm
    for name in ["ld", "lg", "pe", "cv", "cs", "cg", "st", "sg", "lw", "lr"]:
        setattr(sm, name, Sem(es, nc, name))

    sb = G()
    g.sb = sb
    sb.ident = es.enter_context(nc.sbuf_tensor("sb_ident", [128, 128], BF16))
    sb.identf = es.enter_context(nc.sbuf_tensor("sb_identf", [128, 128], F32))
    sb.diagmask = es.enter_context(nc.sbuf_tensor("sb_diagmask", [128, 128], F32))
    sb.m8 = es.enter_context(nc.sbuf_tensor("m8", [128, 8], F32))
    sb.fc = es.enter_context(nc.sbuf_tensor("fc", [128, 8], F32))
    sb.thr = es.enter_context(nc.sbuf_tensor("thr", [128, 4], F32))
    sb.zrec = es.enter_context(nc.sbuf_tensor("zrec", [128, 4], F32))
    sb.stats = es.enter_context(nc.sbuf_tensor("stats", [128, 8], F32))
    sb.rss = es.enter_context(nc.sbuf_tensor("rss", [128, 24], F32))
    sb.onesb = es.enter_context(nc.sbuf_tensor("onesb", [128, 1], BF16))

    ar = Arena(nc, es, 206 * 1024)
    g.ar = ar

    A = G()
    g.A = A
    ra = Reg(ar)
    A.wq_hi = ra.alloc(128, 8 * 1024, BF16)
    A.wq_lo = ra.alloc(128, 8 * 1024, BF16)
    A.wkv_hi = ra.alloc(128, 8 * 512, BF16)
    A.wkv_lo = ra.alloc(128, 8 * 512, BF16)
    A.wo = ra.alloc(128, 8 * 1024, BF16)
    A.bq = ra.alloc(128, 1024, F32)
    A.bkv = ra.alloc(128, 512, F32)
    A.bo = ra.alloc(128, 1024, F32)
    A.qn = ra.alloc(128, 1024, F32)
    A.kn = ra.alloc(128, 256, F32)
    A.khl = ra.alloc(128, NKV * T, BF16)
    A.kaug = ra.alloc(6, T, BF16)
    A.vbuf = ra.alloc(128, NT * 256, BF16)
    A.lhsT1 = ra.alloc(128, H * 128, BF16)
    A.lhsT1x = ra.alloc(128, H * 128, BF16)
    A.qaug = ra.alloc(6, H * 128, BF16)
    A.S = ra.alloc(128, T, F32)
    A.P = [ra.alloc(128, T, BF16), None]
    A.coll = ra.alloc(128, 512, F32)
    A.PT = ra.alloc(128, 1024, BF16)
    A.yfm = ra.alloc(128, 8 * 128, BF16)
    _xt_off = ra.off
    A.xt = ra.alloc(128, 1024, F32)
    A.xn = ra.alloc(128, 1024, F32)
    A.P[1] = ar.view(_xt_off, 128, T, BF16)
    A.tmpf = ra.alloc(128, 1024, F32)
    A.stile = ra.alloc(128, 1024, F32)
    A.k2s = ra.alloc(128, 256, F32)
    A.hi_tm = ra.alloc(128, 1024, BF16)
    A.lo_tm = ra.alloc(128, 1024, BF16)
    A.hi_fm = ra.alloc(128, 1024, BF16)
    A.lo_fm = ra.alloc(128, 1024, BF16)
    A.kvsb = ra.alloc(128, 512, F32)
    A.khat = ra.alloc(128, 256, F32)
    A.k_hi = ra.alloc(128, 256, BF16)
    A.k_lo = ra.alloc(128, 256, BF16)
    g.A_end = ra.off
    assert ra.off <= ar.nbytes, ra.off

    ps = G()
    g.ps = ps
    ps.S = es.enter_context(nc.psum_tensor("psS", [128, 2048], F32))
    ps.tr = es.enter_context(nc.psum_tensor("pstr", [128, 1024], BF16))
    ps.proj = es.enter_context(nc.psum_tensor("psproj", [128, 1024], F32))
    ps.y = es.enter_context(nc.psum_tensor("psy", [128, 512], F32))

    g.war = {}

    blk = es.enter_context(nc.Block())

    if "B0" in phases:
        for dst, srcn in [(sb.ident[:], "ident"), (sb.identf[:], "ident"),
                          (sb.diagmask[:], "diagmask"), (sb.fc[:], "fconsts")]:
            EMIT(g, "gpsimd" if dst.dtype == BF16 else "sync",
                 lambda e, dst=dst, srcn=srcn: e.dma_start(out=dst, in_=dr[srcn][:]),
                 inc=(sm.lg, 16) if dst.dtype == BF16 else (sm.ld, 16))
        _emit_wbf_precast(g, layers=[0])
        g.proA = {"lg": sm.lg.n, "ld": sm.ld.n, "cv": sm.cv.n}
        _phaseB(g, mini=True)
    else:
        _prologue(g)
        if "A" in phases:
            for tau in range(NT):
                _phaseA_tau(g, tau)
        if "B" in phases:
            _phaseB(g)
        if "C" in phases:
            _phaseC(g)
    EMIT(g, "sync", lambda e: e.nop(), waits=[(sm.st, sm.st.n)])
    EMIT(g, "gpsimd", lambda e: e.nop(), waits=[(sm.sg, sm.sg.n)])

    @blk.sync
    def _(e):
        for f in g.prog["sync"]:
            f(e)

    @blk.gpsimd
    def _(e):
        for f in g.prog["gpsimd"]:
            f(e)

    @blk.tensor
    def _(e):
        for f in g.prog["tensor"]:
            f(e)

    @blk.vector
    def _(e):
        for f in g.prog["vector"]:
            f(e)

    @blk.scalar
    def _(e):
        for f in g.prog["scalar"]:
            f(e)



def _prologue(g):
    nc, sb, A, dr, sm = g.nc, g.sb, g.A, g.dr, g.sm

    for dst, src in [(sb.identf[:], dr["ident"]), (sb.diagmask[:], dr["diagmask"]),
                     (A.bq, dr["bq"]), (A.bkv, dr["bkv"]), (A.bo, dr["bo"]),
                     (A.qn, dr["qn"]), (A.kn, dr["kn"]), (sb.fc[:], dr["fconsts"])]:
        EMIT(g, "sync", lambda e, dst=dst, src=src: e.dma_start(out=dst, in_=src[:]),
             inc=(sm.ld, 16))
    for dst, src in [(sb.ident[:], dr["ident"]), (A.wq_hi, dr["wq"]),
                     (A.wkv_hi, dr["wkv"]), (A.wo, dr["wo"]),
                     (A.kaug, dr["kaug"]), (A.qaug, dr["qaug"])]:
        EMIT(g, "gpsimd", lambda e, dst=dst, src=src: e.dma_start(out=dst, in_=src[:]),
             inc=(sm.lg, 16))
    lg_hi = sm.lg.n

    for (lo, hi, src, ncols) in [(A.wq_lo, A.wq_hi, dr["wq"], 8192),
                                 (A.wkv_lo, A.wkv_hi, dr["wkv"], 4096)]:
        for c0 in range(0, ncols, 4096):
            w = min(4096, ncols - c0)
            ldv = EMIT(g, "sync", lambda e, c0=c0, w=w, src=src:
                       e.dma_start(out=A.S[:, :w], in_=src[:, c0:c0 + w]),
                       waits=W(g, "A.S"), inc=(sm.ld, 16))
            cvv = EMIT(g, "vector", lambda e, lo=lo, hi=hi, c0=c0, w=w:
                       e.tensor_tensor(out=lo[:, c0:c0 + w], in0=A.S[:, :w],
                                       in1=hi[:, c0:c0 + w], op=ALU.subtract),
                       waits=[(sm.ld, ldv), (sm.lg, lg_hi)], inc=(sm.cv, 1))
            MARK(g, "A.S", sm.cv, cvv)


    _emit_wbf_precast(g)
    g.proA = {"lg": sm.lg.n, "ld": sm.ld.n, "cv": sm.cv.n}


def _emit_wbf_precast(g, layers=range(DEPTH)):
    nc, sb, dr, sm = g.nc, g.sb, g.dr, g.sm
    WCWO = 8 * 3072
    WCW1 = WCWO + 8 * 1024
    WCW2 = WCW1 + 22 * 2048
    EMIT(g, "gpsimd", lambda e: e.dma_start(out=sb.onesb[:], in_=dr["fconsts"][:, 3:4],
                                            allow_slow_non_contiguous=True),
         inc=(sm.lg, 16))
    g.ob_rdy = sm.lg.n
    for l in layers:
        for (srcn, off, wd) in [("wqkv", 0, 8 * 3072), ("woin", WCWO, 8 * 1024),
                                ("w1", WCW1, 22 * 2048), ("w2", WCW2, 22 * 1024)]:
            EMIT(g, "gpsimd", lambda e, l=l, srcn=srcn, off=off, wd=wd: e.dma_start(
                out=dr["wbf"][l, :, off:off + wd], in_=dr[srcn][l]),
                inc=(sm.lg, 16))
    g.wbf_rdy = sm.lg.n


def W(g, *names):
    out = []
    fl = getattr(g, "war_floor", [])
    for n in names:
        out.extend(g.war.get(n, fl))
    return out


def MARK(g, name, sem, val, add=False):
    if add and name in g.war:
        g.war[name] = g.war[name] + [(sem, val)]
    else:
        g.war[name] = [(sem, val)]



def _phaseA_tau(g, tau):
    nc, sb, A, ps, dr, sm = g.nc, g.sb, g.A, g.ps, g.dr, g.sm
    L = (tau + 1) * 128
    r0 = tau * 128

    ld_x = EMIT(g, "sync", lambda e: e.dma_start(out=A.xt, in_=dr["x"][r0:r0 + 128, :]),
                waits=W(g, "A.xt") + [(sm.cv, g.proA["cv"])], inc=(sm.ld, 16))

    cv1 = EMIT(g, "vector", lambda e: e.reduce_sum(out=sb.stats[:, 0:1], in_=A.xt, axis=AX.X),
               waits=[(sm.ld, ld_x)] + W(g, "sb.stats"), inc=(sm.cv, 1))
    cs1 = EMIT(g, "scalar", lambda e: e.activation(out=A.stile, in_=A.xt, func=AF.Square,
                                                   accum_out=sb.stats[:, 1:2]),
               waits=[(sm.ld, ld_x)] + W(g, "A.stile", "sb.stats"), inc=(sm.cs, 1))
    cv2 = EMIT(g, "vector", lambda e: e.tensor_scalar(out=sb.stats[:, 2:3], in0=sb.stats[:, 0:1],
                                                      scalar1=1.0 / D, scalar2=None, op0=ALU.mult),
               waits=[(sm.cv, cv1)], inc=(sm.cv, 1))
    cv3 = EMIT(g, "vector", lambda e: e.tensor_scalar(out=sb.stats[:, 3:4], in0=sb.stats[:, 1:2],
                                                      scalar1=1.0 / D, scalar2=None, op0=ALU.mult),
               waits=[(sm.cs, cs1)], inc=(sm.cv, 1))
    cv4 = EMIT(g, "vector", lambda e: e.tensor_tensor(out=sb.stats[:, 4:5], in0=sb.stats[:, 2:3],
                                                      in1=sb.stats[:, 2:3], op=ALU.mult),
               waits=[(sm.cv, cv2)], inc=(sm.cv, 1))
    cv5 = EMIT(g, "vector", lambda e: e.tensor_tensor(out=sb.stats[:, 4:5], in0=sb.stats[:, 3:4],
                                                      in1=sb.stats[:, 4:5], op=ALU.subtract),
               waits=[(sm.cv, cv4), (sm.cv, cv3)], inc=(sm.cv, 1))
    cs2 = EMIT(g, "scalar", lambda e: e.activation(out=sb.stats[:, 5:6], in_=sb.stats[:, 4:5],
                                                   func=AF.Ln, bias=sb.fc[:, 0:1]),
               waits=[(sm.cv, cv5)], inc=(sm.cs, 1))
    cs3 = EMIT(g, "scalar", lambda e: e.activation(out=sb.stats[:, 6:7], in_=sb.stats[:, 5:6],
                                                   func=AF.Exp, scale=-0.5),
               waits=[(sm.cs, cs2)], inc=(sm.cs, 1))
    cv_xn = EMIT(g, "vector", lambda e: e.tensor_scalar(out=A.xn, in0=A.xt,
                                                        scalar1=sb.stats[:, 2:3],
                                                        scalar2=sb.stats[:, 6:7],
                                                        op0=ALU.subtract, op1=ALU.mult),
                 waits=[(sm.cs, cs3)] + W(g, "A.xn"), inc=(sm.cv, 1))
    MARK(g, "sb.stats", sm.cv, cv_xn)
    cv_hi = EMIT(g, "vector", lambda e: e.tensor_copy(out=A.hi_tm, in_=A.xn),
                 waits=[(sm.cv, cv_xn)] + W(g, "A.hi_tm"), inc=(sm.cv, 1))
    cv_t = EMIT(g, "vector", lambda e: e.tensor_tensor(out=A.tmpf, in0=A.xn, in1=A.hi_tm,
                                                       op=ALU.subtract),
                waits=[(sm.cv, cv_hi)] + W(g, "A.tmpf"), inc=(sm.cv, 1))
    cv_lo = EMIT(g, "vector", lambda e: e.tensor_copy(out=A.lo_tm, in_=A.tmpf),
                 waits=[(sm.cv, cv_t)] + W(g, "A.lo_tm"), inc=(sm.cv, 1))

    pe1 = EMIT(g, "tensor", lambda e: [e.transpose(ps.tr[:, c * 128:(c + 1) * 128],
                                                   A.hi_tm[:, c * 128:(c + 1) * 128],
                                                   sb.ident[:]) for c in range(8)][-1],
               waits=[(sm.cv, cv_hi), (sm.lg, g.proA["lg"])] + W(g, "ps.tr"), inc=(sm.pe, 1))
    cv_eh = EMIT(g, "vector", lambda e: e.tensor_copy(out=A.hi_fm, in_=ps.tr[:, :]),
                 waits=[(sm.pe, pe1)] + W(g, "A.hi_fm"), inc=(sm.cv, 1))
    pe2 = EMIT(g, "tensor", lambda e: [e.transpose(ps.tr[:, c * 128:(c + 1) * 128],
                                                   A.lo_tm[:, c * 128:(c + 1) * 128],
                                                   sb.ident[:]) for c in range(8)][-1],
               waits=[(sm.cv, cv_lo), (sm.cv, cv_eh)], inc=(sm.pe, 1))
    cv_el = EMIT(g, "vector", lambda e: e.tensor_copy(out=A.lo_fm, in_=ps.tr[:, :]),
                 waits=[(sm.pe, pe2)] + W(g, "A.lo_fm"), inc=(sm.cv, 1))
    MARK(g, "ps.tr", sm.cv, cv_el)
    MARK(g, "A.hi_tm", sm.pe, pe1)
    MARK(g, "A.lo_tm", sm.pe, pe2)
    cv_r = EMIT(g, "vector", lambda e: e.tensor_tensor(out=A.tmpf, in0=A.hi_fm, in1=A.lo_fm,
                                                       op=ALU.add),
                waits=[(sm.cv, cv_el), (sm.cv, cv_lo)], inc=(sm.cv, 1))
    st_r = EMIT(g, "sync", lambda e, tau=tau: e.dma_start(
        out=dr["r_scr"].rearrange("p (c t) -> p c t", c=8)[:, :, tau * 128:tau * 128 + 128],
        in_=A.tmpf.rearrange("p (c t) -> p c t", c=8)),
        waits=[(sm.cv, cv_r)], inc=(sm.st, 16))
    MARK(g, "A.tmpf", sm.st, st_r)

    def qproj(e):
        last = None
        for oc in range(2):
            for c in range(8):
                sl = ps.proj[:, oc * 512:(oc + 1) * 512]
                co = c * 1024 + oc * 512
                last = e.matmul(sl, A.hi_fm[:, c * 128:(c + 1) * 128],
                                A.wq_hi[:, co:co + 512], start=(c == 0), stop=False)
                last = e.matmul(sl, A.hi_fm[:, c * 128:(c + 1) * 128],
                                A.wq_lo[:, co:co + 512], start=False, stop=False)
                last = e.matmul(sl, A.lo_fm[:, c * 128:(c + 1) * 128],
                                A.wq_hi[:, co:co + 512], start=False, stop=(c == 7))
        return last
    pe_q = EMIT(g, "tensor", qproj,
                waits=[(sm.cv, cv_el), (sm.cv, g.proA["cv"])] + W(g, "ps.proj"),
                inc=(sm.pe, 1))
    cv_q = EMIT(g, "vector", lambda e: e.tensor_tensor(out=A.xt, in0=ps.proj[:, :], in1=A.bq,
                                                       op=ALU.add),
                waits=[(sm.pe, pe_q)], inc=(sm.cv, 1))

    def kvproj(e):
        last = None
        for c in range(8):
            sl = ps.proj[:, 0:512]
            last = e.matmul(sl, A.hi_fm[:, c * 128:(c + 1) * 128],
                            A.wkv_hi[:, c * 512:(c + 1) * 512], start=(c == 0), stop=False)
            last = e.matmul(sl, A.hi_fm[:, c * 128:(c + 1) * 128],
                            A.wkv_lo[:, c * 512:(c + 1) * 512], start=False, stop=False)
            last = e.matmul(sl, A.lo_fm[:, c * 128:(c + 1) * 128],
                            A.wkv_hi[:, c * 512:(c + 1) * 512], start=False, stop=(c == 7))
        return last
    pe_kv = EMIT(g, "tensor", kvproj, waits=[(sm.cv, cv_q)], inc=(sm.pe, 1))
    cv_kv = EMIT(g, "vector", lambda e: e.tensor_tensor(out=A.kvsb, in0=ps.proj[:, 0:512],
                                                        in1=A.bkv, op=ALU.add),
                 waits=[(sm.pe, pe_kv)] + W(g, "A.kvsb"), inc=(sm.cv, 1))
    MARK(g, "ps.proj", sm.cv, cv_kv)
    MARK(g, "A.hi_fm", sm.pe, pe_kv)
    MARK(g, "A.lo_fm", sm.pe, pe_kv)
    MARK(g, "A.xn", sm.cv, cv_lo)

    cs_q2 = EMIT(g, "scalar", lambda e: e.activation(out=A.stile, in_=A.xt, func=AF.Square),
                 waits=[(sm.cv, cv_q), (sm.cs, cs1)] + W(g, "A.stile"), inc=(sm.cs, 1))
    cs_k2 = EMIT(g, "scalar", lambda e: e.activation(out=A.k2s, in_=A.kvsb[:, 0:256],
                                                     func=AF.Square),
                 waits=[(sm.cv, cv_kv)] + W(g, "A.k2s"), inc=(sm.cs, 1))
    cv_ssq = EMIT(g, "vector", lambda e: e.tensor_reduce(
        out=sb.rss[:, 0:16], in_=A.stile.rearrange("p (h d) -> p h d", h=16),
        axis=AX.X, op=ALU.add), waits=[(sm.cs, cs_q2)] + W(g, "sb.rss"), inc=(sm.cv, 1))
    cv_ssk = EMIT(g, "vector", lambda e: e.tensor_reduce(
        out=sb.rss[:, 16:20], in_=A.k2s.rearrange("p (h d) -> p h d", h=4),
        axis=AX.X, op=ALU.add), waits=[(sm.cs, cs_k2)], inc=(sm.cv, 1))
    MARK(g, "A.stile", sm.cv, cv_ssq)
    MARK(g, "A.k2s", sm.cv, cv_ssk)
    cs_l1 = EMIT(g, "scalar", lambda e: e.activation(out=sb.rss[:, 0:16], in_=sb.rss[:, 0:16],
                                                     func=AF.Ln, scale=1.0 / 64, bias=sb.fc[:, 1:2]),
                 waits=[(sm.cv, cv_ssq)], inc=(sm.cs, 1))
    cs_e1 = EMIT(g, "scalar", lambda e: e.activation(out=sb.rss[:, 0:16], in_=sb.rss[:, 0:16],
                                                     func=AF.Exp, scale=-0.5, bias=sb.fc[:, 2:3]),
                 waits=[(sm.cs, cs_l1)], inc=(sm.cs, 1))
    cs_l2 = EMIT(g, "scalar", lambda e: e.activation(out=sb.rss[:, 16:20], in_=sb.rss[:, 16:20],
                                                     func=AF.Ln, scale=1.0 / 64, bias=sb.fc[:, 1:2]),
                 waits=[(sm.cv, cv_ssk), (sm.cs, cs_e1)], inc=(sm.cs, 1))
    cs_e2 = EMIT(g, "scalar", lambda e: e.activation(out=sb.rss[:, 16:20], in_=sb.rss[:, 16:20],
                                                     func=AF.Exp, scale=-0.5),
                 waits=[(sm.cs, cs_l2)], inc=(sm.cs, 1))
    cv_q1 = EMIT(g, "vector", lambda e: e.tensor_tensor(
        out=A.xn.rearrange("p (h d) -> p h d", h=16),
        in0=A.xt.rearrange("p (h d) -> p h d", h=16),
        in1=sb.rss[:, 0:16].unsqueeze(2).to_broadcast([128, 16, 64]), op=ALU.mult),
        waits=[(sm.cs, cs_e1), (sm.cv, cv_lo)], inc=(sm.cv, 1))
    cv_q2 = EMIT(g, "vector", lambda e: e.tensor_tensor(out=A.xn, in0=A.xn, in1=A.qn,
                                                        op=ALU.mult),
                 waits=[(sm.cv, cv_q1)], inc=(sm.cv, 1))
    MARK(g, "A.xt", sm.cv, cv_q1)
    cv_k1 = EMIT(g, "vector", lambda e: e.tensor_tensor(
        out=A.khat.rearrange("p (h d) -> p h d", h=4),
        in0=A.kvsb[:, 0:256].rearrange("p (h d) -> p h d", h=4),
        in1=sb.rss[:, 16:20].unsqueeze(2).to_broadcast([128, 4, 64]), op=ALU.mult),
        waits=[(sm.cs, cs_e2)] + W(g, "A.khat"), inc=(sm.cv, 1))
    cv_k2 = EMIT(g, "vector", lambda e: e.tensor_tensor(out=A.khat, in0=A.khat, in1=A.kn,
                                                        op=ALU.mult),
                 waits=[(sm.cv, cv_k1)], inc=(sm.cv, 1))
    MARK(g, "sb.rss", sm.cv, cv_k1)
    cv_qh = EMIT(g, "vector", lambda e: e.tensor_copy(out=A.hi_tm, in_=A.xn),
                 waits=[(sm.cv, cv_q2), (sm.pe, pe1)], inc=(sm.cv, 1))
    cv_qt = EMIT(g, "vector", lambda e: e.tensor_tensor(out=A.tmpf, in0=A.xn, in1=A.hi_tm,
                                                        op=ALU.subtract),
                 waits=[(sm.cv, cv_qh)] + W(g, "A.tmpf"), inc=(sm.cv, 1))
    cv_ql = EMIT(g, "vector", lambda e: e.tensor_copy(out=A.lo_tm, in_=A.tmpf),
                 waits=[(sm.cv, cv_qt), (sm.pe, pe2)], inc=(sm.cv, 1))
    MARK(g, "A.xn", sm.cv, cv_qt, add=("q" in g.debug))
    MARK(g, "A.P1", sm.cv, max(cv_qt, cv_q1))
    cv_kh = EMIT(g, "vector", lambda e: e.tensor_copy(out=A.k_hi, in_=A.khat),
                 waits=[(sm.cv, cv_k2)] + W(g, "A.k_hi"), inc=(sm.cv, 1))
    cv_kt = EMIT(g, "vector", lambda e: e.tensor_tensor(out=A.tmpf[:, 0:256], in0=A.khat,
                                                        in1=A.k_hi, op=ALU.subtract),
                 waits=[(sm.cv, cv_kh), (sm.cv, cv_ql)], inc=(sm.cv, 1))
    cv_kl = EMIT(g, "vector", lambda e: e.tensor_copy(out=A.k_lo, in_=A.tmpf[:, 0:256]),
                 waits=[(sm.cv, cv_kt)] + W(g, "A.k_lo"), inc=(sm.cv, 1))
    MARK(g, "A.khat", sm.cv, cv_kt, add=("kv" in g.debug))
    MARK(g, "A.tmpf", sm.cv, cv_kl)
    cv_v = EMIT(g, "vector", lambda e, tau=tau: e.tensor_copy(
        out=A.vbuf[:, tau * 256:(tau + 1) * 256], in_=A.kvsb[:, 256:512]),
        waits=[(sm.cv, cv_kv)], inc=(sm.cv, 1))
    MARK(g, "A.kvsb", sm.cv, cv_v, add=("kv" in g.debug))
    MARK(g, "A.kvsb", sm.cv, cv_k1, add=True)

    if "q" in g.debug:
        _sq = EMIT(g, "sync", lambda e, tau=tau: e.dma_start(
            out=g.dbg["q"][tau * 128:tau * 128 + 128, :], in_=A.xn),
            waits=[(sm.cv, cv_q2)], inc=(sm.st, 16))
        MARK(g, "A.xn", sm.st, _sq, add=True)
    if "kv" in g.debug:
        _sk = EMIT(g, "sync", lambda e, tau=tau: e.dma_start(
            out=g.dbg["kv"][tau * 128:tau * 128 + 128, 0:256], in_=A.khat),
            waits=[(sm.cv, cv_k2)], inc=(sm.st, 16))
        MARK(g, "A.khat", sm.st, _sk, add=True)
        _sk2 = EMIT(g, "sync", lambda e, tau=tau: e.dma_start(
            out=g.dbg["kv"][tau * 128:tau * 128 + 128, 256:512], in_=A.kvsb[:, 256:512]),
            waits=[(sm.cv, cv_kv)], inc=(sm.st, 16))
        MARK(g, "A.kvsb", sm.st, _sk2, add=True)
    def ktr(e):
        last = None
        for kvh in range(4):
            last = e.transpose(ps.tr[0:64, kvh * 128:kvh * 128 + 128],
                               A.k_hi[:, kvh * 64:(kvh + 1) * 64], sb.ident[:])
            last = e.transpose(ps.tr[64:128, kvh * 128:kvh * 128 + 128],
                               A.k_lo[:, kvh * 64:(kvh + 1) * 64], sb.ident[:])
        return last
    pe_k = EMIT(g, "tensor", ktr, waits=[(sm.cv, cv_kl)] + W(g, "ps.tr"), inc=(sm.pe, 1))
    cv_ke = EMIT(g, "vector", lambda e, tau=tau: e.tensor_copy(
        out=A.khl.rearrange("p (kv t) -> p kv t", kv=4)[:, :, tau * 128:tau * 128 + 128],
        in_=ps.tr[:, 0:512].rearrange("p (kv t) -> p kv t", kv=4)),
        waits=[(sm.pe, pe_k)], inc=(sm.cv, 1))
    MARK(g, "A.k_hi", sm.pe, pe_k)
    MARK(g, "A.k_lo", sm.pe, pe_k)

    def qtr_hi(e):
        last = None
        for h in range(16):
            pr = (h % 2) * 64
            last = e.transpose(ps.tr[pr:pr + 64, (h // 2) * 128:(h // 2) * 128 + 128],
                               A.hi_tm[:, h * 64:(h + 1) * 64], sb.ident[:])
        return last
    pe_qh = EMIT(g, "tensor", qtr_hi, waits=[(sm.cv, cv_qh), (sm.cv, cv_ke)], inc=(sm.pe, 1))

    def qev_hi(e):
        last = None
        for h in range(16):
            pr = (h % 2) * 64
            src = ps.tr[pr:pr + 64, (h // 2) * 128:(h // 2) * 128 + 128]
            last = e.tensor_copy(out=A.lhsT1[0:64, h * 128:(h + 1) * 128], in_=src)
            last = e.tensor_copy(out=A.lhsT1x[64:128, h * 128:(h + 1) * 128], in_=src)
        return last
    cv_qeh = EMIT(g, "vector", qev_hi, waits=[(sm.pe, pe_qh)] + W(g, "A.lhsT1.r"),
                  inc=(sm.cv, 1))

    def qtr_lo(e):
        last = None
        for h in range(16):
            pr = (h % 2) * 64
            last = e.transpose(ps.tr[pr:pr + 64, (h // 2) * 128:(h // 2) * 128 + 128],
                               A.lo_tm[:, h * 64:(h + 1) * 64], sb.ident[:])
        return last
    pe_ql = EMIT(g, "tensor", qtr_lo, waits=[(sm.cv, cv_ql), (sm.cv, cv_qeh)], inc=(sm.pe, 1))

    def qev_lo(e):
        last = None
        for h in range(16):
            pr = (h % 2) * 64
            src = ps.tr[pr:pr + 64, (h // 2) * 128:(h // 2) * 128 + 128]
            last = e.tensor_copy(out=A.lhsT1[64:128, h * 128:(h + 1) * 128], in_=src)
            last = e.tensor_copy(out=A.lhsT1x[0:64, h * 128:(h + 1) * 128], in_=src)
        return last
    cv_qel = EMIT(g, "vector", qev_lo, waits=[(sm.pe, pe_ql)], inc=(sm.cv, 1))
    MARK(g, "ps.tr", sm.cv, cv_qel)
    MARK(g, "A.hi_tm", sm.pe, pe_qh)
    MARK(g, "A.lo_tm", sm.pe, pe_ql)

    kready = max(cv_ke, cv_qel)
    pns = {}
    for h in range(H):
        if h >= 1:
            _phaseA_head_y(g, tau, h - 1, pns[h - 1])
        pns[h] = _phaseA_head_soft(g, tau, h, kready)
    _phaseA_head_y(g, tau, H - 1, pns[H - 1])

    yw = g.war["A.yfm.w"]

    def oproj(e):
        last = None
        for oc in range(2):
            for c in range(8):
                last = e.matmul(ps.proj[:, oc * 512:(oc + 1) * 512],
                                A.yfm[:, c * 128:(c + 1) * 128],
                                A.wo[:, c * 1024 + oc * 512: c * 1024 + oc * 512 + 512],
                                start=(c == 0), stop=(c == 7))
        return last
    pe_s = EMIT(g, "tensor", oproj, waits=yw + W(g, "ps.proj"), inc=(sm.pe, 1))
    cv_s = EMIT(g, "vector", lambda e: e.tensor_tensor(out=A.stile, in0=ps.proj[:, :],
                                                       in1=A.bo, op=ALU.add),
                waits=[(sm.pe, pe_s)] + W(g, "A.stile"), inc=(sm.cv, 1))
    st_s = EMIT(g, "sync", lambda e, tau=tau: e.dma_start(
        out=dr["s_scr"][tau * 128:tau * 128 + 128, :], in_=A.stile),
        waits=[(sm.cv, cv_s)], inc=(sm.st, 16))
    MARK(g, "ps.proj", sm.cv, cv_s)
    MARK(g, "A.stile", sm.st, st_s)
    MARK(g, "A.yfm.r", sm.pe, pe_s)


def _phaseA_head_soft(g, tau, h, kready):
    nc, sb, A, ps, dr, sm = g.nc, g.sb, g.A, g.ps, g.dr, g.sm
    L = (tau + 1) * 128
    kv = h // 4
    pp = h % 2
    P = A.P[pp]
    thr = sb.thr[:, pp * 2:pp * 2 + 2]
    zr = sb.zrec[:, pp * 2:pp * 2 + 2]

    rounds = [(0, min(L, 2048))]
    if L > 2048:
        rounds.append((2048, L))
    cs_ev = None
    for (c0, c1) in rounds:
        def smm(e, c0=c0, c1=c1, h=h, kv=kv):
            last = None
            for cc in range(c0, c1, 512):
                w = min(512, c1 - cc)
                sl = ps.S[:, cc - c0:cc - c0 + w]
                last = e.matmul(sl, A.lhsT1[:, h * 128:(h + 1) * 128],
                                A.khl[:, kv * T + cc: kv * T + cc + w],
                                start=True, stop=False)
                last = e.matmul(sl, A.lhsT1x[:, h * 128:(h + 1) * 128],
                                A.khl[:, kv * T + cc: kv * T + cc + w],
                                start=False, stop=False)
                last = e.matmul(sl, A.qaug[:, h * 128:(h + 1) * 128],
                                A.kaug[:, cc:cc + w], start=False, stop=True)
            return last
        pe_sc = EMIT(g, "tensor", smm, waits=[(sm.cv, kready)] + W(g, "ps.S"),
                     inc=(sm.pe, 1))
        cs_ev = EMIT(g, "scalar", lambda e, c0=c0, c1=c1: e.copy(out=A.S[:, c0:c1],
                                                                 in_=ps.S[:, 0:c1 - c0]),
                     waits=[(sm.pe, pe_sc)] + (W(g, "A.S") if c0 == 0 else []),
                     inc=(sm.cs, 1))
        MARK(g, "ps.S", sm.cs, cs_ev)
    MARK(g, "A.lhsT1.r", sm.pe, pe_sc)

    cv_dg = EMIT(g, "vector", lambda e, L=L: e.tensor_tensor(
        out=A.S[:, L - 128:L], in0=A.S[:, L - 128:L], in1=sb.diagmask[:], op=ALU.add),
        waits=[(sm.cs, cs_ev)], inc=(sm.cv, 1))

    if tau <= 3:
        if L < 512:
            EMIT(g, "vector", lambda e, L=L: e.memset(A.coll[:, L:512], NEG),
                 waits=W(g, "A.coll"), inc=(sm.cv, 1))
        cv_coll = EMIT(g, "vector", lambda e, L=L: e.tensor_copy(out=A.coll[:, 0:L],
                                                                 in_=A.S[:, 0:L]),
                       waits=[(sm.cv, cv_dg)] + W(g, "A.coll"), inc=(sm.cv, 1))
    else:
        def strips(e, L=L):
            S3 = A.S[:, 0:L].rearrange("p (a b) -> p a b", b=64)
            last = None
            for i in range(64):
                last = e.max(out=A.coll[:, i * 8:(i + 1) * 8], in_=S3[:, :, i])
            return last
        cv_coll = EMIT(g, "vector", strips, waits=[(sm.cv, cv_dg)] + W(g, "A.coll"),
                       inc=(sm.cv, 1))

    prev = cv_coll
    cs_exp = None
    for r in range(8):
        cv_m = EMIT(g, "vector", lambda e: e.max(out=sb.m8[:, :], in_=A.coll[:, :]),
                    waits=[(sm.cv, prev)] + W(g, "sb.m8"), inc=(sm.cv, 1))
        if r == 0:
            cv_nm = EMIT(g, "vector", lambda e: e.tensor_scalar(
                out=thr[:, 1:2], in0=sb.m8[:, 0:1], scalar1=-1.0, scalar2=None,
                op0=ALU.mult), waits=[(sm.cv, cv_m)] + W(g, "A.thr%d" % pp),
                inc=(sm.cv, 1))
            cs_exp = EMIT(g, "scalar", lambda e, L=L: e.activation(
                out=P[:, 0:L], in_=A.S[:, 0:L], func=AF.Exp, bias=thr[:, 1:2]),
                waits=[(sm.cv, cv_nm)] + W(g, "A.P%d" % pp), inc=(sm.cs, 1))
        if r < 7:
            prev = EMIT(g, "vector", lambda e: e.match_replace(
                out=A.coll[:, :], in_to_replace=sb.m8[:, :], in_values=A.coll[:, :],
                imm_value=NEG), waits=[(sm.cv, cv_m)], inc=(sm.cv, 1))
            MARK(g, "sb.m8", sm.cv, prev)
        else:
            cv_th = EMIT(g, "vector", lambda e: e.tensor_copy(out=thr[:, 0:1],
                                                              in_=sb.m8[:, 7:8]),
                         waits=[(sm.cv, cv_m)], inc=(sm.cv, 1))
            MARK(g, "sb.m8", sm.cv, cv_th)
            MARK(g, "A.coll", sm.cv, cv_th)

    cv_mk = EMIT(g, "vector", lambda e, L=L: e.scalar_tensor_tensor(
        out=P[:, 0:L], in0=A.S[:, 0:L], scalar=thr[:, 0:1], in1=P[:, 0:L],
        op0=ALU.is_ge, op1=ALU.mult, accum_out=zr[:, 0:1]),
        waits=[(sm.cs, cs_exp), (sm.cv, cv_th)] + W(g, "A.zr%d" % pp), inc=(sm.cv, 1))
    MARK(g, "A.S", sm.cv, cv_mk)
    MARK(g, "A.thr%d" % pp, sm.cv, cv_mk)
    if "thr" in g.debug:
        _stv = EMIT(g, "sync", lambda e, tau=tau, h=h: e.dma_start(
            out=g.dbg["thr"][tau * 128:tau * 128 + 128, h:h + 1], in_=thr[:, 0:1],
            allow_slow_non_contiguous=True),
            waits=[(sm.cv, cv_th)], inc=(sm.st, 16))
        MARK(g, "A.thr%d" % pp, sm.st, _stv, add=True)
    cv_rz = EMIT(g, "vector", lambda e: e.reciprocal(out=zr[:, 1:2], in_=zr[:, 0:1]),
                 waits=[(sm.cv, cv_mk)], inc=(sm.cv, 1))
    cs_pn = EMIT(g, "scalar", lambda e, L=L: e.activation(
        out=P[:, 0:L], in_=P[:, 0:L], func=AF.Copy, scale=zr[:, 1:2]),
        waits=[(sm.cv, cv_rz)], inc=(sm.cs, 1))
    MARK(g, "A.zr%d" % pp, sm.cs, cs_pn)
    return cs_pn


def _phaseA_head_y(g, tau, h, cs_pn):
    nc, sb, A, ps, dr, sm = g.nc, g.sb, g.A, g.ps, g.dr, g.sm
    L = (tau + 1) * 128
    kv = h // 4
    pp = h % 2
    P = A.P[pp]
    ntile = L // 128
    pe_y = None
    pe_pt = None
    for b0 in range(0, ntile, 8):
        b1 = min(ntile, b0 + 8)

        def ptr(e, b0=b0, b1=b1):
            last = None
            for i in range(b0, b1):
                last = e.transpose(ps.tr[:, (i - b0) * 128:(i - b0 + 1) * 128],
                                   P[:, i * 128:(i + 1) * 128], sb.ident[:])
            return last
        pe_pt = EMIT(g, "tensor", ptr, waits=[(sm.cs, cs_pn)] + W(g, "ps.tr"),
                     inc=(sm.pe, 1))
        cv_pt = EMIT(g, "vector", lambda e, b0=b0, b1=b1: e.tensor_copy(
            out=A.PT[:, 0:(b1 - b0) * 128], in_=ps.tr[:, 0:(b1 - b0) * 128]),
            waits=[(sm.pe, pe_pt)] + W(g, "A.PT"), inc=(sm.cv, 1))
        MARK(g, "ps.tr", sm.cv, cv_pt)

        def ymm(e, b0=b0, b1=b1, ntile=ntile, kv=kv):
            last = None
            for i in range(b0, b1):
                last = e.matmul(ps.y[0:64, 0:128],
                                A.vbuf[:, i * 256 + kv * 64:i * 256 + kv * 64 + 64],
                                A.PT[:, (i - b0) * 128:(i - b0 + 1) * 128],
                                start=(i == 0), stop=(i == ntile - 1))
            return last
        pe_y = EMIT(g, "tensor", ymm,
                    waits=[(sm.cv, cv_pt)] + (W(g, "ps.y") if b0 == 0 else []),
                    inc=(sm.pe, 1))
        MARK(g, "A.PT", sm.pe, pe_y)
    MARK(g, "A.P%d" % pp, sm.pe, pe_pt)
    if pp == 1:
        MARK(g, "A.xt", sm.pe, pe_pt)
        MARK(g, "A.xn", sm.pe, pe_pt)
    cs_y = EMIT(g, "scalar", lambda e, h=h: e.copy(
        out=A.yfm[(h % 2) * 64:(h % 2) * 64 + 64, (h // 2) * 128:(h // 2) * 128 + 128],
        in_=ps.y[0:64, 0:128]),
        waits=[(sm.pe, pe_y)] + (W(g, "A.yfm.r") if h == 0 else []), inc=(sm.cs, 1))
    MARK(g, "ps.y", sm.cs, cs_y)
    MARK(g, "A.yfm.w", sm.cs, cs_y)


def _phaseB(g, mini=False):
    nc, sb, dr, sm, ar = g.nc, g.sb, g.dr, g.sm, g.ar
    B = G()
    g.B = B
    rb = Reg(ar)
    B.rcsq = rb.alloc(128, NT * 64, F32)
    B.rssq = rb.alloc(128, NT * 64, F32)
    B.rcsk = rb.alloc(128, NT * 64, F32)
    B.rssk = rb.alloc(128, NT * 64, F32)
    B.qsc = rb.alloc(128, 1024, F32)
    B.ksc = rb.alloc(128, 1024, F32)
    B.lvec = rb.alloc(128, 72, F32)
    B.bndtab = rb.alloc(128, 8, F32)
    B.bmask = rb.alloc(128, 3 * 384, BF16)
    B.onesrow = rb.alloc(1, 128, F32)
    B.rtile = rb.alloc(128, 8 * 1025, F32)
    B.bnd = rb.alloc(128, 8, F32)
    common_end = rb.off
    r1 = Reg(ar, common_end)
    B.wqkv = r1.alloc(128, 8 * 3072, BF16)
    B.kTring = r1.alloc(128, 8 * 1280, BF16)
    B.vring = r1.alloc(128, 10 * 1024, BF16)
    B.qt = r1.alloc(128, 1024, F32)
    B.kt = r1.alloc(128, 1024, F32)
    _woff = (r1.off + 511) // 512 * 512
    B.t0 = r1.alloc(128, 1024, F32)
    B.t1 = r1.alloc(128, 1024, F32)
    B.t2 = B.qt
    B.ainT = r1.alloc(128, 8 * 128, BF16)
    B.qhb = r1.alloc(128, 1024, BF16)
    B.khb = r1.alloc(128, 1024, BF16)
    B.qT = r1.alloc(128, 8 * 128, BF16)
    B.woin = ar.view(_woff, 128, 8 * 1024, BF16)
    assert r1.off - _woff >= 16 * 1024
    B.Ssb = [r1.alloc(128, 384, F32), r1.alloc(128, 384, F32)]
    B.Pb = [r1.alloc(128, 384, BF16), r1.alloc(128, 384, BF16)]
    B.PT = [r1.alloc(128, 384, BF16), r1.alloc(128, 384, BF16)]
    B.yfm = r1.alloc(128, 8 * 512, BF16)
    B.zr = r1.alloc(128, 8, F32)
    b1_end = r1.off
    r2 = Reg(ar, common_end)
    B.w1buf = [r2.alloc(128, 2048, BF16), r2.alloc(128, 2048, BF16)]
    B.w2buf = [r2.alloc(128, 1024, BF16), r2.alloc(128, 1024, BF16)]
    B.xnf = r2.alloc(128, 8 * 1024, BF16)
    B.hbuf = [r2.alloc(128, 22 * 512, BF16), r2.alloc(128, 22 * 512, BF16)]
    B.rbf = r2.alloc(128, 8 * 1025, BF16)
    B.xsc = r2.alloc(128, 1024, F32)
    B.gateg = [r2.alloc(128, 512, BF16), r2.alloc(128, 512, BF16)]
    B.mub = r2.alloc(128, 1024, BF16)
    B.rsb = r2.alloc(128, 1024, F32)
    B.st2 = r2.alloc(1, 2 * 1024, F32)
    b2_end = r2.off
    assert max(b1_end, b2_end) <= ar.nbytes, (b1_end, b2_end, ar.nbytes)

    bar = [(sm.pe, sm.pe.n), (sm.cv, sm.cv.n), (sm.cs, sm.cs.n),
           (sm.st, sm.st.n), (sm.sg, sm.sg.n), (sm.cg, sm.cg.n), (sm.ld, sm.ld.n),
           (sm.lg, sm.lg.n), (sm.lw, sm.lw.n), (sm.lr, sm.lr.n)]
    g.war_floor = bar
    for dst, srcn in [(B.bndtab, "bndtab"), (B.onesrow, "onesrow"),
                      (B.rcsq, "rcsq"), (B.rssq, "rssq"), (B.rcsk, "rcsk"),
                      (B.rssk, "rssk")]:
        EMIT(g, "sync", lambda e, dst=dst, srcn=srcn: e.dma_start(out=dst, in_=dr[srcn][:]),
             waits=bar, inc=(sm.ld, 16))
    EMIT(g, "gpsimd", lambda e: e.dma_start(out=B.bmask, in_=dr["bmask"][:]),
         waits=bar, inc=(sm.lg, 16))
    g.Bld0 = sm.ld.n
    g.mini = mini
    if mini:
        _phaseB_layer(g, 0)
        if "rl" in g.debug:
            EMIT(g, "sync", lambda e: e.dma_start(out=g.dbg["rl"][0], in_=dr["r_scr"][:]),
                 waits=[(sm.st, sm.st.n), (sm.cv, sm.cv.n)], inc=(sm.ld, 16))
        return
    if "rl" in g.debug:
        EMIT(g, "sync", lambda e: e.dma_start(out=g.dbg["rl"][0], in_=dr["r_scr"][:]),
             waits=[(sm.st, sm.st.n), (sm.cv, sm.cv.n)], inc=(sm.ld, 16))
    for l in range(DEPTH):
        _phaseB_layer(g, l)
        if "rl" in g.debug:
            EMIT(g, "sync", lambda e, l=l: e.dma_start(out=g.dbg["rl"][l + 1], in_=dr["r_scr"][:]),
                 waits=[(sm.st, sm.st.n), (sm.cv, sm.cv.n)], inc=(sm.ld, 16))


def _phaseB_layer(g, l):
    nc, sb, dr, sm, B = g.nc, g.sb, g.dr, g.sm, g.B
    WCWO = 8 * 3072
    WCW1 = 8 * 3072 + 8 * 1024
    WCW2 = WCW1 + 22 * 2048
    for dst, srcn in [(B.lvec[:, 0:22], "b1a"), (B.lvec[:, 22:44], "b1g"),
                      (B.lvec[:, 44:52], "b2"), (B.lvec[:, 52:60], "lng"),
                      (B.lvec[:, 60:68], "lnb"), (B.qsc, "qsc"), (B.ksc, "ksc")]:
        EMIT(g, "sync", lambda e, dst=dst, srcn=srcn: e.dma_start(out=dst, in_=dr[srcn][l]),
             waits=W(g, "B.lconsts") + [(sm.ld, g.Bld0)], inc=(sm.ld, 16))
    g.ld_lv = sm.ld.n
    ldq = EMIT(g, "sync", lambda e: e.dma_start(
        out=B.wqkv, in_=dr["wbf"][l, :, 0:8 * 3072]),
        waits=W(g, "B.wqkv") + [(sm.lg, g.wbf_rdy)], inc=(sm.lw, 16))
    g.ld_wqkv = ldq
    for wb in range(getattr(g, "mini_n", None) or (1 if getattr(g, "mini", False) else 4)):
        _phaseB_attn_wb(g, l, wb, WCWO)
    if "ra" in g.debug and l == 0:
        EMIT(g, "sync", lambda e: e.dma_start(out=g.dbg["ra"], in_=dr["r_scr"][:]),
             waits=[(sm.st, sm.st.n), (sm.cv, sm.cv.n), (sm.cs, sm.cs.n)], inc=(sm.ld, 16))
        MARK(g, "B.lconsts", sm.cv, sm.cv.n)
    for pc in range(getattr(g, "mini_n", None) or (1 if getattr(g, "mini", False) else 4)):
        _phaseB_ffn_pair(g, l, pc, WCW1)
    MARK(g, "B.lconsts", sm.cv, sm.cv.n)
    MARK(g, "B.lconsts", sm.cs, sm.cs.n, add=True)
    MARK(g, "B.lconsts", sm.cg, sm.cg.n, add=True)


def _load_rtile(g, t0, ncols, first_seg):
    nc, dr, sm, B = g.nc, g.dr, g.sm, g.B
    rv = B.rtile.rearrange("p (c t) -> p c t", c=8)
    rsrc = dr["r_init"] if getattr(g, "mini", False) else dr["r_scr"]
    ldr = EMIT(g, "sync", lambda e: e.dma_start(
        out=rv[:, :, 1:1 + ncols],
        in_=rsrc.rearrange("p (c t) -> p c t", c=8)[:, :, t0:t0 + ncols]),
        waits=W(g, "B.rtile"), inc=(sm.lr, 16))
    if first_seg:
        cvz = EMIT(g, "vector", lambda e: e.memset(rv[:, :, 0:1], 0.0),
                   waits=[(sm.lr, ldr)], inc=(sm.cv, 1))
    else:
        cvz = EMIT(g, "vector", lambda e: e.tensor_copy(out=rv[:, :, 0:1], in_=B.bnd.unsqueeze(2)),
                   waits=[(sm.lr, ldr)] + W(g, "B.bnd"), inc=(sm.cv, 1))
    r_ready = [(sm.lr, ldr), (sm.cv, cvz)]
    cvb = EMIT(g, "vector", lambda e: e.tensor_copy(out=B.bnd.unsqueeze(2),
                                                    in_=rv[:, :, ncols:ncols + 1]),
               waits=r_ready, inc=(sm.cv, 1))
    MARK(g, "B.bnd", sm.cv, cvb)
    return rv, r_ready


def _phaseB_attn_wb(g, l, wb, WCWO):
    nc, sb, dr, sm, B, ps = g.nc, g.sb, g.dr, g.sm, g.B, g.ps
    t0 = wb * 1024
    rv, r_ready = _load_rtile(g, t0, 1024, wb == 0)
    for ti in range(8):
        _phaseB_tile(g, l, wb, ti, rv, r_ready)
        if ti == 3 or ti == 7:
            _phaseB_oproj(g, l, wb, ti // 4, rv, WCWO, r_ready)
    st_r = EMIT(g, "sync", lambda e: e.dma_start(
        out=dr["r_scr"].rearrange("p (c t) -> p c t", c=8)[:, :, t0:t0 + 1024],
        in_=rv[:, :, 1:1025]),
        waits=W(g, "B.rtile.w"), inc=(sm.st, 16))
    MARK(g, "B.rtile", sm.st, st_r)


def _phaseB_tile(g, l, wb, ti, rv, r_ready):
    nc, sb, dr, sm, B, ps = g.nc, g.sb, g.dr, g.sm, g.B, g.ps
    gt = wb * 8 + ti
    slot = gt % 10
    c0 = ti * 128

    at = B.ainT.rearrange("p (c t) -> p c t", c=8)
    cva = EMIT(g, "vector", lambda e: e.tensor_copy(out=at[:, 0:4, :], in_=rv[:, 0:4, 1 + c0:1 + c0 + 128]),
               waits=r_ready + W(g, "B.ainT"), inc=(sm.cv, 1))
    cva2 = EMIT(g, "vector", lambda e: e.tensor_copy(out=at[:, 4:8, :], in_=rv[:, 4:8, c0:c0 + 128]),
                waits=r_ready, inc=(sm.cv, 1))
    a_rdy = [(sm.cv, cva), (sm.cv, cva2)]

    def proj(e, o0):
        last = None
        for oc in range(2):
            sl = ps.proj[:, oc * 512:(oc + 1) * 512]
            for c in range(8):
                last = e.matmul(sl, at[:, c, :],
                                B.wqkv[:, c * 3072 + (o0 + oc) * 512: c * 3072 + (o0 + oc) * 512 + 512],
                                start=(c == 0), stop=(c == 7))
        return last
    pe_q = EMIT(g, "tensor", lambda e: proj(e, 0),
                waits=a_rdy + W(g, "ps.proj") + [(sm.lw, g.ld_wqkv)], inc=(sm.pe, 1))
    cv_q = EMIT(g, "scalar", lambda e: e.copy(out=B.qt, in_=ps.proj[:, :]),
                waits=[(sm.pe, pe_q)] + W(g, "B.qt") + W(g, "B.t2"), inc=(sm.cs, 1))
    MARK(g, "ps.proj", sm.cs, cv_q)
    pe_k = EMIT(g, "tensor", lambda e: proj(e, 2),
                waits=[(sm.cs, cv_q)], inc=(sm.pe, 1))
    cv_k = EMIT(g, "scalar", lambda e: e.copy(out=B.kt, in_=ps.proj[:, :]),
                waits=[(sm.pe, pe_k)] + W(g, "B.kt"), inc=(sm.cs, 1))
    MARK(g, "ps.proj", sm.cs, cv_k)
    pe_v = EMIT(g, "tensor", lambda e: proj(e, 4),
                waits=[(sm.cs, cv_k)], inc=(sm.pe, 1))
    cv_v = EMIT(g, "scalar", lambda e: e.copy(
        out=B.vring[:, slot * 1024:(slot + 1) * 1024], in_=ps.proj[:, :]),
        waits=[(sm.pe, pe_v)] + W(g, "B.vring%d" % slot), inc=(sm.cs, 1))
    MARK(g, "ps.proj", sm.cs, cv_v)

    def norm_rot(which, src, scl, cs_t, ss_t, outb, prev):
        sqv = EMIT(g, "scalar", lambda e: e.activation(out=B.t0, in_=src, func=AF.Square),
                   waits=[(sm.cs, prev)] + W(g, "B.t0"), inc=(sm.cs, 1))
        ssv = EMIT(g, "vector", lambda e: e.tensor_reduce(
            out=sb.rss[:, 0:16], in_=B.t0.rearrange("p (h d) -> p h d", h=16),
            axis=AX.X, op=ALU.add), waits=[(sm.cs, sqv)] + W(g, "sb.rss"), inc=(sm.cv, 1))
        MARK(g, "B.t0", sm.cv, ssv)
        cl = EMIT(g, "vector", lambda e: e.tensor_scalar(
            out=sb.rss[:, 0:16], in0=sb.rss[:, 0:16], scalar1=1e-24, scalar2=None,
            op0=ALU.max), waits=[(sm.cv, ssv)], inc=(sm.cv, 1))
        l1 = EMIT(g, "scalar", lambda e: e.activation(out=sb.rss[:, 0:16], in_=sb.rss[:, 0:16],
                                                      func=AF.Ln),
                  waits=[(sm.cv, cl)], inc=(sm.cs, 1))
        l2 = EMIT(g, "scalar", lambda e: e.activation(out=sb.rss[:, 0:16], in_=sb.rss[:, 0:16],
                                                      func=AF.Exp, scale=-0.5),
                  waits=[(sm.cs, l1)], inc=(sm.cs, 1))
        MARK(g, "sb.rss", sm.cs, l2)
        m1 = EMIT(g, "vector", lambda e: e.tensor_tensor(
            out=B.t1.rearrange("p (h d) -> p h d", h=16),
            in0=src.rearrange("p (h d) -> p h d", h=16),
            in1=sb.rss[:, 0:16].unsqueeze(2).to_broadcast([128, 16, 64]), op=ALU.mult),
            waits=[(sm.cs, l2)] + W(g, "B.t1"), inc=(sm.cv, 1))
        MARK(g, "B.qt.rd", sm.cv, m1)
        m2 = EMIT(g, "gpsimd", lambda e: e.tensor_tensor(out=B.t1, in0=B.t1, in1=scl,
                                                         op=ALU.mult),
                  waits=[(sm.cv, m1), (sm.ld, g.ld_lv)], inc=(sm.cg, 1))
        csl = cs_t[:, gt * 64:(gt + 1) * 64]
        ssl = ss_t[:, gt * 64:(gt + 1) * 64]
        r1v = EMIT(g, "vector", lambda e: e.tensor_tensor(
            out=B.t2.rearrange("p (h d) -> p h d", h=16),
            in0=B.t1.rearrange("p (h d) -> p h d", h=16),
            in1=csl.unsqueeze(1).to_broadcast([128, 16, 64]), op=ALU.mult),
            waits=[(sm.cg, m2)] + W(g, "B.t2") + W(g, "B.qt.rd"), inc=(sm.cv, 1))
        t1v = B.t1.rearrange("p (h s d) -> p h s d", h=16, s=2)
        t0v = B.t0.rearrange("p (h s d) -> p h s d", h=16, s=2)
        r2v = EMIT(g, "vector", lambda e: e.tensor_tensor(
            out=t0v[:, :, 0, :], in0=t1v[:, :, 1, :],
            in1=ssl[:, 0:32].unsqueeze(1).to_broadcast([128, 16, 32]), op=ALU.mult),
            waits=[(sm.cg, m2), (sm.cv, ssv)] + W(g, "B.t0"), inc=(sm.cv, 1))
        r3v = EMIT(g, "vector", lambda e: e.tensor_tensor(
            out=t0v[:, :, 1, :], in0=t1v[:, :, 0, :],
            in1=ssl[:, 32:64].unsqueeze(1).to_broadcast([128, 16, 32]), op=ALU.mult),
            waits=[(sm.cg, m2)], inc=(sm.cv, 1))
        fin = EMIT(g, "vector", lambda e: e.tensor_tensor(out=outb, in0=B.t2, in1=B.t0,
                                                          op=ALU.add),
                   waits=[(sm.cv, r1v), (sm.cv, r2v), (sm.cv, r3v)] + W(g, "B." + which),
                   inc=(sm.cv, 1))
        MARK(g, "B.t0", sm.cv, fin)
        MARK(g, "B.t1", sm.cv, max(r1v, r2v, r3v))
        MARK(g, "B.t2", sm.cv, fin)
        return fin
    fq = norm_rot("qhb", B.qt, B.qsc, B.rcsq, B.rssq, B.qhb, cv_q)
    MARK(g, "B.qt", sm.cv, fq)
    fk = norm_rot("khb", B.kt, B.ksc, B.rcsk, B.rssk, B.khb, cv_k)
    MARK(g, "B.kt", sm.cv, fk)
    if "b0" in g.debug and l == 0 and gt == 0:
        for si, buf in [(0, B.qhb), (1, B.khb)]:
            _sv = EMIT(g, "gpsimd", lambda e, si=si, buf=buf: e.dma_start(
                out=g.dbg["b0"][si], in_=buf),
                waits=[(sm.cv, fq), (sm.cv, fk)], inc=(sm.sg, 16))
        MARK(g, "B.qhb", sm.sg, sm.sg.n, add=True)
        MARK(g, "B.khb", sm.sg, sm.sg.n, add=True)
        _sv = EMIT(g, "gpsimd", lambda e: e.dma_start(
            out=g.dbg["b0"][2], in_=B.vring[:, 0:1024]),
            waits=[(sm.cs, cv_v)], inc=(sm.sg, 16))
        MARK(g, "B.vring0", sm.sg, _sv)

    trv = ps.tr.bitcast(BF16)
    def qtr(e):
        last = None
        for h in range(16):
            pr = (h % 2) * 64
            last = e.transpose(trv[pr:pr + 64, (h // 2) * 128:(h // 2) * 128 + 128],
                               B.qhb[:, h * 64:(h + 1) * 64], sb.ident[:])
        return last
    pe_qt = EMIT(g, "tensor", qtr, waits=[(sm.cv, fq)] + W(g, "ps.tr"), inc=(sm.pe, 1))
    cv_qt = EMIT(g, "scalar", lambda e: e.copy(out=B.qT, in_=trv[:, 0:1024]),
                 waits=[(sm.pe, pe_qt)] + W(g, "B.qT"), inc=(sm.cs, 1))
    MARK(g, "ps.tr", sm.cs, cv_qt)

    def ktr(e):
        last = None
        for h in range(16):
            pr = (h % 2) * 64
            last = e.transpose(trv[pr:pr + 64, (h // 2) * 128:(h // 2) * 128 + 128],
                               B.khb[:, h * 64:(h + 1) * 64], sb.ident[:])
        return last
    pe_kt = EMIT(g, "tensor", ktr, waits=[(sm.cv, fk), (sm.cs, cv_qt)], inc=(sm.pe, 1))
    kr = B.kTring.rearrange("p (b s) -> p b s", b=8)
    cv_kt = EMIT(g, "scalar", lambda e: e.copy(
        out=kr[:, :, slot * 128:(slot + 1) * 128],
        in_=trv[:, 0:1024].rearrange("p (b s) -> p b s", b=8)),
        waits=[(sm.pe, pe_kt)] + W(g, "B.kring%d" % slot), inc=(sm.cs, 1))
    MARK(g, "ps.tr", sm.cs, cv_kt)
    MARK(g, "B.qhb", sm.pe, pe_qt)
    MARK(g, "B.khb", sm.pe, pe_kt)

    nw = min(3, gt + 1)
    var = min(gt, 2)
    slots = [(gt - nw + 1 + j) % 10 for j in range(nw)]
    runs = []
    run = [slots[0]]
    for s_ in slots[1:]:
        if s_ == run[-1] + 1:
            run.append(s_)
        else:
            runs.append(run)
            run = [s_]
    runs.append(run)
    kready = [(sm.cs, cv_kt), (sm.cs, cv_qt)]
    vready = [(sm.cs, cv_v)]
    sc = {}
    pns = {}
    for h in range(4):
        sc[h] = _phaseB_head_scores(g, h, runs, kready)
    for k in range(H + 2):
        if k >= 2:
            _phaseB_head_y(g, ti, k - 2, nw, slots, pns[k - 2], vready)
        if k < H:
            pns[k] = _phaseB_head_soft(g, l, gt, k, nw, var, sc[k])
        if k + 4 < H:
            sc[k + 4] = _phaseB_head_scores(g, k + 4, runs, kready)


def _phaseB_head_scores(g, h, runs, kready):
    nc, sb, sm, B, ps = g.nc, g.sb, g.sm, g.B, g.ps
    pr = (h % 2) * 64
    blk = h // 2
    kr = B.kTring.rearrange("p (b s) -> p b s", b=8)
    psl = (h % 4) * 512

    def smm(e):
        last = None
        col = 0
        for run in runs:
            w = len(run) * 128
            last = e.matmul(ps.S[:, psl + col: psl + col + w],
                            B.qT[pr:pr + 64, blk * 128:(blk + 1) * 128],
                            kr[pr:pr + 64, blk, run[0] * 128:run[0] * 128 + w],
                            start=True, stop=True)
            col += w
        return last
    return EMIT(g, "tensor", smm, waits=kready + W(g, "ps.S%d" % (h % 4)), inc=(sm.pe, 1))


def _phaseB_head_soft(g, l, gt, h, nw, var, pe_s):
    nc, sb, sm, B, ps = g.nc, g.sb, g.sm, g.B, g.ps
    pp = h % 2
    psl = (h % 4) * 512
    Pb = B.Pb[pp]
    Ssb = B.Ssb[pp]
    zr = B.zr[:, pp * 4:pp * 4 + 4]
    cv_ev = EMIT(g, "vector", lambda e: e.tensor_tensor(
        out=Ssb[:, 0:nw * 128], in0=ps.S[:, psl:psl + nw * 128],
        in1=B.bmask[:, var * 384: var * 384 + nw * 128], op=ALU.add),
        waits=[(sm.pe, pe_s)] + W(g, "B.Ssb%d" % pp), inc=(sm.cv, 1))
    MARK(g, "ps.S%d" % (h % 4), sm.cv, cv_ev)
    cv_mx = EMIT(g, "vector", lambda e: e.tensor_reduce(
        out=zr[:, 3:4], in_=Ssb[:, 0:nw * 128], axis=AX.X, op=ALU.max, negate=True),
        waits=[(sm.cv, cv_ev)] + W(g, "B.zrA%d" % pp), inc=(sm.cv, 1))
    cs_e = EMIT(g, "scalar", lambda e: e.activation(
        out=Pb[:, 0:nw * 128], in_=Ssb[:, 0:nw * 128], func=AF.Exp,
        bias=zr[:, 3:4], accum_out=zr[:, 0:1]),
        waits=[(sm.cv, cv_mx)] + W(g, "B.Pb%d" % pp) + W(g, "B.zrB%d" % pp),
        inc=(sm.cs, 1))
    MARK(g, "B.Ssb%d" % pp, sm.cs, cs_e)
    MARK(g, "B.zrA%d" % pp, sm.cs, cs_e)
    cv_rz = EMIT(g, "vector", lambda e: e.reciprocal(out=zr[:, 1:2], in_=zr[:, 0:1]),
                 waits=[(sm.cs, cs_e)], inc=(sm.cv, 1))
    cv_pn = EMIT(g, "vector", lambda e: e.tensor_scalar(
        out=Pb[:, 0:nw * 128], in0=Pb[:, 0:nw * 128], scalar1=zr[:, 1:2],
        scalar2=None, op0=ALU.mult),
        waits=[(sm.cv, cv_rz)], inc=(sm.cv, 1))
    MARK(g, "B.zrB%d" % pp, sm.cv, cv_pn)
    if "b0" in g.debug and l == 0 and gt == 0 and h == 0:
        _s2 = EMIT(g, "gpsimd", lambda e: e.dma_start(
            out=g.dbg["b0"][4, :, 0:nw * 128], in_=Pb[:, 0:nw * 128]),
            waits=[(sm.cv, cv_pn)], inc=(sm.sg, 16))
        MARK(g, "B.Pb%d" % pp, sm.sg, _s2, add=True)
    return cv_pn


def _phaseB_head_y(g, ti, h, nw, slots, cv_pn, vready):
    nc, sb, sm, B, ps = g.nc, g.sb, g.sm, g.B, g.ps
    pr = (h % 2) * 64
    blk = h // 2
    pp = h % 2
    Pb = B.Pb[pp]
    PT = B.PT[pp]
    trv = ps.tr.bitcast(BF16)

    def ptr(e):
        last = None
        for j in range(nw):
            last = e.transpose(trv[:, j * 128:(j + 1) * 128],
                               Pb[:, j * 128:(j + 1) * 128], sb.ident[:])
        return last
    pe_p = EMIT(g, "tensor", ptr, waits=[(sm.cv, cv_pn)] + W(g, "ps.tr"), inc=(sm.pe, 1))
    MARK(g, "B.Pb%d" % pp, sm.pe, pe_p)
    cs_pt = EMIT(g, "scalar", lambda e: e.copy(out=PT[:, 0:nw * 128],
                                               in_=trv[:, 0:nw * 128]),
                 waits=[(sm.pe, pe_p)] + W(g, "B.PT%d" % pp), inc=(sm.cs, 1))
    MARK(g, "ps.tr", sm.cs, cs_pt)

    def ymm(e):
        last = None
        for j in range(nw):
            last = e.matmul(ps.y[0:64, 0:128],
                            B.vring[:, slots[j] * 1024 + h * 64: slots[j] * 1024 + h * 64 + 64],
                            PT[:, j * 128:(j + 1) * 128],
                            start=(j == 0), stop=(j == nw - 1))
        return last
    pe_y = EMIT(g, "tensor", ymm, waits=[(sm.cs, cs_pt)] + vready + W(g, "ps.y"),
                inc=(sm.pe, 1))
    MARK(g, "B.PT%d" % pp, sm.pe, pe_y)
    cs_y = EMIT(g, "scalar", lambda e: e.copy(
        out=B.yfm[pr:pr + 64, blk * 512 + (ti % 4) * 128: blk * 512 + (ti % 4) * 128 + 128],
        in_=ps.y[0:64, 0:128]),
        waits=[(sm.pe, pe_y)] + (W(g, "B.yfm.r") if (ti % 4 == 0 and h == 0) else []),
        inc=(sm.cs, 1))
    MARK(g, "ps.y", sm.cs, cs_y)
    MARK(g, "B.yfm.w", sm.cs, cs_y)


def _phaseB_oproj(g, l, wb, half, rv, WCWO, r_ready):
    nc, sb, dr, sm, B, ps = g.nc, g.sb, g.dr, g.sm, g.B, g.ps
    yw = W(g, "B.yfm.w")
    tbase = 1 + half * 512
    ldo = EMIT(g, "sync", lambda e: e.dma_start(
        out=B.woin, in_=dr["wbf"][l, :, WCWO:WCWO + 8 * 1024]),
        waits=W(g, "B.t0", "B.t1", "B.ainT", "B.qhb", "B.khb", "B.qT", "B.t2")
        + [(sm.cv, sm.cv.n), (sm.cs, sm.cs.n), (sm.cg, sm.cg.n)], inc=(sm.lw, 16))
    MARK(g, "B.t0", sm.lw, ldo); MARK(g, "B.t1", sm.lw, ldo)
    MARK(g, "B.ainT", sm.lw, ldo); MARK(g, "B.qhb", sm.lw, ldo)
    MARK(g, "B.khb", sm.lw, ldo); MARK(g, "B.qT", sm.lw, ldo)
    MARK(g, "B.t2", sm.lw, ldo)
    for dc in range(8):
        def mmdc(e, dc=dc):
            last = None
            for ic in range(8):
                last = e.matmul(ps.proj[:, 0:512],
                                B.woin[:, ic * 1024 + dc * 128: ic * 1024 + dc * 128 + 128],
                                B.yfm[:, ic * 512:(ic + 1) * 512],
                                start=(ic == 0), stop=(ic == 7))
            return last
        pe_o = EMIT(g, "tensor", mmdc,
                    waits=yw + W(g, "ps.proj") + [(sm.lw, ldo)], inc=(sm.pe, 1))
        cv_o = EMIT(g, "vector", lambda e, dc=dc: e.tensor_tensor(
            out=rv[:, dc, tbase:tbase + 512], in0=ps.proj[:, 0:512],
            in1=rv[:, dc, tbase:tbase + 512], op=ALU.add),
            waits=[(sm.pe, pe_o)] + r_ready + W(g, "B.bnd"), inc=(sm.cv, 1))
        MARK(g, "ps.proj", sm.cv, cv_o)
        MARK(g, "B.rtile.w", sm.cv, cv_o, add=(not (wb == 0 and half == 0 and dc == 0)))
    for nmw in ["B.t0", "B.t1", "B.ainT", "B.qhb", "B.khb", "B.qT", "B.t2"]:
        MARK(g, nmw, sm.pe, sm.pe.n)
    MARK(g, "B.yfm.r", sm.pe, sm.pe.n)


def _phaseB_ffn_pair(g, l, pc, WCW1):
    nc, sb, dr, sm, B, ps = g.nc, g.sb, g.dr, g.sm, g.B, g.ps
    t0 = pc * 1024
    rv, r_ready = _load_rtile(g, t0, 1024, pc == 0)
    rsrc = dr["r_init"] if getattr(g, "mini", False) else dr["r_scr"]
    rb3 = B.rbf.rearrange("p (c t) -> p c t", c=8)
    ldrb = EMIT(g, "gpsimd", lambda e: e.dma_start(
        out=rb3[:, :, 1:1025],
        in_=rsrc.rearrange("p (c t) -> p c t", c=8)[:, :, t0:t0 + 1024]),
        waits=W(g, "B.rbf") + W(g, "B.rtile") + [(sm.sg, sm.sg.n)], inc=(sm.lg, 16))
    cvzb = EMIT(g, "vector", lambda e: e.tensor_copy(out=rb3[:, :, 0:1], in_=rv[:, :, 0:1]),
                waits=[(sm.lg, ldrb)] + r_ready, inc=(sm.cv, 1))
    rb_rdy = [(sm.lg, ldrb), (sm.cv, cvzb)]

    def shb(dc, h2):
        if dc < 4:
            return rb3[:, dc, 1 + h2 * 512:513 + h2 * 512]
        return rb3[:, dc, h2 * 512:512 + h2 * 512]

    def sh(dc):
        if dc < 4:
            return rv[:, dc, 1:1025]
        return rv[:, dc, 0:1024]

    sslots = [ps.proj[0:1, 0:512], ps.proj[0:1, 512:1024],
              ps.y[0:1, 0:512], ps.tr[:].bitcast(F32)[0:1, 0:512]]

    def mmsum(e):
        last = None
        for h2 in range(2):
            for dc in range(8):
                last = e.matmul(sslots[h2], sb.onesb[:, 0:1], shb(dc, h2),
                                start=(dc == 0), stop=(dc == 7))
        return last
    pe_su = EMIT(g, "tensor", mmsum, waits=rb_rdy + W(g, "ps.proj") + [(sm.lg, g.ob_rdy)],
                 inc=(sm.pe, 1))
    cs_sq = EMIT(g, "scalar", lambda e: e.activation(out=B.rbf, in_=B.rbf, func=AF.Square),
                 waits=[(sm.pe, pe_su)], inc=(sm.cs, 1))

    def mmssq(e):
        last = None
        for h2 in range(2):
            for dc in range(8):
                last = e.matmul(sslots[2 + h2], sb.onesb[:, 0:1], shb(dc, h2),
                                start=(dc == 0), stop=(dc == 7))
        return last
    pe_sq = EMIT(g, "tensor", mmssq, waits=[(sm.cs, cs_sq)] + W(g, "ps.y", "ps.tr"),
                 inc=(sm.pe, 1))
    MARK(g, "B.rbf", sm.pe, pe_sq)
    cs_st = EMIT(g, "scalar", lambda e: [
        e.copy(out=B.st2[0:1, 0:512], in_=sslots[0]),
        e.copy(out=B.st2[0:1, 512:1024], in_=sslots[1]),
        e.copy(out=B.st2[0:1, 1024:1536], in_=sslots[2]),
        e.copy(out=B.st2[0:1, 1536:2048], in_=sslots[3])][-1],
        waits=[(sm.pe, pe_sq)] + W(g, "B.st2"), inc=(sm.cs, 1))
    MARK(g, "ps.y", sm.cs, cs_st)
    MARK(g, "ps.tr", sm.cs, cs_st)
    c12 = EMIT(g, "vector", lambda e: e.tensor_scalar(
        out=B.st2[0:1, 0:2048], in0=B.st2[0:1, 0:2048], scalar1=1.0 / D, scalar2=None,
        op0=ALU.mult), waits=[(sm.cs, cs_st)], inc=(sm.cv, 1))
    pe_b = EMIT(g, "tensor", lambda e: [
        e.matmul(ps.proj[:, 0:512], B.onesrow[0:1, :], B.st2[0:1, 0:512],
                 start=True, stop=True),
        e.matmul(ps.proj[:, 512:1024], B.onesrow[0:1, :], B.st2[0:1, 512:1024],
                 start=True, stop=True)][-1],
        waits=[(sm.cv, c12), (sm.cs, cs_st)] + W(g, "ps.proj"), inc=(sm.pe, 1))
    cs_mb = EMIT(g, "scalar", lambda e: e.copy(out=B.mub, in_=ps.proj[:, 0:1024]),
                 waits=[(sm.pe, pe_b)] + W(g, "B.mub"), inc=(sm.cs, 1))
    c3 = EMIT(g, "vector", lambda e: e.tensor_tensor(
        out=B.st2[0:1, 0:1024], in0=B.st2[0:1, 0:1024], in1=B.st2[0:1, 0:1024],
        op=ALU.mult), waits=[(sm.pe, pe_b)], inc=(sm.cv, 1))
    c4 = EMIT(g, "vector", lambda e: e.tensor_tensor(
        out=B.st2[0:1, 1024:2048], in0=B.st2[0:1, 1024:2048], in1=B.st2[0:1, 0:1024],
        op=ALU.subtract), waits=[(sm.cv, c3)], inc=(sm.cv, 1))
    c5 = EMIT(g, "scalar", lambda e: e.activation(out=B.st2[0:1, 1024:2048],
                                                  in_=B.st2[0:1, 1024:2048],
                                                  func=AF.Ln, bias=sb.fc[0:1, 0:1]),
              waits=[(sm.cv, c4)], inc=(sm.cs, 1))
    c6 = EMIT(g, "scalar", lambda e: e.activation(out=B.st2[0:1, 1024:2048],
                                                  in_=B.st2[0:1, 1024:2048],
                                                  func=AF.Exp, scale=-0.5),
              waits=[(sm.cs, c5)], inc=(sm.cs, 1))
    pe_b2 = EMIT(g, "tensor", lambda e: [
        e.matmul(ps.proj[:, 0:512], B.onesrow[0:1, :], B.st2[0:1, 1024:1536],
                 start=True, stop=True),
        e.matmul(ps.proj[:, 512:1024], B.onesrow[0:1, :], B.st2[0:1, 1536:2048],
                 start=True, stop=True)][-1],
        waits=[(sm.cs, c6), (sm.cs, cs_mb)], inc=(sm.pe, 1))
    cs_rb = EMIT(g, "scalar", lambda e: e.copy(out=B.rsb, in_=ps.proj[:, 0:1024]),
                 waits=[(sm.pe, pe_b2)] + W(g, "B.rsb"), inc=(sm.cs, 1))
    MARK(g, "ps.proj", sm.cs, cs_rb)
    MARK(g, "B.st2", sm.pe, pe_b2)

    xv = B.xnf.rearrange("p (c t) -> p c t", c=8)
    cv_xn = None
    for dc in range(8):
        s1 = EMIT(g, "gpsimd", lambda e, dc=dc: e.tensor_tensor(
            out=B.xsc, in0=sh(dc), in1=B.mub, op=ALU.subtract),
            waits=[(sm.cs, cs_mb)] + r_ready + W(g, "B.xsc"), inc=(sm.cg, 1))
        s2 = EMIT(g, "gpsimd", lambda e, dc=dc: e.tensor_tensor(
            out=B.xsc, in0=B.xsc, in1=B.rsb, op=ALU.mult),
            waits=[(sm.cs, cs_rb)], inc=(sm.cg, 1))
        cv_xn = EMIT(g, "vector", lambda e, dc=dc: e.tensor_scalar(
            out=xv[:, dc, :], in0=B.xsc, scalar1=B.lvec[:, 52 + dc:53 + dc],
            scalar2=B.lvec[:, 60 + dc:61 + dc], op0=ALU.mult, op1=ALU.add),
            waits=[(sm.cg, s2), (sm.ld, g.ld_lv)] + (W(g, "B.xnf") if dc == 0 else []),
            inc=(sm.cv, 1))
        MARK(g, "B.xsc", sm.cv, cv_xn)
    xn_rdy = [(sm.cv, cv_xn)]
    MARK(g, "B.mub", sm.cg, sm.cg.n)
    MARK(g, "B.rsb", sm.cg, sm.cg.n)

    if "f0" in g.debug and l == 0 and pc == 0:
        _f0 = EMIT(g, "gpsimd", lambda e: e.dma_start(out=g.dbg["f0"][0].bitcast(BF16),
                                                      in_=B.xnf),
                   waits=[(sm.cv, cv_xn)], inc=(sm.sg, 16))
        MARK(g, "B.xnf", sm.sg, _f0, add=True)
        _f1 = EMIT(g, "sync", lambda e: e.dma_start(out=g.dbg["f0"][1, :, 0:512].bitcast(BF16),
                                                    in_=B.mub),
                   waits=[(sm.cs, cs_mb)], inc=(sm.st, 16))
        MARK(g, "B.mub", sm.st, _f1, add=True)
        _f2 = EMIT(g, "sync", lambda e: e.dma_start(out=g.dbg["f0"][1, :, 1024:2048], in_=B.rsb),
                   waits=[(sm.cs, cs_rb)], inc=(sm.st, 16))
        MARK(g, "B.rsb", sm.st, _f2, add=True)

    h_last = [None, None]
    for m in range(22):
        wdt = min(128, 2730 - m * 128)
        buf = B.w1buf[m % 2]
        ldw = EMIT(g, "sync", lambda e, m=m, buf=buf: e.dma_start(
            out=buf, in_=dr["wbf"][l, :, WCW1 + m * 2048:WCW1 + (m + 1) * 2048]),
            waits=W(g, "B.w1buf%d" % (m % 2)) + [(sm.lg, g.wbf_rdy)], inc=(sm.lw, 16))
        for h2 in range(2):
            u = m * 2 + h2
            pb = (u % 2) * 1024

            def amm(e, m=m, wdt=wdt, buf=buf, h2=h2, pb=pb):
                last = None
                for dc in range(8):
                    last = e.matmul(ps.S[0:wdt, pb:pb + 512],
                                    buf[:, dc * wdt: (dc + 1) * wdt],
                                    B.xnf[:, dc * 1024 + h2 * 512:dc * 1024 + h2 * 512 + 512],
                                    start=(dc == 0), stop=(dc == 7))
                for dc in range(8):
                    last = e.matmul(ps.S[0:wdt, pb + 512:pb + 1024],
                                    buf[:, 8 * wdt + dc * wdt: 8 * wdt + (dc + 1) * wdt],
                                    B.xnf[:, dc * 1024 + h2 * 512:dc * 1024 + h2 * 512 + 512],
                                    start=(dc == 0), stop=(dc == 7))
                return last
            pe_a = EMIT(g, "tensor", amm,
                        waits=xn_rdy + [(sm.lw, ldw)] + W(g, "ps.Sab%d" % (u % 2)),
                        inc=(sm.pe, 1))
            cs_g = EMIT(g, "scalar", lambda e, m=m, wdt=wdt, h2=h2, pb=pb: e.activation(
                out=B.gateg[h2][0:wdt, :], in_=ps.S[0:wdt, pb + 512:pb + 1024], func=AF.Gelu,
                bias=B.lvec[0:wdt, 22 + m:23 + m]),
                waits=[(sm.pe, pe_a)] + W(g, "B.gateg%d" % h2), inc=(sm.cs, 1))
            cv_h = EMIT(g, "vector", lambda e, m=m, wdt=wdt, h2=h2, pb=pb: e.scalar_tensor_tensor(
                out=B.hbuf[h2][0:wdt, m * 512:(m + 1) * 512], in0=ps.S[0:wdt, pb:pb + 512],
                scalar=B.lvec[0:wdt, m:m + 1], in1=B.gateg[h2][0:wdt, :],
                op0=ALU.add, op1=ALU.mult),
                waits=[(sm.cs, cs_g)] + (W(g, "B.hbuf%d" % h2) if m == 0 else []),
                inc=(sm.cv, 1))
            MARK(g, "B.gateg%d" % h2, sm.cv, cv_h)
            MARK(g, "ps.Sab%d" % (u % 2), sm.cv, cv_h)
            h_last[h2] = cv_h
        MARK(g, "B.w1buf%d" % (m % 2), sm.pe, pe_a)
    if "f0" in g.debug and l == 0 and pc == 0:
        _f3 = EMIT(g, "gpsimd", lambda e: e.dma_start(out=g.dbg["f0"][2].bitcast(BF16),
                                                      in_=B.hbuf[0][:, 0:8192]),
                   waits=[(sm.cv, h_last[0])], inc=(sm.sg, 16))
        MARK(g, "B.hbuf0", sm.sg, sm.sg.n, add=True)

    banks = [ps.S[:, 0:512], ps.S[:, 512:1024], ps.S[:, 1024:1536], ps.S[:, 1536:2048],
             ps.proj[:, 0:512], ps.proj[:, 512:1024], ps.y[:, 0:512],
             ps.tr[:].bitcast(F32)[:, 0:512]]
    WCW2 = WCW1 + 22 * 2048
    for h2 in range(2):
        tbase = 1 + h2 * 512
        pe_w2 = None
        for m in range(22):
            wdt = min(128, 2730 - m * 128)
            buf = B.w2buf[m % 2]
            ldw2 = EMIT(g, "sync", lambda e, m=m, buf=buf, h2=h2: e.dma_start(
                out=buf, in_=dr["wbf"][l, :, WCW2 + m * 1024:WCW2 + (m + 1) * 1024]),
                waits=W(g, "B.w2buf%d" % (m % 2)) + [(sm.lg, g.wbf_rdy)], inc=(sm.lw, 16))

            def w2mm(e, m=m, wdt=wdt, buf=buf, h2=h2):
                last = None
                for dco in range(8):
                    last = e.matmul(banks[dco][0:128, :] if wdt == 128 else banks[dco],
                                    buf[0:wdt, dco * 128:(dco + 1) * 128],
                                    B.hbuf[h2][0:wdt, m * 512:(m + 1) * 512],
                                    start=(m == 0), stop=(m == 21))
                return last
            pe_w2 = EMIT(g, "tensor", w2mm,
                         waits=[(sm.cv, h_last[0]), (sm.cv, h_last[1]), (sm.lw, ldw2)]
                         + (W(g, "ps.w2") if m == 0 else []), inc=(sm.pe, 1))
            MARK(g, "B.w2buf%d" % (m % 2), sm.pe, pe_w2)
        MARK(g, "B.hbuf%d" % h2, sm.pe, pe_w2)
        for dco in range(8):
            cv_o = EMIT(g, "vector", lambda e, dco=dco, tbase=tbase: e.scalar_tensor_tensor(
                out=rv[:, dco, tbase:tbase + 512], in0=banks[dco],
                scalar=B.lvec[:, 44 + dco:45 + dco], in1=rv[:, dco, tbase:tbase + 512],
                op0=ALU.add, op1=ALU.add),
                waits=[(sm.pe, pe_w2)] + r_ready, inc=(sm.cv, 1))
            MARK(g, "B.rtile.w", sm.cv, cv_o, add=(dco > 0 or h2 > 0))
            if h2 == 1 and dco == 5:
                MARK(g, "ps.proj", sm.cv, cv_o)
            if h2 == 1 and dco == 6:
                MARK(g, "ps.y", sm.cv, cv_o)
            if dco == 7:
                MARK(g, "ps.w2", sm.cv, cv_o)
                if h2 == 1:
                    MARK(g, "ps.tr", sm.cv, cv_o)
    st_r = EMIT(g, "sync", lambda e: e.dma_start(
        out=dr["r_scr"].rearrange("p (c t) -> p c t", c=8)[:, :, t0:t0 + 1024],
        in_=rv[:, :, 1:1025]),
        waits=W(g, "B.rtile.w"), inc=(sm.st, 16))
    MARK(g, "B.rtile", sm.st, st_r)


def _phaseC(g):
    nc, sb, dr, sm, ar = g.nc, g.sb, g.dr, g.sm, g.ar
    ps = g.ps
    C = G()
    rc = Reg(ar)
    C.xt = rc.alloc(128, 1024, F32)
    C.st = rc.alloc(128, 1024, F32)
    C.rt = rc.alloc(128, 1024, F32)
    C.gt = rc.alloc(128, 1024, F32)
    C.ot = rc.alloc(128, 1024, F32)
    bar = [(sm.pe, sm.pe.n), (sm.cv, sm.cv.n), (sm.cs, sm.cs.n),
           (sm.st, sm.st.n), (sm.sg, sm.sg.n), (sm.cg, sm.cg.n), (sm.ld, sm.ld.n),
           (sm.lg, sm.lg.n), (sm.lw, sm.lw.n), (sm.lr, sm.lr.n)]
    g.war_floor = bar
    first = True
    for tau in range(NT):
        t0 = tau * 128
        wts = bar if first else []
        ld_x = EMIT(g, "sync", lambda e, t0=t0: e.dma_start(
            out=C.xt, in_=dr["x"][t0:t0 + 128, :]),
            waits=wts + W(g, "C.xt"), inc=(sm.ld, 16))
        ld_s = EMIT(g, "sync", lambda e, t0=t0: e.dma_start(
            out=C.st, in_=dr["s_scr"][t0:t0 + 128, :]),
            waits=wts + W(g, "C.st"), inc=(sm.ld, 16))
        ld_r = EMIT(g, "sync", lambda e, tau=tau: e.dma_start(
            out=C.rt.rearrange("p (c t) -> p c t", c=8),
            in_=dr["r_scr"].rearrange("p (c t) -> p c t", c=8)[:, :, tau * 128:tau * 128 + 128]),
            waits=wts + W(g, "C.rt"), inc=(sm.ld, 16))
        first = False

        def rtr(e):
            last = None
            for c in range(8):
                last = e.transpose(ps.proj[:, c * 128:(c + 1) * 128],
                                   C.rt[:, c * 128:(c + 1) * 128], sb.identf[:])
            return last
        pe_t = EMIT(g, "tensor", rtr, waits=[(sm.ld, ld_r)] + wts + W(g, "ps.proj"),
                    inc=(sm.pe, 1))
        cs_g = EMIT(g, "scalar", lambda e: e.activation(out=C.gt, in_=ps.proj[:, :],
                                                        func=AF.Silu),
                    waits=[(sm.pe, pe_t)] + wts + W(g, "C.gt"), inc=(sm.cs, 1))
        if "c0" in g.debug and tau == 0:
            _c0 = EMIT(g, "sync", lambda e: e.dma_start(out=g.dbg["c0"][0], in_=C.rt),
                       waits=[(sm.ld, ld_r)], inc=(sm.st, 16))
            MARK(g, "C.rt", sm.st, _c0, add=True)
            _c1 = EMIT(g, "sync", lambda e: e.dma_start(out=g.dbg["c0"][1], in_=C.gt),
                       waits=[(sm.cs, cs_g)], inc=(sm.st, 16))
            MARK(g, "C.gt", sm.st, _c1, add=True)
            _c2 = EMIT(g, "sync", lambda e: e.dma_start(out=g.dbg["c0"][2], in_=C.st),
                       waits=[(sm.ld, ld_s)], inc=(sm.st, 16))
            MARK(g, "C.st", sm.st, _c2, add=True)
        MARK(g, "ps.proj", sm.cs, cs_g)
        MARK(g, "C.rt", sm.pe, pe_t)
        cv_m = EMIT(g, "vector", lambda e: e.tensor_tensor(out=C.gt, in0=C.st, in1=C.gt,
                                                           op=ALU.mult),
                    waits=[(sm.cs, cs_g), (sm.ld, ld_s)] + wts, inc=(sm.cv, 1))
        MARK(g, "C.st", sm.cv, cv_m)
        cv_o = EMIT(g, "vector", lambda e: e.tensor_tensor(out=C.ot, in0=C.xt, in1=C.gt,
                                                           op=ALU.add),
                    waits=[(sm.cv, cv_m), (sm.ld, ld_x)] + W(g, "C.ot"), inc=(sm.cv, 1))
        MARK(g, "C.xt", sm.cv, cv_o)
        MARK(g, "C.gt", sm.cv, cv_o)
        st_o = EMIT(g, "sync", lambda e, t0=t0: e.dma_start(
            out=dr["out"][t0:t0 + 128, :], in_=C.ot),
            waits=[(sm.cv, cv_o)], inc=(sm.st, 16))
        MARK(g, "C.ot", sm.st, st_o)



def _prep_in_maps(inputs):
    c = host_constants()
    base = {}
    base["wq"] = _fm(np.asarray(inputs["L0_attn_wq"], dtype=np.float32))
    base["wkv"] = _fm(np.asarray(inputs["L0_attn_wkv"], dtype=np.float32))
    base["wo"] = _fm(np.asarray(inputs["L0_attn_wo"], dtype=np.float32))
    base["bq"] = _rep(inputs["L0_attn_bq"])
    base["bkv"] = _rep(inputs["L0_attn_bkv"])
    base["bo"] = _rep(inputs["L0_attn_bo"])
    base["qn"] = _rep(np.tile(np.asarray(inputs["L0_attn_qn_w"], dtype=np.float32), H))
    base["kn"] = _rep(np.tile(np.asarray(inputs["L0_attn_kn_w"], dtype=np.float32), NKV))
    Wqkv = np.asarray(inputs["L0_ffn_Wqkv"], dtype=np.float32)
    Woin = np.asarray(inputs["L0_ffn_Wo"], dtype=np.float32)
    W1 = np.asarray(inputs["L0_ffn_W1"], dtype=np.float32)
    W2 = np.asarray(inputs["L0_ffn_W2"], dtype=np.float32)
    base["wqkv"] = np.stack([_fm(Wqkv[l]) for l in range(DEPTH)])
    base["woin"] = np.stack([_fm(Woin[l]) for l in range(DEPTH)])
    w1l = np.zeros((DEPTH, 128, 22 * 2048), dtype=np.float32)
    for l in range(DEPTH):
        for m in range(22):
            wdt = min(128, 2730 - m * 128)
            a = W1[l][:, m * 128:m * 128 + wdt]
            gt = W1[l][:, 2730 + m * 128:2730 + m * 128 + wdt]
            w1l[l][:, m * 2048:m * 2048 + 8 * wdt] = \
                a.reshape(8, 128, wdt).transpose(1, 0, 2).reshape(128, 8 * wdt)
            w1l[l][:, m * 2048 + 8 * wdt:m * 2048 + 16 * wdt] = \
                gt.reshape(8, 128, wdt).transpose(1, 0, 2).reshape(128, 8 * wdt)
    base["w1"] = w1l
    w2l = np.zeros((DEPTH, 128, 22 * 1024), dtype=np.float32)
    for l in range(DEPTH):
        for m in range(22):
            wdt = min(128, 2730 - m * 128)
            w2l[l][:wdt, m * 1024:(m + 1) * 1024] = W2[l][m * 128:m * 128 + wdt, :]
    base["w2"] = w2l
    b1 = np.asarray(inputs["L0_ffn_b1"], dtype=np.float32)
    b1a = np.zeros((DEPTH, 128, 22), dtype=np.float32)
    b1g = np.zeros((DEPTH, 128, 22), dtype=np.float32)
    for l in range(DEPTH):
        for m in range(22):
            wdt = min(128, 2730 - m * 128)
            b1a[l, :wdt, m] = b1[l, m * 128:m * 128 + wdt]
            b1g[l, :wdt, m] = b1[l, 2730 + m * 128:2730 + m * 128 + wdt]
    base["b1a"] = b1a
    base["b1g"] = b1g
    base["b2"] = np.stack([np.asarray(inputs["L0_ffn_b2"], dtype=np.float32)[l].reshape(8, 128).T
                           for l in range(DEPTH)])
    base["lng"] = np.stack([np.asarray(inputs["L0_ffn_ln_g"], dtype=np.float32)[l].reshape(8, 128).T
                            for l in range(DEPTH)])
    base["lnb"] = np.stack([np.asarray(inputs["L0_ffn_ln_b"], dtype=np.float32)[l].reshape(8, 128).T
                            for l in range(DEPTH)])
    base["qsc"] = np.stack([_rep(np.tile(np.asarray(inputs["L0_ffn_q_scale"], dtype=np.float32)[l], H))
                            for l in range(DEPTH)])
    base["ksc"] = np.stack([_rep(np.tile(np.asarray(inputs["L0_ffn_k_scale"], dtype=np.float32)[l], H))
                            for l in range(DEPTH)])
    fc = np.zeros((128, 8), dtype=np.float32)
    fc[:, 0] = 1e-5; fc[:, 1] = 1e-8; fc[:, 2] = -LN8; fc[:, 3] = 1.0
    base["fconsts"] = fc
    p_ = np.arange(128, dtype=np.float32)
    bt = np.zeros((128, 8), dtype=np.float32)
    bt[:, 0] = 0.0;       bt[:, 3] = p_ + 1
    bt[:, 1] = 0.0;       bt[:, 4] = p_ + 129
    bt[:, 2] = p_;        bt[:, 5] = p_ + 257
    base["bndtab"] = bt
    base["onesrow"] = np.ones((1, 128), dtype=np.float32)
    bm = np.zeros((128, 3 * 384), dtype=np.float32)
    cc = np.arange(384)[None, :]
    pp = p_[:, None]
    bm[:, 0:128] = np.where(cc[:, :128] <= pp, 0.0, NEG)
    bm[:, 384:384 + 256] = np.where(cc[:, :256] <= 128 + pp, 0.0, NEG)
    bm[:, 768:768 + 384] = np.where((cc >= pp) & (cc <= pp + 256), 0.0, NEG)
    base["bmask"] = bm
    for k in ["ident", "diagmask", "qaug", "kaug"]:
        base[k] = c[k]
    for k in ["rcsq", "rssq", "rcsk", "rssk"]:
        base[k] = _tile_tok(c[k])
    x = np.asarray(inputs["L0_attn_x"], dtype=np.float32)
    return [dict(base, x=np.ascontiguousarray(x[b])) for b in range(8)]


def kernel(**inputs):
    in_maps = _prep_in_maps(inputs)
    nc = build_nc()
    res = run_bass_kernel_spmd(nc, in_maps, core_ids=list(range(8)))
    return np.stack([np.asarray(res.results[b]["out"]) for b in range(8)]).astype(np.float32)
